# version 11
# baseline (speedup 1.0000x reference)
import contextlib
import numpy as np
import ml_dtypes
import concourse.bass as bass
import concourse.mybir as mybir
from concourse.bass import IndirectOffsetOnAxis
from concourse.bass_utils import run_bass_kernel_spmd

F32 = mybir.dt.float32
BF16 = mybir.dt.bfloat16
I32 = mybir.dt.int32
AF = mybir.ActivationFunctionType
AL = mybir.AluOpType
AX = mybir.AxisListType

B, T, D = 2, 4096, 2048
H, DK, DV, CKW, CH = 16, 128, 128, 4, 64
HL, FQ, TOK = 4, 512, 1024
TC, NTC = 512, 8
NCH_C = TC // CH
DT = D // 128
FI = 8192
NIT = FI // 128

DEBUG = False


class Sched:
    ENGINES = ("tensor", "vector", "scalar", "gpsimd", "sync")

    def __init__(self):
        self.ops = {e: [] for e in self.ENGINES}
        self.cnt = {}
        self.sems = {}

    def _sem(self, name):
        self.cnt.setdefault(name, 0)
        return name

    def val(self, name):
        return self.cnt.get(name, 0)

    def allv(self, *names):
        return [(n, self.val(n)) for n in names]

    def emit(self, eng, fn, waits=(), inc=None, inc_n=1):
        waits = list(waits)
        if eng == "vector":
            waits.append(("V", self.val("V")))
            if inc is None:
                inc = "V"
        elif eng == "scalar":
            waits.append(("A", self.val("A")))
            if inc is None:
                inc = "A"
        if inc is not None:
            self._sem(inc)
            self.cnt[inc] += inc_n
        waits = tuple((self._sem(s), v) for s, v in waits if v > 0)
        incn = (inc, inc_n) if inc else None

        def closure(engine, _fn=fn, _w=waits, _i=incn):
            for s, v in _w:
                engine.wait_ge(self.sems[s], v)
            ins = _fn(engine)
            if _i is not None:
                ins.then_inc(self.sems[_i[0]], _i[1])
        self.ops[eng].append(closure)

    def barrier(self):
        self._sem("B")
        n0 = self.val("B")
        pre = [(k, v) for k, v in self.cnt.items() if k != "B" and v > 0]
        for eng in self.ENGINES:
            self.emit(eng, lambda e: e.nop(), inc="B",
                      waits=pre if eng == "sync" else ())
        tgt = n0 + len(self.ENGINES)
        assert self.cnt["B"] == tgt
        for eng in self.ENGINES:
            self.emit(eng, lambda e: e.nop(), waits=[("B", tgt)])

    def dma(self, eng, out, in_, sem="L", waits=(), transpose=False):
        if transpose:
            self.emit(eng, lambda e, o=out, i=in_: e.dma_start(out=o, in_=i, transpose=True),
                      waits=waits, inc=sem, inc_n=16)
        else:
            self.emit(eng, lambda e, o=out, i=in_: e.dma_start(out=o, in_=i),
                      waits=waits, inc=sem, inc_n=16)


class Arena:
    def __init__(self, nc, name, cols, dtype):
        self.t = nc.ctx.enter_context(nc.sbuf_tensor(name, [128, cols], dtype))
        self.cols = cols
        self.off = 0
        self.marks = []

    def alloc(self, n, align=8):
        o = (self.off + align - 1) // align * align
        assert o + n <= self.cols, f"arena overflow {o}+{n}>{self.cols}"
        self.off = o + n
        return self.t[:, o:o + n]

    def push(self):
        self.marks.append(self.off)

    def pop(self):
        self.off = self.marks.pop()


def v2(t, a, b2):
    return t.rearrange("p (a b) -> p a b", a=a, b=b2)


def v3(t, a, b2, c3):
    return t.rearrange("p (a b c) -> p a b c", a=a, b=b2, c=c3)


def build_program():
    nc = bass.Bass("TRN2", target_bir_lowering=False)
    S = Sched()

    def din(name, shape, dt=F32):
        return nc.dram_tensor(name, shape, dt, kind="ExternalInput").ap()

    x_b = din("x_b", [T, D])
    x_own = din("x_own", [TOK, D])
    wq_i = din("wq", [D, FQ]); wk_i = din("wk", [D, FQ]); wv_i = din("wv", [D, FQ])
    cd_i = {n: din("cd" + n, [HL, CKW, 128, 128]) for n in "qkv"}
    wad_i = din("wad", [D, DK]); wau_i = din("wau", [DK, FQ])
    wgd_i = din("wgd", [D, DV]); wgu_i = din("wgu", [DV, FQ])
    wb_i = din("wb", [D, HL])
    rms_i = din("rms_bc", [128, FQ])
    wo_i = din("wo", [D, D])
    wmod_i = din("wmod_s", [D, 1536])
    cT_i = din("c_T", [D, B])
    wg_i = din("wg_t", [NIT, 128, DT * 128])
    wu_i = din("wu_t", [NIT, 128, DT * 128])
    wd_i = din("wd_t", [NIT, 128, D])
    ident_i = din("ident_in", [128, 128])
    masks_i = din("masks_in", [64, 192])
    mrep_i = din("mrep_in", [64, 1536])
    ones1_i = din("ones1_in", [1, 128])
    bsel_i = din("bsel128", [128, 1])
    ogidx_i = din("ogidx", [128, 32], dt=I32)
    out_s = nc.dram_tensor("out_s", [TOK, D], F32, kind="ExternalOutput").ap()

    xhat_d = nc.dram_tensor("xhat_d", [T, D], BF16).ap()
    modp_d = nc.dram_tensor("modp_d", [1536, B], F32).ap()
    modf_d = nc.dram_tensor("modf_d", [12288, B], F32).ap()
    og_d = nc.dram_tensor("og_d", [T, FQ], BF16).ap()
    ogg_d = [nc.dram_tensor("ogg_d1", [3 * T, FQ], BF16).ap(),
             nc.dram_tensor("ogg_d2", [T, FQ], BF16).ap()]
    ogsl_d = nc.dram_tensor("ogsl_d", [4 * TOK, FQ], BF16).ap()
    x1_d = nc.dram_tensor("x1_d", [TOK, D], F32).ap()
    grow_d = nc.dram_tensor("grow_d", [2, D], F32).ap()
    hs_d = nc.dram_tensor("hs_d", [NIT, 128, TOK], BF16).ap()
    wdb_d = nc.dram_tensor("wdb_d", [NIT, 128, D], BF16).ap()
    wgb_d = nc.dram_tensor("wgb_d", [NIT, 128, DT * 128], BF16).ap()
    wub_d = nc.dram_tensor("wub_d", [NIT, 128, DT * 128], BF16).ap()

    dbg = {}

    def ddbg(name, shape, dt):
        if DEBUG:
            dbg[name] = nc.dram_tensor(name, shape, dt, kind="ExternalOutput").ap()
            return dbg[name]
        return None
    dbg_h = ddbg("dbg_h", [D, T], BF16)
    dbg_qn = ddbg("dbg_qn", [FQ, T], BF16)
    dbg_kn = ddbg("dbg_kn", [FQ, T], BF16)
    dbg_sp = ddbg("dbg_sp", [FQ, T], F32)
    dbg_h2 = ddbg("dbg_h2", [D, TOK], BF16)

    AB = Arena(nc, "ab", 87136, BF16)
    AFP = Arena(nc, "af", 9608, F32)
    ogidx_s = nc.ctx.enter_context(nc.sbuf_tensor("ogidx_s", [128, 32], I32))

    ps_stack = contextlib.ExitStack()
    ps_big = nc.ctx.enter_context(nc.psum_tensor("psbig", [128, 6 * 512], F32))
    ps_f = [ps_big[:, i * 512:(i + 1) * 512] for i in range(6)]
    ps_b = [ps_stack.enter_context(nc.psum_tensor(f"psb{i}", [64, 512], BF16)) for i in range(2)]
    ps_state = {"i": 0, "guard": [()] * 8, "bi": 0, "bguard": [()] * 2, "n": 6}

    def psum_acq():
        i = ps_state["i"] % ps_state["n"]
        ps_state["i"] += 1
        return i, ps_state["guard"][i]

    def psum_rel(i, sem):
        ps_state["guard"][i] = ((sem, S.val(sem)),)

    def psb_acq():
        i = ps_state["bi"] % 2
        ps_state["bi"] += 1
        return i, ps_state["bguard"][i]

    def psb_rel(i, sem):
        ps_state["bguard"][i] = ((sem, S.val(sem)),)

    def psum_acq_set(idxs):
        g = []
        for i in idxs:
            g.extend(ps_state["guard"][i])
        return tuple(g)

    def psum_rel_set(idxs, sem):
        for i in idxs:
            ps_state["guard"][i] = ((sem, S.val(sem)),)

    ident = AB.alloc(128)
    mask_sl = AB.alloc(64)
    mask_ui = AB.alloc(64)
    eye64 = AB.alloc(64)
    ones_bf = AB.alloc(128)
    nmsk = AB.alloc(512)
    umsk = AB.alloc(512)
    erep = AB.alloc(512)
    ones1_f = AFP.alloc(128)
    identF = AFP.alloc(128)
    rms_sb = AFP.alloc(FQ)
    sa_c = AFP.alloc(DT); ca1_c = AFP.alloc(DT)
    sb_c = AFP.alloc(DT); cb1_c = AFP.alloc(DT)
    ga_bc = AFP.alloc(D); gb_bc = AFP.alloc(D)
    bsel_s = AFP.alloc(1)

    S.dma("gpsimd", ident, ident_i[:, :], "Gc")
    S.dma("gpsimd", mask_sl[0:64, :], masks_i[:, 0:64], "Gc")
    S.dma("gpsimd", mask_ui[0:64, :], masks_i[:, 64:128], "Gc")
    S.dma("gpsimd", eye64[0:64, :], masks_i[:, 128:192], "Gc")
    S.dma("gpsimd", nmsk[0:64, :], mrep_i[:, 0:512], "Gc")
    S.dma("gpsimd", umsk[0:64, :], mrep_i[:, 512:1024], "Gc")
    S.dma("gpsimd", erep[0:64, :], mrep_i[:, 1024:1536], "Gc")
    S.emit("vector", lambda v: v.memset(ones_bf, 1.0), inc="V")
    S.dma("sync", ones1_f[0:1, :], ones1_i[:, :], "Lc")
    S.dma("sync", identF, ident_i[:, :], "Lc")
    S.dma("sync", rms_sb, rms_i[:, :], "Lc")
    S.dma("sync", bsel_s, bsel_i[:, :], "Lc")
    S.dma("sync", ogidx_s[:, :], ogidx_i[:, :], "Lc")

    AB.push(); AFP.push()
    csil_v = v2(AB.alloc(DT * B), DT, B)
    modp_v = v2(AFP.alloc(12 * B), 12, B)
    S.dma("gpsimd", csil_v[:, :, :], cT_i.rearrange("(a p) b -> p a b", p=128), "Gc")
    S.emit("scalar", lambda s: s.activation(csil_v[:, :, :], csil_v[:, :, :], AF.Silu),
           waits=[("Gc", S.val("Gc"))], inc="A")
    csil_done = S.val("A")

    wmod_v = [v2(AB.alloc(128 * DT), DT, 128) for _ in range(2)]
    for jt in range(12):
        buf = wmod_v[jt % 2]
        gm = "Gm%d" % (jt % 2)
        wpe = [("P", S.val("P"))] if jt >= 2 else []
        S.dma("gpsimd", buf[:, :, :],
              wmod_i[:, jt * 128:(jt + 1) * 128].rearrange("(a p) j -> p a j", p=128),
              gm, waits=wpe)
        pi, pg = psum_acq()
        ldv = S.val(gm)
        for d in range(DT):
            S.emit("tensor", lambda t, b2=buf, d2=d, p=pi: t.matmul(
                ps_f[p][0:128, 0:B], b2[:, d2, :], csil_v[:, d2, :],
                start=(d2 == 0), stop=(d2 == DT - 1)),
                waits=([(gm, ldv), ("A", csil_done)] + list(pg)) if d == 0 else (),
                inc="P" if d == DT - 1 else None)
        S.emit("scalar", lambda s, p=pi, j=jt: s.copy(modp_v[:, j, :], ps_f[p][0:128, 0:B]),
               waits=[("P", S.val("P"))], inc="A")
        psum_rel(pi, "A")
    S.dma("sync", modp_d.rearrange("(a p) b -> p a b", p=128), modp_v[:, :, :], "Sm",
          waits=[("A", S.val("A"))])
    S.emit("gpsimd", lambda g: g.collective_compute(
        "AllGather", AL.bypass, replica_groups=[list(range(8))],
        ins=[modp_d], outs=[modf_d]),
        waits=[("Sm", S.val("Sm"))], inc="C")
    cc_mod = S.val("C")

    m2 = {}
    for grp, nm in ((0, "sa"), (1, "ca"), (2, "ga"), (3, "sb"), (4, "cb"), (5, "gb")):
        m2[nm] = v2(AFP.alloc(DT * B), DT, B)
        S.dma("sync", m2[nm],
              modf_d[grp * D:(grp + 1) * D, :].rearrange("(a p) b -> p a b", p=128),
              "Lm", waits=[("C", cc_mod)])
    mod_ld = S.val("Lm")
    ga_c = AFP.alloc(DT); gb_c = AFP.alloc(DT)

    def select2(dst, key):
        src = m2[key]
        S.emit("vector", lambda v: v.tensor_tensor(dst, src[:, :, 1], src[:, :, 0], AL.subtract),
               waits=[("Lm", mod_ld)], inc="V")
        S.emit("vector", lambda v: v.scalar_tensor_tensor(
            dst, dst, bsel_s, src[:, :, 0], AL.mult, AL.add),
            waits=[("Lc", S.val("Lc"))], inc="V")

    select2(sa_c, "sa"); select2(ca1_c, "ca"); select2(sb_c, "sb")
    select2(cb1_c, "cb"); select2(ga_c, "ga"); select2(gb_c, "gb")
    S.emit("vector", lambda v: v.tensor_single_scalar(ca1_c, ca1_c, 1.0, AL.add), inc="V")
    S.emit("vector", lambda v: v.tensor_single_scalar(cb1_c, cb1_c, 1.0, AL.add), inc="V")

    sgn = AFP.alloc(DT)
    growsb = AFP.alloc(D)
    for gi, (gc, bcd) in enumerate(((ga_c, ga_bc), (gb_c, gb_bc))):
        S.emit("vector", lambda v, g=gc: v.tensor_single_scalar(sgn, g, 0.0, AL.is_ge),
               waits=[("P", S.val("P"))], inc="V")
        S.emit("vector", lambda v: v.tensor_scalar(sgn, sgn, 2.0, -1.0, AL.mult, AL.add), inc="V")
        S.emit("scalar", lambda s, g=gc: s.activation(g, g, AF.Abs),
               waits=[("V", S.val("V"))], inc="A")
        S.emit("vector", lambda v, g=gc: v.tensor_single_scalar(g, g, 0.25, AL.max),
               waits=[("A", S.val("A"))], inc="V")
        S.emit("vector", lambda v, g=gc: v.tensor_tensor(g, g, sgn, AL.mult), inc="V")
        gd = S.val("V")
        S.emit("sync", lambda e, g=gc, gi2=gi: e.dma_start(
            out=grow_d[gi2:gi2 + 1, :].rearrange("one (a p) -> p a one", p=128),
            in_=g.rearrange("p (a one) -> p a one", one=1),
            allow_slow_non_contiguous=True),
            waits=[("V", gd)], inc="Sm", inc_n=16)
        S.dma("sync", bcd.rearrange("p (one d) -> p one d", one=1),
              grow_d[gi:gi + 1, :].partition_broadcast(128), "Lb",
              waits=[("Sm", S.val("Sm"))])
    AB.push(); AFP.push()
    NLB = 4
    x_t4 = [[AB.alloc(D) for _ in range(NLB)] for _ in range(2)]
    xh_t4 = [[AB.alloc(D) for _ in range(NLB)] for _ in range(2)]
    sq_scr = AB.alloc(D)
    stat4 = AFP.alloc(4 * NLB)
    ln_guard = [(0, 0), (0, 0)]
    ln_ch = []

    for ib in range(T // 128 // NLB):
        bi = ib % 2
        lx = "Gx%d" % bi
        sx = "Sx%d" % bi
        gv, gst = ln_guard[bi]
        for k in range(NLB):
            i = ib * NLB + k
            S.dma("gpsimd", x_t4[bi][k], x_b[i * 128:(i + 1) * 128, :], lx,
                  waits=[("V", gv), (sx, gst)] if k == 0 else ())
        ldv = S.val(lx)
        mu4 = stat4[:, 0:NLB]; ssq4 = stat4[:, NLB:2 * NLB]
        var4 = stat4[:, 2 * NLB:3 * NLB]; rstd4 = stat4[:, 3 * NLB:4 * NLB]
        for k in range(NLB):
            S.emit("vector", lambda v, x=x_t4[bi][k], k2=k: v.reduce_sum(
                stat4[:, k2:k2 + 1], x, AX.X),
                waits=[(lx, ldv), ("A", S.val("A"))] if k == 0 else (), inc="V")
        for k in range(NLB):
            S.emit("scalar", lambda st, x=x_t4[bi][k], k2=k: st.activation(
                sq_scr, x, AF.Square, accum_out=stat4[:, NLB + k2:NLB + k2 + 1]),
                waits=[(lx, ldv), ("V", S.val("V"))] if k == 0 else (), inc="A")
        S.emit("vector", lambda v: v.tensor_single_scalar(mu4, mu4, 1.0 / D, AL.mult), inc="V")
        S.emit("vector", lambda v: v.tensor_tensor(var4, mu4, mu4, AL.mult), inc="V")
        S.emit("vector", lambda v: v.scalar_tensor_tensor(var4, ssq4, 1.0 / D, var4,
                                                          AL.mult, AL.subtract),
               waits=[("A", S.val("A"))], inc="V")
        S.emit("vector", lambda v: v.tensor_single_scalar(var4, var4, 1e-5, AL.add), inc="V")
        S.emit("vector", lambda v: v.reciprocal(rstd4, var4), inc="V")
        S.emit("scalar", lambda st: st.activation(rstd4, rstd4, AF.Sqrt),
               waits=[("V", S.val("V"))], inc="A")
        for k in range(NLB):
            S.emit("vector", lambda v, x=x_t4[bi][k], o=xh_t4[bi][k], k2=k: v.tensor_scalar(
                o, x, stat4[:, k2:k2 + 1], stat4[:, 3 * NLB + k2:3 * NLB + k2 + 1],
                AL.subtract, AL.mult),
                waits=[("A", S.val("A"))] if k == 0 else (), inc="V")
        xv = S.val("V")
        for k in range(NLB):
            i = ib * NLB + k
            S.dma("sync", xhat_d[i * 128:(i + 1) * 128, :], xh_t4[bi][k], sx,
                  waits=[("V", xv)] if k == 0 else ())
        ln_guard[bi] = (xv, S.val(sx))
        ln_ch.append((sx, S.val(sx)))

    ln_done = [("Sx0", S.val("Sx0")), ("Sx1", S.val("Sx1"))]
    AB.pop(); AFP.pop()
    AB.pop(); AFP.pop()
    S.barrier()

    AB.push(); AFP.push()
    wq_v = v2(AB.alloc(FQ * DT), DT, FQ)
    wk_v = v2(AB.alloc(FQ * DT), DT, FQ)
    wv_v = v2(AB.alloc(FQ * DT), DT, FQ)
    wad_v = v2(AB.alloc(DK * DT), DT, DK)
    wgd_v = v2(AB.alloc(DV * DT), DT, DV)
    wb_v = v2(AB.alloc(HL * DT), DT, HL)
    wau_sb = AB.alloc(FQ); wgu_sb = AB.alloc(FQ)
    cd_sb = {n: v3(AB.alloc(HL * CKW * 128), HL, CKW, 128) for n in "qkv"}
    for dst, src in ((wq_v, wq_i), (wk_v, wk_i), (wv_v, wv_i),
                     (wad_v, wad_i), (wgd_v, wgd_i), (wb_v, wb_i)):
        for d in range(DT):
            S.dma("gpsimd", dst[:, d, :], src[d * 128:(d + 1) * 128, :], "Gw")
    S.dma("gpsimd", wau_sb, wau_i[:, :], "Gw")
    S.dma("gpsimd", wgu_sb, wgu_i[:, :], "Gw")
    for n in "qkv":
        for hh in range(HL):
            for j in range(CKW):
                S.dma("gpsimd", cd_sb[n][:, hh, j, :], cd_i[n][hh, j, :, :], "Gw")
    wld = S.val("Gw")


    h_v = v2(AB.alloc(DT * TC), DT, TC)
    pre = {n: v2(AB.alloc(HL * (TC + 8)), HL, TC + 8) for n in "qkv"}
    sil = {n: pre[n][:, :, 0:TC] for n in "qkv"}
    qn_v = sil["q"]
    kn_v = sil["k"]
    qt_v = sil["q"]
    kt_v = v2(AB.alloc(HL * TC), HL, TC)
    kb_v = v2(AB.alloc(HL * TC), HL, TC)
    kdf_v = v2(AB.alloc(HL * TC), HL, TC)
    kdt_v3 = v3(AB.alloc(HL * NCH_C * 128), HL, NCH_C, 128)
    vt_v3 = v3(AB.alloc(HL * NCH_C * 128), HL, NCH_C, 128)
    sqs = AB.alloc(HL * TC)
    z_sb = sqs[:, 512:1024]; zg_sb = sqs[:, 0:512]
    eb_raw = AB.alloc(HL * TC)
    eb_b = v2(eb_raw, HL, TC)
    ebm_b = v2(AB.alloc(HL * TC), HL, TC)
    ebd_b = ebm_b
    sp_b = v2(AFP.alloc(HL * TC), HL, TC)
    cum_b = v2(AFP.alloc(HL * TC), HL, TC)
    r_big = sp_b.rearrange("p a b -> p (a b)")
    ebl_v = v2(AFP.alloc(HL * NCH_C), HL, NCH_C)
    bhr = AB.alloc(TC)
    bhT_v = v2(AB.alloc(NCH_C * HL), NCH_C, HL)
    gate_v = v2(AB.alloc(4 * TC), 4, TC)
    TT3 = v3(eb_raw, NCH_C, HL, 64)
    AT3 = v3(sqs, NCH_C, HL, 64)
    Npr = [AB.alloc(512) for _ in range(2)]
    Ppr = [AB.alloc(512) for _ in range(2)]
    Qpr = [AB.alloc(512) for _ in range(2)]
    Rpr = [AB.alloc(512) for _ in range(2)]
    og_sb = AB.alloc(FQ)
    Sf_v = v2(AFP.alloc(HL * DV), HL, DV)
    Sb_v = v2(AB.alloc(HL * DV), HL, DV)
    rhsu = AB.alloc(FQ)
    u_sb = rhsu
    ms_sb = AFP.alloc(HL)
    rr_sb = AFP.alloc(HL)
    S.emit("vector", lambda v: v.memset(Sf_v[:, :, :], 0.0), inc="V")
    S.emit("vector", lambda v: v.memset(Sb_v[:, :, :], 0.0), inc="V")
    for n in "qkv":
        S.emit("vector", lambda v, n2=n: v.memset(pre[n2][:, :, 0:3], 0.0), inc="V")

    sab = AFP.alloc(16)
    sa_bf = AB.alloc(DT)
    S.emit("vector", lambda v: v.tensor_copy(sa_bf, sa_c), inc="V")
    sabf_v = S.val("V")
    blist = []
    for base, wvv in ((0, wq_v), (4, wk_v), (8, wv_v)):
        for hh in range(HL):
            blist.append((lambda d, w=wvv, h3=hh: w[:, d, h3 * 128:(h3 + 1) * 128],
                          base + hh, 128))
    blist.append(((lambda d: wad_v[:, d, :]), 12, 128))
    blist.append(((lambda d: wgd_v[:, d, :]), 13, 128))
    blist.append(((lambda d: wb_v[:, d, :]), 14, HL))
    pi_b, pg_b = psum_acq()
    lastb = (len(blist) - 1, DT - 1)
    for ui, (wf, col, rows) in enumerate(blist):
        for d in range(DT):
            S.emit("tensor", lambda t, f=wf, d2=d, c2=col, r=rows, p=pi_b,
                   st=(d == 0), sp=(d == DT - 1): t.matmul(
                ps_f[p][0:r, c2:c2 + 1], f(d2), sa_bf[:, d2:d2 + 1],
                start=st, stop=sp),
                waits=([("Gw", wld), ("V", sabf_v)] + list(pg_b))
                if (ui == 0 and d == 0) else (),
                inc="P" if (ui, d) == lastb else None)
    S.emit("scalar", lambda s2, p=pi_b: s2.copy(sab[:, 0:14], ps_f[p][:, 0:14]),
           waits=[("P", S.val("P"))], inc="A")
    S.emit("scalar", lambda s2, p=pi_b: s2.copy(sab[0:HL, 14:15], ps_f[p][0:HL, 14:15]),
           inc="A")
    psum_rel(pi_b, "A")
    for wvv, wdim in ((wq_v, FQ), (wk_v, FQ), (wv_v, FQ),
                      (wad_v, DK), (wgd_v, DV), (wb_v, HL)):
        S.emit("vector", lambda v, w=wvv, n2=wdim: v.tensor_tensor(
            w[:, :, :], w[:, :, :],
            ca1_c.rearrange("p (a one) -> p a one", one=1).to_broadcast([128, DT, n2]),
            AL.mult),
            waits=[("P", S.val("P"))], inc="V")
    wscale_done = S.val("V")

    ldh_map = {}
    h_free_hist = [0] * NTC

    def emit_h(tc2, pwait):
        lh = "Lh%d" % (tc2 % 2)
        t02 = tc2 * TC
        for ft in range(DT):
            S.dma("sync", h_v[:, ft, :], xhat_d[t02:t02 + TC, ft * 128:(ft + 1) * 128],
                  lh, waits=([ln_ch[tc2], ("P", pwait)]) if ft == 0 else (),
                  transpose=True)
        ldh_map[tc2] = (lh, S.val(lh))

    h_free = [0]
    so_half1 = [0]
    emit_h(0, 0)
    for tc in range(NTC):
        t0 = tc * TC
        hw = ldh_map[tc]
        if DEBUG:
            for ft in range(DT):
                S.dma("sync", dbg_h[ft * 128:(ft + 1) * 128, t0:t0 + TC], h_v[:, ft, :],
                      "Sd", waits=[hw])

        if tc > 0:
            for n in "qkv":
                for hh in range(HL):
                    S.emit("vector", lambda v, n2=n, h3=hh: v.tensor_copy(
                        pre[n2][:, h3, 0:3], pre[n2][:, h3, TC:TC + 3]),
                        waits=[("A", S.val("A"))] if (n == "q" and hh == 0) else (), inc="V")
        halo_done = S.val("V")
        evac_a = {}
        for uo, (n, wvv) in enumerate((("q", wq_v), ("k", wk_v), ("v", wv_v))):
            for hh in range(HL):
                pi, pg = psum_acq()
                for d in range(DT):
                    S.emit("tensor", lambda t, w=wvv, d2=d, h3=hh, p=pi, h2=h_v: t.matmul(
                        ps_f[p][:, :], w[:, d2, h3 * 128:(h3 + 1) * 128], h2[:, d2, :],
                        start=(d2 == 0), stop=(d2 == DT - 1)),
                        waits=([hw, ("V", wscale_done)] + list(pg)) if d == 0 else (),
                        inc="P" if d == DT - 1 else None)
                S.emit("scalar", lambda s, n2=n, h3=hh, p=pi, c2=uo * 4 + hh: s.activation(
                    pre[n2][:, h3, 3:3 + TC], ps_f[p][:, :], AF.Identity,
                    bias=sab[:, c2:c2 + 1]),
                    waits=[("P", S.val("P")), ("V", halo_done)], inc="A")
                psum_rel(pi, "A")
                evac_a[(n, hh)] = S.val("A")
        for n in "qkv":
            for hh in range(HL):
                pi, pg = psum_acq()
                for j in range(CKW):
                    S.emit("tensor", lambda t, n2=n, h3=hh, j2=j, p=pi: t.matmul(
                        ps_f[p][:, :], cd_sb[n2][:, h3, j2, :], pre[n2][:, h3, j2:j2 + TC],
                        start=(j2 == 0), stop=(j2 == CKW - 1)),
                        waits=([("A", evac_a[(n, hh)])] + list(pg)) if j == 0 else (),
                        inc="P" if j == CKW - 1 else None)
                S.emit("scalar", lambda s, n2=n, h3=hh, p=pi: s.activation(
                    sil[n2][:, h3, :], ps_f[p][:, :], AF.Silu),
                    waits=[("P", S.val("P"))], inc="A")
                psum_rel(pi, "A")
        sil_done = S.val("A")
        for n in ("q", "k"):
            S.emit("scalar", lambda s, n2=n: s.activation(
                v2(sqs, HL, TC), sil[n2][:, :, :], AF.Square),
                waits=[("A", sil_done), ("P", S.val("P"))], inc="A")
            sqd = S.val("A")
            qb = (0, 1, 2, 3)
            pgq = psum_acq_set(qb)
            for hh in range(HL):
                S.emit("tensor", lambda t, h3=hh: t.matmul(
                    ps_f[h3][:, :], ones_bf, sqs[:, h3 * TC:(h3 + 1) * TC],
                    start=True, stop=True),
                    waits=([("A", sqd)] + list(pgq)) if hh == 0 else (),
                    inc="P" if hh == HL - 1 else None)
            S.emit("vector", lambda v: v.tensor_single_scalar(
                r_big, ps_big[:, 0:2048], 1e-12, AL.max),
                waits=[("P", S.val("P"))], inc="V")
            psum_rel_set(qb, "V")
            S.emit("vector", lambda v: v.reciprocal(r_big, r_big), inc="V")
            S.emit("scalar", lambda s: s.activation(r_big, r_big, AF.Sqrt),
                   waits=[("V", S.val("V"))], inc="A")
            S.emit("vector", lambda v, n2=n: v.tensor_tensor(
                sil[n2][:, :, :], sil[n2][:, :, :], v2(r_big, HL, TC), AL.mult),
                waits=[("A", S.val("A"))], inc="V")
        l2_done = S.val("V")
        if DEBUG:
            for hh in range(HL):
                S.dma("sync", dbg_qn[hh * 128:(hh + 1) * 128, t0:t0 + TC], qn_v[:, hh, :],
                      "Sd", waits=[("V", S.val("V"))])
                S.dma("sync", dbg_kn[hh * 128:(hh + 1) * 128, t0:t0 + TC], kn_v[:, hh, :],
                      "Sd", waits=[("V", S.val("V"))])

        def shared_proj(wtile, dst_sb, bcol):
            pi, pg = psum_acq()
            for d in range(DT):
                S.emit("tensor", lambda t, d2=d, p=pi, w=wtile, h2=h_v: t.matmul(
                    ps_f[p][:, :], w[:, d2, :], h2[:, d2, :],
                    start=(d2 == 0), stop=(d2 == DT - 1)),
                    waits=([hw] + list(pg)) if d == 0 else (),
                    inc="P" if d == DT - 1 else None)
            S.emit("scalar", lambda s, p=pi, o=dst_sb, c2=bcol: s.activation(
                o, ps_f[p][:, :], AF.Identity, bias=sab[:, c2:c2 + 1]),
                   waits=[("P", S.val("P"))], inc="A")
            psum_rel(pi, "A")
            return S.val("A")

        zd = shared_proj(wad_v, z_sb, 12)
        zgd = shared_proj(wgd_v, zg_sb, 13)
        pi, pg = psum_acq()
        for d in range(DT):
            S.emit("tensor", lambda t, d2=d, p=pi, h2=h_v: t.matmul(
                ps_f[p][0:HL, :], wb_v[:, d2, :], h2[:, d2, :],
                start=(d2 == 0), stop=(d2 == DT - 1)),
                waits=([hw] + list(pg)) if d == 0 else (),
                inc="P" if d == DT - 1 else None)
        h_free_hist[tc] = S.val("P")
        if tc + 1 < NTC:
            emit_h(tc + 1, h_free_hist[tc])
        S.emit("vector", lambda v, p=pi: v.tensor_scalar(
            bhr[0:HL, :], ps_f[p][0:HL, :], sab[0:HL, 14:15], -10.0, AL.add, AL.max),
            waits=[("P", S.val("P")), ("A", S.val("A"))], inc="V")
        psum_rel(pi, "V")
        S.emit("vector", lambda v: v.tensor_single_scalar(
            bhr[0:HL, :], bhr[0:HL, :], 10.0, AL.min), inc="V")
        S.emit("scalar", lambda s: s.activation(bhr[0:HL, :], bhr[0:HL, :], AF.Sigmoid),
               waits=[("V", S.val("V"))], inc="A")
        bhd = S.val("A")
        bi2, bg = psb_acq()
        for c8 in range(NCH_C):
            S.emit("tensor", lambda t, s2=c8, p=bi2: t.matmul(
                ps_b[p][0:64, s2 * HL:(s2 + 1) * HL], bhr[0:HL, s2 * 64:(s2 + 1) * 64],
                ident[0:HL, 0:HL], is_transpose=True,
                start=(s2 == 0), stop=(s2 == NCH_C - 1)),
                waits=([("A", bhd)] + list(bg)) if c8 == 0 else (),
                inc="P" if c8 == NCH_C - 1 else None)
        S.emit("vector", lambda v, p=bi2: v.tensor_copy(
            bhT_v[0:64, :, :], ps_b[p][0:64, 0:NCH_C * HL]),
            waits=[("P", S.val("P"))], inc="V")
        psb_rel(bi2, "V")
        qb = (0, 1, 2, 3)
        pgq = psum_acq_set(qb)
        for sub in range(4):
            S.emit("tensor", lambda t, s2=sub: t.matmul(
                ps_f[s2][:, :], zg_sb[:, s2 * 128:(s2 + 1) * 128], wgu_sb,
                start=True, stop=True),
                waits=([("A", zgd)] + list(pgq)) if sub == 0 else (),
                inc="P" if sub == 3 else None)
        S.emit("scalar", lambda s: s.activation(
            gate_v[:, :, :], ps_big[:, 0:2048], AF.Sigmoid),
            waits=[("P", S.val("P"))], inc="A")
        psum_rel_set(qb, "A")
        S.emit("vector", lambda v: v.tensor_tensor(
            gate_v[:, :, :], gate_v[:, :, :],
            v2(rms_sb, 1, FQ).to_broadcast([128, 4, FQ]), AL.mult),
            waits=[("A", S.val("A")), ("So", S.val("So"))], inc="V")

        qb = (0, 1, 2, 3)
        pgq = psum_acq_set(qb)
        for hh in range(HL):
            S.emit("tensor", lambda t, h3=hh: t.matmul(
                ps_f[h3][:, :], wau_sb[:, h3 * 128:(h3 + 1) * 128], z_sb,
                start=True, stop=True),
                waits=([("A", zd)] + list(pgq)) if hh == 0 else (),
                inc="P" if hh == HL - 1 else None)
        S.emit("vector", lambda v: v.tensor_scalar(
            sp_b.rearrange("p a b -> p (a b)"), ps_big[:, 0:2048], -10.0, 10.0,
            AL.max, AL.min),
            waits=[("P", S.val("P"))], inc="V")
        psum_rel_set(qb, "V")
        S.emit("scalar", lambda s: s.activation(sp_b[:, :, :], sp_b[:, :, :], AF.Exp),
               waits=[("V", S.val("V"))], inc="A")
        S.emit("scalar", lambda s: s.activation(sp_b[:, :, :], sp_b[:, :, :], AF.Ln, bias=1.0),
               inc="A")
        spd = S.val("A")
        for hh in range(HL):
            for c in range(NCH_C):
                S.emit("vector", lambda v, h3=hh, c2=c: v.tensor_tensor_scan(
                    cum_b[:, h3, c2 * CH:(c2 + 1) * CH], sp_b[:, h3, c2 * CH:(c2 + 1) * CH],
                    sp_b[:, h3, c2 * CH:(c2 + 1) * CH], 0.0, AL.add, AL.bypass),
                    waits=[("A", spd)] if (hh == 0 and c == 0) else (), inc="V")
        cumd = S.val("V")
        S.emit("scalar", lambda s: s.activation(eb_b[:, :, :], cum_b[:, :, :], AF.Exp, scale=-1.0),
               waits=[("V", cumd)], inc="A")
        S.emit("scalar", lambda s: s.activation(ebm_b[:, :, :], cum_b[:, :, :], AF.Exp), inc="A")
        S.emit("scalar", lambda s: s.activation(
            ebl_v[:, :, :], cum_b.rearrange("p a (c t) -> p a c t", t=CH)[:, :, :, CH - 1],
            AF.Exp, scale=-1.0), inc="A")
        ebld = S.val("A")
        S.emit("vector", lambda v: v.tensor_tensor(
            kt_v[:, :, :], kn_v[:, :, :], eb_b[:, :, :], AL.mult),
            waits=[("A", ebld), ("P", S.val("P"))], inc="V")
        S.emit("vector", lambda v: v.tensor_tensor(
            kb_v[:, :, :], kn_v[:, :, :], ebm_b[:, :, :], AL.mult), inc="V")
        S.emit("vector", lambda v: v.tensor_tensor(
            ebd_b.rearrange("p a (c t) -> p a c t", t=CH),
            ebm_b.rearrange("p a (c t) -> p a c t", t=CH),
            ebl_v.rearrange("p a (c o) -> p a c o", o=1).to_broadcast([128, HL, NCH_C, CH]),
            AL.mult), inc="V")
        S.emit("vector", lambda v: v.tensor_tensor(
            kdf_v[:, :, :], kn_v[:, :, :], ebd_b[:, :, :], AL.mult), inc="V")
        S.emit("vector", lambda v: v.tensor_tensor(
            qt_v[:, :, :], qn_v[:, :, :], eb_b[:, :, :], AL.mult), inc="V")
        scald = S.val("V")
        for h3c in range(HL):
            for src_t, dst_t, wsem in ((kdf_v, kdt_v3, ("V", scald)),
                                       (sil["v"], vt_v3, ("A", sil_done))):
                for cg in range(2):
                    bi2, bg = psb_acq()
                    for cc4 in range(4):
                        c8 = cg * 4 + cc4
                        S.emit("tensor", lambda t, h4=h3c, b2=c8, k4=cc4, p=bi2, st=src_t: t.matmul(
                            ps_b[p][0:64, k4 * 128:(k4 + 1) * 128],
                            st[:, h4, b2 * 64:(b2 + 1) * 64], ident, is_transpose=True,
                            start=(k4 == 0), stop=(k4 == 3)),
                            waits=([wsem] + list(bg)) if cc4 == 0 else (),
                            inc="P" if cc4 == 3 else None)
                    S.emit("scalar", lambda s, h4=h3c, g2=cg, p=bi2, dt2=dst_t: s.copy(
                        dt2[0:64, h4, g2 * 4:(g2 + 1) * 4, :], ps_b[p][0:64, :]),
                        waits=[("P", S.val("P"))], inc="A")
                    psb_rel(bi2, "A")
        kdvt_A = S.val("A")
        ebl_done = ebld
        ttv = [0, 0, 0, 0]
        seq_sv = [S.val("V")]

        def precompute_pair(p):
            st = {}
            thunks = []

            def stage_A():
                pga = psum_acq_set((0,))
                for u in range(8):
                    par, hh = u // 4, u % 4
                    c = 2 * p + par
                    csl = slice(c * CH, (c + 1) * CH)
                    S.emit("tensor", lambda t, h3=hh, cs=csl, u2=u: t.matmul(
                        ps_f[0][0:64, u2 * 64:(u2 + 1) * 64], kt_v[:, h3, cs], kb_v[:, h3, cs],
                        start=(u2 == 0), stop=(u2 == 7)),
                        waits=([("V", scald)] + list(pga)) if u == 0 else (),
                        inc="P" if u == 7 else None)
                st["amm"] = S.val("P")
            thunks.append(stage_A)

            def stage_N():
                S.emit("vector", lambda v: v.tensor_tensor(
                    Npr[p % 2][0:64, :].rearrange("q (a b) -> q a b", b=64),
                    ps_f[0][0:64, :].rearrange("q (a b) -> q a b", b=64),
                    bhT_v[0:64, 2 * p:2 * p + 2, :].rearrange("q a b -> q (a b)")
                    .rearrange("q (a o) -> q a o", o=1).to_broadcast([64, 8, 64]),
                    AL.mult), waits=[("P", st["amm"])], inc="V")
                S.emit("vector", lambda v: v.tensor_tensor(
                    Npr[p % 2][0:64, :], Npr[p % 2][0:64, :], nmsk[0:64, :], AL.mult), inc="V")
                psum_rel_set((0,), "V")
                st["n"] = S.val("V")
            thunks.append(stage_N)

            def stage_P0():
                bi2, bg = psb_acq()
                for u in range(8):
                    S.emit("tensor", lambda t, u2=u, pb=bi2: t.matmul(
                        ps_b[pb][0:64, u2 * 64:(u2 + 1) * 64],
                        Npr[p % 2][0:64, u2 * 64:(u2 + 1) * 64],
                        ident[0:64, 0:64], is_transpose=True,
                        start=(u2 == 0), stop=(u2 == 7)),
                        waits=([("V", st["n"])] + list(bg)) if u == 0 else (),
                        inc="P" if u == 7 else None)
                S.emit("scalar", lambda s2, pb=bi2: s2.copy(
                    Ppr[p % 2][0:64, :], ps_b[pb][0:64, :]),
                    waits=[("P", S.val("P"))], inc="A")
                psb_rel(bi2, "A")
                S.emit("vector", lambda v: v.tensor_copy(Qpr[p % 2][0:64, :], Npr[p % 2][0:64, :]),
                       inc="V")
                S.emit("vector", lambda v: v.tensor_tensor(
                    Rpr[p % 2][0:64, :], Ppr[p % 2][0:64, :], erep[0:64, :], AL.add),
                    waits=[("A", S.val("A"))], inc="V")
                st["ev"] = (S.val("V"), S.val("A"))
            thunks.append(stage_P0)

            def mk_lvl_mm(lev):
                def stage():
                    pga = psum_acq_set((0, 1, 2))
                    ev, ea = st["ev"]
                    for u in range(8):
                        usl = slice(u * 64, (u + 1) * 64)
                        S.emit("tensor", lambda t, us=usl: t.matmul(
                            ps_f[0][0:64, us], Ppr[p % 2][0:64, us], Qpr[p % 2][0:64, us],
                            start=(us.start == 0), stop=(us.start == 448)),
                            waits=([("V", ev), ("A", ea)] + list(pga)) if u == 0 else ())
                    for u in range(8):
                        usl = slice(u * 64, (u + 1) * 64)
                        S.emit("tensor", lambda t, us=usl: t.matmul(
                            ps_f[1][0:64, us], Qpr[p % 2][0:64, us], Ppr[p % 2][0:64, us],
                            start=(us.start == 0), stop=(us.start == 448)))
                    for u in range(8):
                        usl = slice(u * 64, (u + 1) * 64)
                        S.emit("tensor", lambda t, us=usl: t.matmul(
                            ps_f[2][0:64, us], Qpr[p % 2][0:64, us], Rpr[p % 2][0:64, us],
                            start=(us.start == 0), stop=(us.start == 448)),
                            inc="P" if u == 7 else None)
                    st["mm"] = S.val("P")
                return stage

            def mk_lvl_ev(lev):
                def stage():
                    S.emit("scalar", lambda s2: s2.copy(Qpr[p % 2][0:64, :], ps_f[0][0:64, :]),
                           waits=[("P", st["mm"])], inc="A")
                    S.emit("scalar", lambda s2: s2.copy(Ppr[p % 2][0:64, :], ps_f[1][0:64, :]),
                           inc="A")
                    S.emit("vector", lambda v: v.tensor_tensor(
                        Rpr[p % 2][0:64, :], ps_f[2][0:64, :], Rpr[p % 2][0:64, :], AL.add),
                        waits=[("P", st["mm"])], inc="V")
                    psum_rel_set((0, 1), "A")
                    psum_rel_set((2,), "V")
                    st["ev"] = (S.val("V"), S.val("A"))
                return stage

            for lev in range(5):
                thunks.append(mk_lvl_mm(lev))
                thunks.append(mk_lvl_ev(lev))

            def stage_TT():
                ev, ea = st["ev"]
                S.emit("vector", lambda v: v.tensor_tensor(
                    TT3[0:64, 2 * p:2 * p + 2, :, :],
                    Rpr[p % 2][0:64, :].rearrange("q (a b c) -> q a b c", a=2, c=64),
                    bhT_v[0:64, 2 * p:2 * p + 2, :].rearrange(
                        "q a (b o) -> q a b o", o=1).to_broadcast([64, 2, HL, 64]),
                    AL.mult), waits=[("V", ev), ("A", ea)], inc="V")
            thunks.append(stage_TT)

            def stage_attnT():
                pga = psum_acq_set((0,))
                for u in range(8):
                    par, hh = u // 4, u % 4
                    c = 2 * p + par
                    csl = slice(c * CH, (c + 1) * CH)
                    S.emit("tensor", lambda t, h3=hh, cs=csl, u2=u: t.matmul(
                        ps_f[0][0:64, u2 * 64:(u2 + 1) * 64], kb_v[:, h3, cs], qt_v[:, h3, cs],
                        start=(u2 == 0), stop=(u2 == 7)),
                        waits=(list(pga) + [("V", scald)]) if u == 0 else (),
                        inc="P" if u == 7 else None)
                amm = S.val("P")
                S.emit("vector", lambda v: v.tensor_tensor(
                    AT3[0:64, 2 * p:2 * p + 2, :, :],
                    ps_f[0][0:64, :].rearrange("q (a b c) -> q a b c", a=2, c=64),
                    umsk[0:64, :].rearrange("q (a b c) -> q a b c", a=2, c=64),
                    AL.mult), waits=[("P", amm)], inc="V")
                psum_rel_set((0,), "V")
                ttv[p] = S.val("V")
            thunks.append(stage_attnT)
            return thunks

        seq_rot = {"i": 0}

        def seq_acq():
            i = 3 + seq_rot["i"] % 3
            seq_rot["i"] += 1
            return i, ps_state["guard"][i]

        def scan_step_pieces(c):
            csl = slice(c * CH, (c + 1) * CH)
            ro = (c % 2) * 64
            rsl = slice(ro, ro + 64)
            pieces = []
            stx = {}

            def piece1():
              pia, pga = seq_acq()
              stx["pia"] = pia
              for hh in range(HL):
                S.emit("tensor", lambda t, h3=hh, cs=csl, p=pia: t.matmul(
                    ps_f[p][0:64, h3 * 128:(h3 + 1) * 128], kt_v[:, h3, cs], Sb_v[:, h3, :],
                    start=(h3 == 0), stop=(h3 == HL - 1)),
                    waits=([("V", max(seq_sv[0], scald)), ("A", kdvt_A)] + list(pga))
                    if hh == 0 else (),
                    inc="P" if hh == HL - 1 else None)
              mm1d = S.val("P")
              S.emit("vector", lambda v, p=pia, c2=c: v.tensor_tensor(
                rhsu[0:64, :], vt_v3[0:64, :, c2, :], ps_f[p][0:64, :], AL.subtract),
                waits=[("P", mm1d), ("A", kdvt_A)], inc="V")
              stx["rhv"] = S.val("V")
              psum_rel(pia, "V")
            pieces.append(piece1)

            def piece2():
              rhv = stx["rhv"]
              pib, pgb = seq_acq()
              for hh in range(HL):
                S.emit("tensor", lambda t, h3=hh, c2=c, p=pib: t.matmul(
                    ps_f[p][0:64, h3 * 128:(h3 + 1) * 128], TT3[0:64, c2, h3, :],
                    rhsu[0:64, h3 * 128:(h3 + 1) * 128],
                    start=(h3 == 0), stop=(h3 == HL - 1)),
                    waits=([("V", max(rhv, ttv[c // 2]))] + list(pgb)) if hh == 0 else (),
                    inc="P" if hh == HL - 1 else None)
              S.emit("scalar", lambda s2, p=pib: s2.copy(u_sb[0:64, :], ps_f[p][0:64, :]),
                     waits=[("P", S.val("P"))], inc="A")
              psum_rel(pib, "A")
              stx["ud"] = S.val("A")
            pieces.append(piece2)

            def piece3():
              ud = stx["ud"]
              pio, pgo = seq_acq()
              for hh in range(HL):
                  S.emit("tensor", lambda t, h3=hh, cs=csl, p=pio, r0=ro: t.matmul(
                      ps_f[p][r0:r0 + 64, h3 * 128:(h3 + 1) * 128], qt_v[:, h3, cs], Sb_v[:, h3, :],
                      start=(h3 == 0), stop=False),
                      waits=([("A", ud)] + list(pgo)) if hh == 0 else ())
              for hh in range(HL):
                  S.emit("tensor", lambda t, h3=hh, c2=c, p=pio, r0=ro: t.matmul(
                      ps_f[p][r0:r0 + 64, h3 * 128:(h3 + 1) * 128], AT3[0:64, c2, h3, :],
                      u_sb[0:64, h3 * 128:(h3 + 1) * 128],
                      start=False, stop=(h3 == HL - 1)),
                      inc="P" if hh == HL - 1 else None)
              od = S.val("P")
              pid, pgd = seq_acq()
              for hh in range(HL):
                  S.emit("tensor", lambda t, h3=hh, c2=c, p=pid: t.matmul(
                      ps_f[p][:, h3 * 128:(h3 + 1) * 128], kdt_v3[0:64, h3, c2, :],
                      u_sb[0:64, h3 * 128:(h3 + 1) * 128],
                      start=(h3 == 0), stop=(h3 == HL - 1)),
                      waits=(list(pgd)) if hh == 0 else (),
                      inc="P" if hh == HL - 1 else None)
              dsd = S.val("P")
              for hh in range(HL):
                  S.emit("vector", lambda v, h3=hh, c2=c, p=pid: v.scalar_tensor_tensor(
                      Sf_v[:, h3, :], Sf_v[:, h3, :], ebl_v[:, h3, c2:c2 + 1],
                      ps_f[p][:, h3 * 128:(h3 + 1) * 128], AL.mult, AL.add),
                      waits=[("P", dsd), ("A", ebl_done)] if hh == 0 else (), inc="V")
              S.emit("vector", lambda v: v.tensor_copy(Sb_v[:, :, :], Sf_v[:, :, :]), inc="V")
              seq_sv[0] = S.val("V")
              psum_rel(pid, "V")
              for hh in range(HL):
                  S.emit("scalar", lambda s2, h3=hh, p=pio, rs=rsl: s2.activation(
                      kdf_v.rearrange("p a b -> p (a b)")[rs, h3 * 128:(h3 + 1) * 128],
                      ps_f[p][rs, h3 * 128:(h3 + 1) * 128], AF.Square,
                      accum_out=ms_sb[rs, h3:h3 + 1]),
                      waits=[("P", od)] if hh == 0 else (), inc="A")
              S.emit("vector", lambda v, rs=rsl: v.tensor_scalar(
                  ms_sb[rs, :], ms_sb[rs, :], 1.0 / DV, 1e-6, AL.mult, AL.add),
                  waits=[("A", S.val("A"))], inc="V")
              S.emit("vector", lambda v, rs=rsl: v.reciprocal(rr_sb[rs, :], ms_sb[rs, :]), inc="V")
              S.emit("scalar", lambda s2, rs=rsl: s2.activation(rr_sb[rs, :], rr_sb[rs, :], AF.Sqrt),
                     waits=[("V", S.val("V"))], inc="A")
              rd = S.val("A")
              for hh in range(HL):
                  ogsl2 = og_sb[rsl, hh * 128:(hh + 1) * 128]
                  gsl = gate_v[rsl, c // 2, hh * 128:(hh + 1) * 128]
                  S.emit("vector", lambda v, h3=hh, p=pio, o=ogsl2, g2=gsl, rs=rsl: v.scalar_tensor_tensor(
                      o, ps_f[p][rs, h3 * 128:(h3 + 1) * 128], rr_sb[rs, h3:h3 + 1],
                      g2, AL.mult, AL.mult),
                      waits=([("A", rd)] + ([("So", S.val("So"))] if c % 2 == 0 else []))
                      if hh == 0 else (), inc="V")
              psum_rel(pio, "V")
              if c % 2 == 1:
                  S.dma("sync", og_d[t0 + (c - 1) * CH:t0 + (c + 1) * CH, :], og_sb,
                        "So", waits=[("V", S.val("V"))])
                  if c == 7 and tc == 5:
                      so_half1[0] = S.val("So")

            pieces.append(piece3)
            return pieces

        th0 = precompute_pair(0)
        for th in th0:
            th()
        for p in range(1, 4):
            ths = precompute_pair(p)
            pcs = scan_step_pieces(2 * (p - 1)) + scan_step_pieces(2 * (p - 1) + 1)
            slots = {1: 0, 3: 1, 5: 2, 7: 3, 9: 4, 11: 5}
            for i, th in enumerate(ths):
                th()
                if i in slots:
                    pcs[slots[i]]()
        for c in (6, 7):
            for pc in scan_step_pieces(c):
                pc()
    attn_st = S.val("So")
    for it in range(NIT):
        S.dma("gpsimd", wdb_d[it, :, :], wd_i[it, :, :], "Wd")
    for it in range(NIT):
        S.dma("gpsimd", wgb_d[it, :, :], wg_i[it, :, :], "Wd")
        S.dma("gpsimd", wub_d[it, :, :], wu_i[it, :, :], "Wd")
    S.emit("gpsimd", lambda g: g.collective_compute(
        "AllGather", AL.bypass, replica_groups=[[0, 1, 2, 3], [4, 5, 6, 7]],
        ins=[og_d[0:3 * T // 4, :]], outs=[ogg_d[0]]),
        waits=[("So", so_half1[0])], inc="C")
    cc_og1 = S.val("C")
    AB.pop(); AFP.pop()
    S.barrier()
    ps_stack.close()
    ps_f.append(nc.ctx.enter_context(nc.psum_tensor("psf6", [128, 512], F32)))
    ps_f.append(nc.ctx.enter_context(nc.psum_tensor("psf7", [128, 512], F32)))
    ps_state["n"] = 8

    AB.push(); AFP.push()
    wo_sb = [AB.alloc(D) for _ in range(16)]
    for ft in range(16):
        S.dma("gpsimd", wo_sb[ft], wo_i[ft * 128:(ft + 1) * 128, :], "Gwo")
    wo_ld = S.val("Gwo")
    gbuf = [AB.alloc(FQ) for _ in range(2)]
    gguard = [0, 0]

    def emit_gathers(kr, ccv):
        for k in kr:
            for r in range(4):
                j = k * 4 + r
                gb2 = gbuf[j % 2]
                gg = "Gg%d" % (j % 2)
                sg = "Sg%d" % (j % 2)
                S.emit("gpsimd", lambda g, o=gb2, j2=j, h=k // 6: g.indirect_dma_start(
                    o, None, ogg_d[h],
                    IndirectOffsetOnAxis(ogidx_s[:, j2:j2 + 1], 0)),
                    waits=[("C", ccv), (("Sg%d" % (j % 2)), gguard[j % 2]),
                           ("Lc", S.val("Lc"))],
                    inc=gg, inc_n=16)
                S.dma("sync", ogsl_d[(r * 8 + k) * 128:(r * 8 + k) * 128 + 128, :], gb2, sg,
                      waits=[(gg, S.val(gg))])
                gguard[j % 2] = S.val(sg)

    og_fm = [AB.alloc(TOK) for _ in range(16)]
    lo_half = [0, 0]

    def emit_ogfm(half):
        lsem = "Lo%d" % half
        c0 = 0 if half == 0 else 768
        c1 = 768 if half == 0 else 1024
        for r in range(4):
            for ftt in range(4):
                S.dma("sync", og_fm[r * 4 + ftt][:, c0:c1],
                      ogsl_d[r * 1024 + c0:r * 1024 + c1,
                             ftt * 128:(ftt + 1) * 128],
                      lsem, waits=[("Sg0", S.val("Sg0")), ("Sg1", S.val("Sg1"))]
                      if (r == 0 and ftt == 0) else (),
                      transpose=True)
        lo_half[half] = (lsem, S.val(lsem))

    emit_gathers(range(6), cc_og1)
    emit_ogfm(0)
    S.emit("gpsimd", lambda g: g.collective_compute(
        "AllGather", AL.bypass, replica_groups=[[0, 1, 2, 3], [4, 5, 6, 7]],
        ins=[og_d[3 * T // 4:T, :]], outs=[ogg_d[1]]),
        waits=[("So", attn_st)], inc="C")
    cc_og2 = S.val("C")
    emit_gathers(range(6, 8), cc_og2)
    emit_ogfm(1)

    xo_t = AFP.alloc(D)
    x1_t = AFP.alloc(D)
    r_scr = AFP.alloc(512)
    st2 = AFP.alloc(8)
    xh2 = AB.alloc(D)
    sq2 = AB.alloc(D)
    mu = st2[:, 0:1]; ssq = st2[:, 1:2]; var = st2[:, 2:3]; rstd = st2[:, 3:4]
    for tt in range(8):
        S.dma("sync", xo_t, x_own[tt * 128:(tt + 1) * 128, :], "Lxo",
              waits=[("V", S.val("V")), ("S1", S.val("S1"))])
        xol = S.val("Lxo")
        for dc in range(4):
            pi, pg = psum_acq()
            for ft in range(16):
                S.emit("tensor", lambda t, f=ft, t2=tt, d2=dc, p=pi: t.matmul(
                    ps_f[p][:, :], og_fm[f][:, t2 * 128:(t2 + 1) * 128],
                    wo_sb[f][:, d2 * 512:(d2 + 1) * 512],
                    start=(f == 0), stop=(f == 15)),
                    waits=([lo_half[0 if tt < 6 else 1], ("Gwo", wo_ld)] + list(pg))
                    if ft == 0 else (),
                    inc="P" if ft == 15 else None)
            S.emit("vector", lambda v, p=pi, d2=dc: v.tensor_tensor(
                r_scr, ps_f[p][:, :], ga_bc[:, d2 * 512:(d2 + 1) * 512], AL.mult),
                waits=[("P", S.val("P")), ("Lb", S.val("Lb"))], inc="V")
            psum_rel(pi, "V")
            S.emit("vector", lambda v, d2=dc: v.tensor_tensor(
                x1_t[:, d2 * 512:(d2 + 1) * 512], r_scr,
                xo_t[:, d2 * 512:(d2 + 1) * 512], AL.add),
                waits=[("Lxo", xol)], inc="V")
        S.dma("sync", x1_d[tt * 128:(tt + 1) * 128, :], x1_t, "S1",
              waits=[("V", S.val("V"))])
        S.emit("vector", lambda v, m=mu: v.reduce_sum(m, x1_t, AX.X), inc="V")
        S.emit("scalar", lambda s, q=ssq: s.activation(sq2, x1_t, AF.Square, accum_out=q),
               waits=[("V", S.val("V"))], inc="A")
        S.emit("vector", lambda v, m=mu: v.tensor_single_scalar(m, m, 1.0 / D, AL.mult),
               inc="V")
        S.emit("vector", lambda v, m=mu, va=var: v.tensor_tensor(va, m, m, AL.mult),
               inc="V")
        S.emit("vector", lambda v, q=ssq, va=var: v.scalar_tensor_tensor(
            va, q, 1.0 / D, va, AL.mult, AL.subtract),
               waits=[("A", S.val("A"))], inc="V")
        S.emit("vector", lambda v, va=var: v.tensor_single_scalar(va, va, 1e-5, AL.add),
               inc="V")
        S.emit("vector", lambda v, va=var, r=rstd: v.reciprocal(r, va), inc="V")
        S.emit("scalar", lambda s, r=rstd: s.activation(r, r, AF.Sqrt),
               waits=[("V", S.val("V"))], inc="A")
        S.emit("vector", lambda v, m=mu, r=rstd: v.tensor_scalar(
            xh2, x1_t, m, r, AL.subtract, AL.mult),
            waits=[("A", S.val("A")), ("S2", S.val("S2"))], inc="V")
        S.dma("sync", xhat_d[tt * 128:(tt + 1) * 128, :], xh2, "S2",
              waits=[("V", S.val("V"))])
    x1_done = S.val("S1")
    ln2_done = S.val("S2")
    AB.pop(); AFP.pop()
    S.barrier()

    AB.push(); AFP.push()
    h2_v = v2(AB.alloc(DT * TOK), DT, TOK)
    for ft in range(DT):
        S.dma("sync", h2_v[:, ft, :], xhat_d[0:TOK, ft * 128:(ft + 1) * 128],
              "Lh2", waits=[("S2", ln2_done)] if ft == 0 else (), transpose=True)
    ldh2 = S.val("Lh2")
    for ft in range(DT):
        S.emit("vector", lambda v, f=ft: v.tensor_scalar(
            h2_v[:, f, :], h2_v[:, f, :], cb1_c[:, f:f + 1], sb_c[:, f:f + 1],
            AL.mult, AL.add), waits=[("Lh2", ldh2)] if ft == 0 else (), inc="V")
    h2_done = S.val("V")
    if DEBUG:
        for ft in range(DT):
            S.dma("sync", dbg_h2[ft * 128:(ft + 1) * 128, :], h2_v[:, ft, :],
                  "Sd", waits=[("V", h2_done)])

    wgb = [v2(AB.alloc(DT * 128), DT, 128) for _ in range(2)]
    wub = [v2(AB.alloc(DT * 128), DT, 128) for _ in range(2)]
    wguard = [0, 0]
    hm_q = [AB.alloc(TOK) for _ in range(4)]
    hs_q = AB.alloc(TOK)
    m_q = [AFP.alloc(TOK) for _ in range(4)]
    cc6 = [AB.alloc(TOK) for _ in range(6)]
    um_ = AB.alloc(TOK)
    msk = AB.alloc(TOK)
    silu_t = AB.alloc(512)

    for j in range(16):
        for q4 in range(4):
            it = q4 * 16 + j
            bi = it % 2
            gf = "Gf%d" % bi
            uf = "Uf%d" % bi
            S.dma("gpsimd", wgb[bi][:, :, :],
                  wgb_d[it, :, :].rearrange("p (a b) -> p a b", a=DT, b=128),
                  gf, waits=[("P", wguard[bi])])
            S.dma("sync", wub[bi][:, :, :],
                  wub_d[it, :, :].rearrange("p (a b) -> p a b", a=DT, b=128),
                  uf, waits=[("P", wguard[bi])])
            wldv = S.val(gf)
            uldv = S.val(uf)
            for th in range(2):
                pi, pg = psum_acq()
                for d in range(DT):
                    S.emit("tensor", lambda t, d2=d, b2=bi, t2=th, p=pi: t.matmul(
                        ps_f[p][:, :], wgb[b2][:, d2, :], h2_v[:, d2, t2 * 512:(t2 + 1) * 512],
                        start=(d2 == 0), stop=(d2 == DT - 1)),
                        waits=([(gf, wldv), ("V", h2_done)] + list(pg)) if d == 0 else (),
                        inc="P" if d == DT - 1 else None)
                pu, pgu = psum_acq()
                for d in range(DT):
                    S.emit("tensor", lambda t, d2=d, b2=bi, t2=th, p=pu: t.matmul(
                        ps_f[p][:, :], wub[b2][:, d2, :], h2_v[:, d2, t2 * 512:(t2 + 1) * 512],
                        start=(d2 == 0), stop=(d2 == DT - 1)),
                        waits=([(uf, uldv)] + list(pgu)) if d == 0 else (),
                        inc="P" if d == DT - 1 else None)
                gud = S.val("P")
                S.emit("scalar", lambda s, p=pi: s.activation(silu_t, ps_f[p][:, :], AF.Silu),
                       waits=[("P", gud), ("V", S.val("V"))], inc="A")
                S.emit("vector", lambda v, q=q4, t2=th, p=pu: v.tensor_tensor(
                    hm_q[q][:, t2 * 512:(t2 + 1) * 512], silu_t, ps_f[p][:, :], AL.mult),
                    waits=[("A", S.val("A"))], inc="V")
                psum_rel(pi, "A"); psum_rel(pu, "V")
            wguard[bi] = S.val("P")
        hmd = S.val("V")
        for q4 in range(4):
            S.emit("scalar", lambda s, q=q4: s.activation(m_q[q], hm_q[q], AF.Square),
                   waits=[("V", hmd)] if q4 == 0 else (), inc="A")
        sqd = S.val("A")
        pairs = [(0, 1), (0, 2), (0, 3), (1, 2), (1, 3), (2, 3)]
        for i6, (a, b2) in enumerate(pairs):
            S.emit("vector", lambda v, i=i6, a2=a, b3=b2: v.tensor_tensor(
                cc6[i], m_q[a2], m_q[b3], AL.is_gt),
                waits=[("A", sqd)] if i6 == 0 else (), inc="V")
        combos = [
            (0, 0, 1, AL.add, 2, AL.add, 2.0, AL.is_ge),
            (1, 3, 4, AL.add, 0, AL.subtract, 1.0, AL.is_ge),
            (2, 5, 1, AL.subtract, 3, AL.subtract, 0.0, AL.is_ge),
            (3, 2, 4, AL.add, 5, AL.add, 1.0, AL.is_le),
        ]
        for q4, i0, i1, op1, i2, op2, thr, opm in combos:
            S.emit("vector", lambda v, a=i0, b3=i1, o=op1: v.tensor_tensor(
                um_, cc6[a], cc6[b3], o),
                waits=[("Sh", S.val("Sh"))] if q4 == 0 else (), inc="V")
            S.emit("vector", lambda v, a=i2, o=op2: v.tensor_tensor(
                um_, um_, cc6[a], o), inc="V")
            S.emit("vector", lambda v, t2=thr, o=opm: v.tensor_single_scalar(
                msk, um_, t2, o), inc="V")
            S.emit("vector", lambda v, q=q4: v.tensor_tensor(
                hs_q, hm_q[q], msk, AL.mult),
                waits=[("Sh", S.val("Sh"))], inc="V")
            it = q4 * 16 + j
            S.dma("sync", hs_d[it, :, :], hs_q, "Sh", waits=[("V", S.val("V"))])
    hs_done = S.val("Sh")
    wd_done = S.val("Wd")
    AB.pop(); AFP.pop()
    S.barrier()

    AB.push(); AFP.push()
    hsr = v2(AB.alloc(NIT * TOK), NIT, TOK)
    wdb4 = [v2(AB.alloc(4 * 512), 4, 512) for _ in range(2)]
    dguard = [0, 0]
    o_t = AFP.alloc(512)
    x1e = AFP.alloc(512)
    ghv = []
    for g8 in range(8):
        gh = "Gh%d" % g8
        S.dma("gpsimd", hsr[:, g8 * 8:(g8 + 1) * 8, :],
              hs_d[g8 * 8:(g8 + 1) * 8, :, :].rearrange("a p t -> p a t"),
              gh, waits=[("Sh", hs_done), ("Wd", wd_done)] if g8 == 0 else ())
        ghv.append((gh, S.val(gh)))
    pis = []
    pgs = []
    for u in range(8):
        pi, pg = psum_acq()
        pis.append(pi)
        pgs.append(list(pg))
    pre_hld = {}

    def emit_wd_load(dq2, c):
        it0 = c * 4
        bi = c % 2
        eng = "sync" if bi == 0 else "gpsimd"
        gd = "Gd%d" % bi
        S.dma(eng, wdb4[bi][:, :, :],
              wdb_d[it0:it0 + 4, :, dq2 * 512:(dq2 + 1) * 512].rearrange("a p d -> p a d"),
              gd, waits=[("P", dguard[bi])])
        pre_hld[(dq2, c)] = S.val(gd)

    for dq in range(4):
        dco = dq * 512
        for c in range(16):
            it0 = c * 4
            bi = c % 2
            gd = "Gd%d" % bi
            if (dq, c) not in pre_hld:
                emit_wd_load(dq, c)
            hld = pre_hld[(dq, c)]
            for ii in range(4):
                it = it0 + ii
                for u in range(8):
                    pi = pis[u]
                    w_ = []
                    if ii == 0 and u == 0:
                        w_ += [(gd, hld), ghv[min(7, (it0 + 3) // 8)]]
                    if it == 0:
                        w_ += pgs[u]
                    S.emit("tensor", lambda t, i2=it, b2=bi, ii2=ii, u2=u, p=pi: t.matmul(
                        ps_f[p][:, :], hsr[:, i2, u2 * 128:u2 * 128 + 128],
                        wdb4[b2][:, ii2, :],
                        start=(i2 == 0), stop=(i2 == NIT - 1)),
                        waits=w_, inc="P" if (ii == 3 and u == 7) else None)
            dguard[bi] = S.val("P")
        dnd = S.val("P")
        if dq < 3:
            emit_wd_load(dq + 1, 0)
            emit_wd_load(dq + 1, 1)
        pgs = []
        new_pis = []
        for u in range(8):
            rows = slice(u * 128, u * 128 + 128)
            S.dma("gpsimd", x1e, x1_d[rows, dco:dco + 512], "Le",
                  waits=[("V", S.val("V"))])
            x1l = S.val("Le")
            pi = pis[u]
            S.emit("vector", lambda v, p=pi, d2=dco: v.tensor_tensor(
                o_t, ps_f[p][:, :], gb_bc[:, d2:d2 + 512], AL.mult),
                waits=[("P", dnd), ("Le", x1l), ("So2", S.val("So2")),
                       ("Lb", S.val("Lb"))], inc="V")
            S.emit("vector", lambda v: v.tensor_tensor(o_t, o_t, x1e, AL.add), inc="V")
            S.dma("sync", out_s[rows, dco:dco + 512], o_t, "So2",
                  waits=[("V", S.val("V"))])
            psum_rel(pi, "V")
            npi, npg = psum_acq()
            new_pis.append(npi)
            pgs.append(list(npg))
        pis = new_pis
    final_st = S.val("So2")
    S.emit("sync", lambda sy: sy.nop(), waits=[("So2", final_st)])
    AB.pop(); AFP.pop()

    return nc, S, dbg


def realize(nc, S):
    with contextlib.ExitStack() as es:
        for name in S.cnt:
            S.sems[name] = es.enter_context(nc.semaphore(name))
        with nc.Block() as block:
            for eng in S.ENGINES:
                if not S.ops[eng]:
                    continue
                getattr(block, eng)(lambda e, _eng=eng: [cl(e) for cl in S.ops[_eng]])
    return nc


_CACHE = {}


def get_nc():
    if "nc" not in _CACHE:
        nc, S, dbg = build_program()
        realize(nc, S)
        _CACHE["nc"] = nc
        _CACHE["dbg"] = dbg
    return _CACHE["nc"]


def _prep_inputs(inputs):
    f32 = np.float32
    x = np.ascontiguousarray(np.asarray(inputs["L0_attn_x"], f32))
    wq = np.asarray(inputs["L0_attn_Wq"], f32)
    wk = np.asarray(inputs["L0_attn_Wk"], f32)
    wv = np.asarray(inputs["L0_attn_Wv"], f32)
    wo = np.ascontiguousarray(np.asarray(inputs["L0_attn_Wo"], f32))
    wad = np.ascontiguousarray(np.asarray(inputs["L0_attn_Wad"], f32))
    wau = np.asarray(inputs["L0_attn_Wau"], f32)
    wb = np.asarray(inputs["L0_attn_Wb"], f32)
    wgd = np.ascontiguousarray(np.asarray(inputs["L0_attn_Wgd"], f32))
    wgu = np.asarray(inputs["L0_attn_Wgu"], f32)
    cq = np.asarray(inputs["L0_attn_conv_q_w"], f32)
    ck = np.asarray(inputs["L0_attn_conv_k_w"], f32)
    cv = np.asarray(inputs["L0_attn_conv_v_w"], f32)
    rms = np.asarray(inputs["L0_attn_rms_w"], f32)
    wg = np.asarray(inputs["L0_ffn_w_gate"], f32)
    wu = np.asarray(inputs["L0_ffn_w_up"], f32)
    wd = np.asarray(inputs["L0_ffn_w_down"], f32)
    cvec = np.asarray(inputs["L0_c"], f32)
    wmod = np.asarray(inputs["L0_wmod"], f32)

    wg_p = wg.reshape(D, FI // 4, 4).transpose(0, 2, 1).reshape(D, FI)
    wu_p = wu.reshape(D, FI // 4, 4).transpose(0, 2, 1).reshape(D, FI)
    wd_p = wd.reshape(FI // 4, 4, D).transpose(1, 0, 2).reshape(FI, D)
    wg_t = np.ascontiguousarray(
        wg_p.reshape(DT, 128, NIT, 128).transpose(2, 1, 0, 3).reshape(NIT, 128, DT * 128))
    wu_t = np.ascontiguousarray(
        wu_p.reshape(DT, 128, NIT, 128).transpose(2, 1, 0, 3).reshape(NIT, 128, DT * 128))
    wd_t = np.ascontiguousarray(wd_p.reshape(NIT, 128, D))

    ident = np.eye(128, dtype=f32)
    masks = np.zeros((64, 192), f32)
    masks[:, 0:64] = np.tril(np.ones((64, 64), f32), -1)
    masks[:, 64:128] = np.triu(np.ones((64, 64), f32))
    masks[:, 128:192] = np.eye(64, dtype=f32)
    mrep = np.zeros((64, 1536), f32)
    mrep[:, 0:512] = np.tile(-np.tril(np.ones((64, 64), f32), -1), (1, 8))
    mrep[:, 512:1024] = np.tile(np.triu(np.ones((64, 64), f32)), (1, 8))
    mrep[:, 1024:1536] = np.tile(np.eye(64, dtype=f32), (1, 8))
    ones1 = np.ones((1, 128), f32)
    rms_bc = np.ascontiguousarray(np.broadcast_to(np.tile(rms, HL), (128, FQ)))
    cT = np.ascontiguousarray(cvec.T)

    def diag_tiles(cw, lo):
        out = np.zeros((HL, CKW, 128, 128), f32)
        for hh in range(HL):
            for j in range(CKW):
                np.fill_diagonal(out[hh, j], cw[lo + hh * 128: lo + (hh + 1) * 128, j])
        return out

    in_maps = []
    for c in range(8):
        b = c // 4
        qd = c % 4
        fs = slice(qd * FQ, (qd + 1) * FQ)
        hsl = slice(qd * HL, (qd + 1) * HL)
        own_tiles = [k * 4 + qd for k in range(8)]
        ogidx = np.zeros((128, 32), np.int32)
        for jj in range(32):
            k, r = jj // 4, jj % 4
            g = own_tiles[k]
            if k < 6:
                ogidx[:, jj] = r * 3072 + g * 128 + np.arange(128)
            else:
                ogidx[:, jj] = r * 1024 + (g * 128 - 3072) + np.arange(128)
        m = {
            "x_b": x[b],
            "x_own": np.ascontiguousarray(
                x[b].reshape(32, 128, D)[own_tiles].reshape(TOK, D)),
            "wq": np.ascontiguousarray(wq[:, fs]),
            "wk": np.ascontiguousarray(wk[:, fs]),
            "wv": np.ascontiguousarray(wv[:, fs]),
            "cdq": diag_tiles(cq, qd * FQ),
            "cdk": diag_tiles(ck, qd * FQ),
            "cdv": diag_tiles(cv, qd * FQ),
            "wad": wad, "wgd": wgd,
            "wau": np.ascontiguousarray(wau[:, fs]),
            "wgu": np.ascontiguousarray(wgu[:, fs]),
            "wb": np.ascontiguousarray(wb[:, hsl]),
            "rms_bc": rms_bc,
            "wo": wo,
            "wmod_s": np.ascontiguousarray(wmod[:, c * 1536:(c + 1) * 1536]),
            "c_T": cT,
            "wg_t": wg_t, "wu_t": wu_t, "wd_t": wd_t,
            "ident_in": ident, "masks_in": masks, "mrep_in": mrep, "ones1_in": ones1,
            "bsel128": np.full((128, 1), float(b), f32),
            "ogidx": ogidx,
        }
        in_maps.append(m)
    return in_maps


def kernel(**inputs):
    nc = get_nc()
    in_maps = _prep_inputs(inputs)
    res = run_bass_kernel_spmd(nc, in_maps, core_ids=list(range(8)))
    out = np.empty((B, T, D), np.float32)
    for c in range(8):
        b, qd = c // 4, c % 4
        o = res.results[c]["out_s"].reshape(8, 128, D)
        for k in range(8):
            t = k * 4 + qd
            out[b, t * 128:(t + 1) * 128] = o[k]
    return out


# version 12
# speedup vs baseline: 1.0642x; 1.0642x over previous
import contextlib
import numpy as np
import ml_dtypes
import concourse.bass as bass
import concourse.mybir as mybir
from concourse.bass import IndirectOffsetOnAxis
from concourse.bass_utils import run_bass_kernel_spmd

F32 = mybir.dt.float32
BF16 = mybir.dt.bfloat16
I32 = mybir.dt.int32
AF = mybir.ActivationFunctionType
AL = mybir.AluOpType
AX = mybir.AxisListType

B, T, D = 2, 4096, 2048
H, DK, DV, CKW, CH = 16, 128, 128, 4, 64
HL, FQ, TOK = 4, 512, 1024
TC, NTC = 512, 8
NCH_C = TC // CH
DT = D // 128
FI = 8192
NIT = FI // 128

DEBUG = False


class Sched:
    ENGINES = ("tensor", "vector", "scalar", "gpsimd", "sync")

    def __init__(self):
        self.ops = {e: [] for e in self.ENGINES}
        self.cnt = {}
        self.sems = {}

    def _sem(self, name):
        self.cnt.setdefault(name, 0)
        return name

    def val(self, name):
        return self.cnt.get(name, 0)

    def allv(self, *names):
        return [(n, self.val(n)) for n in names]

    def emit(self, eng, fn, waits=(), inc=None, inc_n=1):
        waits = list(waits)
        if eng == "vector":
            waits.append(("V", self.val("V")))
            if inc is None:
                inc = "V"
        elif eng == "scalar":
            waits.append(("A", self.val("A")))
            if inc is None:
                inc = "A"
        if inc is not None:
            self._sem(inc)
            self.cnt[inc] += inc_n
        waits = tuple((self._sem(s), v) for s, v in waits if v > 0)
        incn = (inc, inc_n) if inc else None

        def closure(engine, _fn=fn, _w=waits, _i=incn):
            for s, v in _w:
                engine.wait_ge(self.sems[s], v)
            ins = _fn(engine)
            if _i is not None:
                ins.then_inc(self.sems[_i[0]], _i[1])
        self.ops[eng].append(closure)

    def barrier(self):
        self._sem("B")
        n0 = self.val("B")
        pre = [(k, v) for k, v in self.cnt.items() if k != "B" and v > 0]
        for eng in self.ENGINES:
            self.emit(eng, lambda e: e.nop(), inc="B",
                      waits=pre if eng == "sync" else ())
        tgt = n0 + len(self.ENGINES)
        assert self.cnt["B"] == tgt
        for eng in self.ENGINES:
            self.emit(eng, lambda e: e.nop(), waits=[("B", tgt)])

    def dma(self, eng, out, in_, sem="L", waits=(), transpose=False):
        if transpose:
            self.emit(eng, lambda e, o=out, i=in_: e.dma_start(out=o, in_=i, transpose=True),
                      waits=waits, inc=sem, inc_n=16)
        else:
            self.emit(eng, lambda e, o=out, i=in_: e.dma_start(out=o, in_=i),
                      waits=waits, inc=sem, inc_n=16)


class Arena:
    def __init__(self, nc, name, cols, dtype):
        self.t = nc.ctx.enter_context(nc.sbuf_tensor(name, [128, cols], dtype))
        self.cols = cols
        self.off = 0
        self.marks = []

    def alloc(self, n, align=8):
        o = (self.off + align - 1) // align * align
        assert o + n <= self.cols, f"arena overflow {o}+{n}>{self.cols}"
        self.off = o + n
        return self.t[:, o:o + n]

    def push(self):
        self.marks.append(self.off)

    def pop(self):
        self.off = self.marks.pop()


def v2(t, a, b2):
    return t.rearrange("p (a b) -> p a b", a=a, b=b2)


def v3(t, a, b2, c3):
    return t.rearrange("p (a b c) -> p a b c", a=a, b=b2, c=c3)


def build_program():
    nc = bass.Bass("TRN2", target_bir_lowering=False)
    S = Sched()

    def din(name, shape, dt=F32):
        return nc.dram_tensor(name, shape, dt, kind="ExternalInput").ap()

    x_b = din("x_b", [T, D])
    x_own = din("x_own", [TOK, D])
    wq_i = din("wq", [D, FQ]); wk_i = din("wk", [D, FQ]); wv_i = din("wv", [D, FQ])
    cd_i = {n: din("cd" + n, [HL, CKW, 128, 128]) for n in "qkv"}
    wad_i = din("wad", [D, DK]); wau_i = din("wau", [DK, FQ])
    wgd_i = din("wgd", [D, DV]); wgu_i = din("wgu", [DV, FQ])
    wb_i = din("wb", [D, HL])
    rms_i = din("rms_bc", [128, FQ])
    wo_i = din("wo", [D, D])
    wmod_i = din("wmod_s", [D, 1536])
    cT_i = din("c_T", [D, B])
    wg_i = din("wg_t", [NIT, 128, DT * 128])
    wu_i = din("wu_t", [NIT, 128, DT * 128])
    wd_i = din("wd_t", [NIT, 128, D])
    ident_i = din("ident_in", [128, 128])
    masks_i = din("masks_in", [64, 192])
    mrep_i = din("mrep_in", [64, 1536])
    ones1_i = din("ones1_in", [1, 128])
    bsel_i = din("bsel128", [128, 1])
    ogidx_i = din("ogidx", [128, 32], dt=I32)
    out_s = nc.dram_tensor("out_s", [TOK, D], F32, kind="ExternalOutput").ap()

    xhat_d = nc.dram_tensor("xhat_d", [T, D], BF16).ap()
    modp_d = nc.dram_tensor("modp_d", [1536, B], F32).ap()
    modf_d = nc.dram_tensor("modf_d", [12288, B], F32).ap()
    og_d = nc.dram_tensor("og_d", [T, FQ], BF16).ap()
    ogg_d = [nc.dram_tensor("ogg_d1", [3 * T, FQ], BF16).ap(),
             nc.dram_tensor("ogg_d2", [T, FQ], BF16).ap()]
    ogsl_d = nc.dram_tensor("ogsl_d", [4 * TOK, FQ], BF16).ap()
    x1_d = nc.dram_tensor("x1_d", [TOK, D], F32).ap()
    grow_d = nc.dram_tensor("grow_d", [2, D], F32).ap()
    hs_d = nc.dram_tensor("hs_d", [NIT, 128, TOK], BF16).ap()
    wdb_d = nc.dram_tensor("wdb_d", [NIT, 128, D], BF16).ap()
    wgb_d = nc.dram_tensor("wgb_d", [NIT, 128, DT * 128], BF16).ap()
    wub_d = nc.dram_tensor("wub_d", [NIT, 128, DT * 128], BF16).ap()

    dbg = {}

    def ddbg(name, shape, dt):
        if DEBUG:
            dbg[name] = nc.dram_tensor(name, shape, dt, kind="ExternalOutput").ap()
            return dbg[name]
        return None
    dbg_h = ddbg("dbg_h", [D, T], BF16)
    dbg_qn = ddbg("dbg_qn", [FQ, T], BF16)
    dbg_kn = ddbg("dbg_kn", [FQ, T], BF16)
    dbg_sp = ddbg("dbg_sp", [FQ, T], F32)
    dbg_h2 = ddbg("dbg_h2", [D, TOK], BF16)

    AB = Arena(nc, "ab", 87136, BF16)
    AFP = Arena(nc, "af", 9608, F32)
    ogidx_s = nc.ctx.enter_context(nc.sbuf_tensor("ogidx_s", [128, 32], I32))

    ps_stack = contextlib.ExitStack()
    ps_big = nc.ctx.enter_context(nc.psum_tensor("psbig", [128, 6 * 512], F32))
    ps_f = [ps_big[:, i * 512:(i + 1) * 512] for i in range(6)]
    ps_b = [ps_stack.enter_context(nc.psum_tensor(f"psb{i}", [64, 512], BF16)) for i in range(2)]
    ps_state = {"i": 0, "guard": [()] * 8, "bi": 0, "bguard": [()] * 2, "n": 6}

    def psum_acq():
        i = ps_state["i"] % ps_state["n"]
        ps_state["i"] += 1
        return i, ps_state["guard"][i]

    def psum_rel(i, sem):
        ps_state["guard"][i] = ((sem, S.val(sem)),)

    def psb_acq():
        i = ps_state["bi"] % 2
        ps_state["bi"] += 1
        return i, ps_state["bguard"][i]

    def psb_rel(i, sem):
        ps_state["bguard"][i] = ((sem, S.val(sem)),)

    def psum_acq_set(idxs):
        g = []
        for i in idxs:
            g.extend(ps_state["guard"][i])
        return tuple(g)

    def psum_rel_set(idxs, sem):
        for i in idxs:
            ps_state["guard"][i] = ((sem, S.val(sem)),)

    ident = AB.alloc(128)
    mask_sl = AB.alloc(64)
    mask_ui = AB.alloc(64)
    eye64 = AB.alloc(64)
    ones_bf = AB.alloc(128)
    nmsk = AB.alloc(512)
    umsk = AB.alloc(512)
    erep = AB.alloc(512)
    ones1_f = AFP.alloc(128)
    identF = AFP.alloc(128)
    rms_sb = AFP.alloc(FQ)
    sa_c = AFP.alloc(DT); ca1_c = AFP.alloc(DT)
    sb_c = AFP.alloc(DT); cb1_c = AFP.alloc(DT)
    ga_bc = AFP.alloc(D); gb_bc = AFP.alloc(D)
    bsel_s = AFP.alloc(1)

    S.dma("gpsimd", ident, ident_i[:, :], "Gc")
    S.dma("gpsimd", mask_sl[0:64, :], masks_i[:, 0:64], "Gc")
    S.dma("gpsimd", mask_ui[0:64, :], masks_i[:, 64:128], "Gc")
    S.dma("gpsimd", eye64[0:64, :], masks_i[:, 128:192], "Gc")
    S.dma("gpsimd", nmsk[0:64, :], mrep_i[:, 0:512], "Gc")
    S.dma("gpsimd", umsk[0:64, :], mrep_i[:, 512:1024], "Gc")
    S.dma("gpsimd", erep[0:64, :], mrep_i[:, 1024:1536], "Gc")
    S.emit("vector", lambda v: v.memset(ones_bf, 1.0), inc="V")
    S.dma("sync", ones1_f[0:1, :], ones1_i[:, :], "Lc")
    S.dma("sync", identF, ident_i[:, :], "Lc")
    S.dma("sync", rms_sb, rms_i[:, :], "Lc")
    S.dma("sync", bsel_s, bsel_i[:, :], "Lc")
    S.dma("sync", ogidx_s[:, :], ogidx_i[:, :], "Lc")

    AB.push(); AFP.push()
    csil_v = v2(AB.alloc(DT * B), DT, B)
    modp_v = v2(AFP.alloc(12 * B), 12, B)
    S.dma("gpsimd", csil_v[:, :, :], cT_i.rearrange("(a p) b -> p a b", p=128), "Gc")
    S.emit("scalar", lambda s: s.activation(csil_v[:, :, :], csil_v[:, :, :], AF.Silu),
           waits=[("Gc", S.val("Gc"))], inc="A")
    csil_done = S.val("A")

    wmod_v = [v2(AB.alloc(128 * DT), DT, 128) for _ in range(2)]
    for jt in range(12):
        buf = wmod_v[jt % 2]
        gm = "Gm%d" % (jt % 2)
        wpe = [("P", S.val("P"))] if jt >= 2 else []
        S.dma("gpsimd", buf[:, :, :],
              wmod_i[:, jt * 128:(jt + 1) * 128].rearrange("(a p) j -> p a j", p=128),
              gm, waits=wpe)
        pi, pg = psum_acq()
        ldv = S.val(gm)
        for d in range(DT):
            S.emit("tensor", lambda t, b2=buf, d2=d, p=pi: t.matmul(
                ps_f[p][0:128, 0:B], b2[:, d2, :], csil_v[:, d2, :],
                start=(d2 == 0), stop=(d2 == DT - 1)),
                waits=([(gm, ldv), ("A", csil_done)] + list(pg)) if d == 0 else (),
                inc="P" if d == DT - 1 else None)
        S.emit("scalar", lambda s, p=pi, j=jt: s.copy(modp_v[:, j, :], ps_f[p][0:128, 0:B]),
               waits=[("P", S.val("P"))], inc="A")
        psum_rel(pi, "A")
    S.dma("sync", modp_d.rearrange("(a p) b -> p a b", p=128), modp_v[:, :, :], "Sm",
          waits=[("A", S.val("A"))])
    S.emit("gpsimd", lambda g: g.collective_compute(
        "AllGather", AL.bypass, replica_groups=[list(range(8))],
        ins=[modp_d], outs=[modf_d]),
        waits=[("Sm", S.val("Sm"))], inc="C")
    cc_mod = S.val("C")

    m2 = {}
    for grp, nm in ((0, "sa"), (1, "ca"), (2, "ga"), (3, "sb"), (4, "cb"), (5, "gb")):
        m2[nm] = v2(AFP.alloc(DT * B), DT, B)
        S.dma("sync", m2[nm],
              modf_d[grp * D:(grp + 1) * D, :].rearrange("(a p) b -> p a b", p=128),
              "Lm", waits=[("C", cc_mod)])
    mod_ld = S.val("Lm")
    ga_c = AFP.alloc(DT); gb_c = AFP.alloc(DT)

    def select2(dst, key):
        src = m2[key]
        S.emit("vector", lambda v: v.tensor_tensor(dst, src[:, :, 1], src[:, :, 0], AL.subtract),
               waits=[("Lm", mod_ld)], inc="V")
        S.emit("vector", lambda v: v.scalar_tensor_tensor(
            dst, dst, bsel_s, src[:, :, 0], AL.mult, AL.add),
            waits=[("Lc", S.val("Lc"))], inc="V")

    select2(sa_c, "sa"); select2(ca1_c, "ca"); select2(sb_c, "sb")
    select2(cb1_c, "cb"); select2(ga_c, "ga"); select2(gb_c, "gb")
    S.emit("vector", lambda v: v.tensor_single_scalar(ca1_c, ca1_c, 1.0, AL.add), inc="V")
    S.emit("vector", lambda v: v.tensor_single_scalar(cb1_c, cb1_c, 1.0, AL.add), inc="V")

    sgn = AFP.alloc(DT)
    growsb = AFP.alloc(D)
    for gi, (gc, bcd) in enumerate(((ga_c, ga_bc), (gb_c, gb_bc))):
        S.emit("vector", lambda v, g=gc: v.tensor_single_scalar(sgn, g, 0.0, AL.is_ge),
               waits=[("P", S.val("P"))], inc="V")
        S.emit("vector", lambda v: v.tensor_scalar(sgn, sgn, 2.0, -1.0, AL.mult, AL.add), inc="V")
        S.emit("scalar", lambda s, g=gc: s.activation(g, g, AF.Abs),
               waits=[("V", S.val("V"))], inc="A")
        S.emit("vector", lambda v, g=gc: v.tensor_single_scalar(g, g, 0.25, AL.max),
               waits=[("A", S.val("A"))], inc="V")
        S.emit("vector", lambda v, g=gc: v.tensor_tensor(g, g, sgn, AL.mult), inc="V")
        gd = S.val("V")
        S.emit("sync", lambda e, g=gc, gi2=gi: e.dma_start(
            out=grow_d[gi2:gi2 + 1, :].rearrange("one (a p) -> p a one", p=128),
            in_=g.rearrange("p (a one) -> p a one", one=1),
            allow_slow_non_contiguous=True),
            waits=[("V", gd)], inc="Sm", inc_n=16)
        S.dma("sync", bcd.rearrange("p (one d) -> p one d", one=1),
              grow_d[gi:gi + 1, :].partition_broadcast(128), "Lb",
              waits=[("Sm", S.val("Sm"))])
    AB.push(); AFP.push()
    NLB = 4
    x_t4 = [[AB.alloc(D) for _ in range(NLB)] for _ in range(2)]
    xh_t4 = [[AB.alloc(D) for _ in range(NLB)] for _ in range(2)]
    sq_scr = AB.alloc(D)
    stat4 = AFP.alloc(4 * NLB)
    ln_guard = [(0, 0), (0, 0)]
    ln_ch = []

    for ib in range(T // 128 // NLB):
        bi = ib % 2
        lx = "Gx%d" % bi
        sx = "Sx%d" % bi
        gv, gst = ln_guard[bi]
        for k in range(NLB):
            i = ib * NLB + k
            S.dma("gpsimd", x_t4[bi][k], x_b[i * 128:(i + 1) * 128, :], lx,
                  waits=[("V", gv), (sx, gst)] if k == 0 else ())
        ldv = S.val(lx)
        mu4 = stat4[:, 0:NLB]; ssq4 = stat4[:, NLB:2 * NLB]
        var4 = stat4[:, 2 * NLB:3 * NLB]; rstd4 = stat4[:, 3 * NLB:4 * NLB]
        for k in range(NLB):
            S.emit("vector", lambda v, x=x_t4[bi][k], k2=k: v.reduce_sum(
                stat4[:, k2:k2 + 1], x, AX.X),
                waits=[(lx, ldv), ("A", S.val("A"))] if k == 0 else (), inc="V")
        for k in range(NLB):
            S.emit("scalar", lambda st, x=x_t4[bi][k], k2=k: st.activation(
                sq_scr, x, AF.Square, accum_out=stat4[:, NLB + k2:NLB + k2 + 1]),
                waits=[(lx, ldv), ("V", S.val("V"))] if k == 0 else (), inc="A")
        S.emit("vector", lambda v: v.tensor_single_scalar(mu4, mu4, 1.0 / D, AL.mult), inc="V")
        S.emit("vector", lambda v: v.tensor_tensor(var4, mu4, mu4, AL.mult), inc="V")
        S.emit("vector", lambda v: v.scalar_tensor_tensor(var4, ssq4, 1.0 / D, var4,
                                                          AL.mult, AL.subtract),
               waits=[("A", S.val("A"))], inc="V")
        S.emit("vector", lambda v: v.tensor_single_scalar(var4, var4, 1e-5, AL.add), inc="V")
        S.emit("vector", lambda v: v.reciprocal(rstd4, var4), inc="V")
        S.emit("scalar", lambda st: st.activation(rstd4, rstd4, AF.Sqrt),
               waits=[("V", S.val("V"))], inc="A")
        for k in range(NLB):
            S.emit("vector", lambda v, x=x_t4[bi][k], o=xh_t4[bi][k], k2=k: v.tensor_scalar(
                o, x, stat4[:, k2:k2 + 1], stat4[:, 3 * NLB + k2:3 * NLB + k2 + 1],
                AL.subtract, AL.mult),
                waits=[("A", S.val("A"))] if k == 0 else (), inc="V")
        xv = S.val("V")
        for k in range(NLB):
            i = ib * NLB + k
            S.dma("sync", xhat_d[i * 128:(i + 1) * 128, :], xh_t4[bi][k], sx,
                  waits=[("V", xv)] if k == 0 else ())
        ln_guard[bi] = (xv, S.val(sx))
        ln_ch.append((sx, S.val(sx)))

    ln_done = [("Sx0", S.val("Sx0")), ("Sx1", S.val("Sx1"))]
    AB.pop(); AFP.pop()
    AB.pop(); AFP.pop()
    S.barrier()

    AB.push(); AFP.push()
    wq_v = v2(AB.alloc(FQ * DT), DT, FQ)
    wk_v = v2(AB.alloc(FQ * DT), DT, FQ)
    wv_v = v2(AB.alloc(FQ * DT), DT, FQ)
    wad_v = v2(AB.alloc(DK * DT), DT, DK)
    wgd_v = v2(AB.alloc(DV * DT), DT, DV)
    wb_v = v2(AB.alloc(HL * DT), DT, HL)
    wau_sb = AB.alloc(FQ); wgu_sb = AB.alloc(FQ)
    cd_sb = {n: v3(AB.alloc(HL * CKW * 128), HL, CKW, 128) for n in "qkv"}
    for dst, src in ((wq_v, wq_i), (wk_v, wk_i), (wv_v, wv_i),
                     (wad_v, wad_i), (wgd_v, wgd_i), (wb_v, wb_i)):
        for d in range(DT):
            S.dma("gpsimd", dst[:, d, :], src[d * 128:(d + 1) * 128, :], "Gw")
    S.dma("gpsimd", wau_sb, wau_i[:, :], "Gw")
    S.dma("gpsimd", wgu_sb, wgu_i[:, :], "Gw")
    for n in "qkv":
        for hh in range(HL):
            for j in range(CKW):
                S.dma("gpsimd", cd_sb[n][:, hh, j, :], cd_i[n][hh, j, :, :], "Gw")
    wld = S.val("Gw")


    h_v = v2(AB.alloc(DT * TC), DT, TC)
    pre = {n: v2(AB.alloc(HL * (TC + 8)), HL, TC + 8) for n in "qkv"}
    sil = {n: pre[n][:, :, 0:TC] for n in "qkv"}
    qn_v = sil["q"]
    kn_v = sil["k"]
    qt_v = sil["q"]
    kt_v = v2(AB.alloc(HL * TC), HL, TC)
    kb_v = v2(AB.alloc(HL * TC), HL, TC)
    kdf_v = v2(AB.alloc(HL * TC), HL, TC)
    kdt_v3 = v3(AB.alloc(HL * NCH_C * 128), HL, NCH_C, 128)
    vt_v3 = v3(AB.alloc(HL * NCH_C * 128), HL, NCH_C, 128)
    sqs = AB.alloc(HL * TC)
    z_sb = sqs[:, 512:1024]; zg_sb = sqs[:, 0:512]
    eb_raw = AB.alloc(HL * TC)
    eb_b = v2(eb_raw, HL, TC)
    ebm_b = v2(AB.alloc(HL * TC), HL, TC)
    ebd_b = ebm_b
    sp_b = v2(AFP.alloc(HL * TC), HL, TC)
    cum_b = v2(AFP.alloc(HL * TC), HL, TC)
    r_big = sp_b.rearrange("p a b -> p (a b)")
    ebl_v = v2(AFP.alloc(HL * NCH_C), HL, NCH_C)
    bhr = AB.alloc(TC)
    bhT_v = v2(AB.alloc(NCH_C * HL), NCH_C, HL)
    gate_v = v2(AB.alloc(4 * TC), 4, TC)
    TT3 = v3(eb_raw, NCH_C, HL, 64)
    AT3 = v3(sqs, NCH_C, HL, 64)
    Npr = [AB.alloc(512) for _ in range(2)]
    Ppr = [AB.alloc(512) for _ in range(2)]
    Qpr = [AB.alloc(512) for _ in range(2)]
    Rpr = [AB.alloc(512) for _ in range(2)]
    og_sb = AB.alloc(FQ)
    Sf_v = v2(AFP.alloc(HL * DV), HL, DV)
    Sb_v = v2(AB.alloc(HL * DV), HL, DV)
    rhsu = AB.alloc(FQ)
    u_sb = rhsu
    ms_sb = AFP.alloc(HL)
    rr_sb = AFP.alloc(HL)
    S.emit("vector", lambda v: v.memset(Sf_v[:, :, :], 0.0), inc="V")
    S.emit("vector", lambda v: v.memset(Sb_v[:, :, :], 0.0), inc="V")
    for n in "qkv":
        S.emit("vector", lambda v, n2=n: v.memset(pre[n2][:, :, 0:3], 0.0), inc="V")

    sab = AFP.alloc(16)
    sa_bf = AB.alloc(DT)
    S.emit("vector", lambda v: v.tensor_copy(sa_bf, sa_c), inc="V")
    sabf_v = S.val("V")
    blist = []
    for base, wvv in ((0, wq_v), (4, wk_v), (8, wv_v)):
        for hh in range(HL):
            blist.append((lambda d, w=wvv, h3=hh: w[:, d, h3 * 128:(h3 + 1) * 128],
                          base + hh, 128))
    blist.append(((lambda d: wad_v[:, d, :]), 12, 128))
    blist.append(((lambda d: wgd_v[:, d, :]), 13, 128))
    blist.append(((lambda d: wb_v[:, d, :]), 14, HL))
    pi_b, pg_b = psum_acq()
    lastb = (len(blist) - 1, DT - 1)
    for ui, (wf, col, rows) in enumerate(blist):
        for d in range(DT):
            S.emit("tensor", lambda t, f=wf, d2=d, c2=col, r=rows, p=pi_b,
                   st=(d == 0), sp=(d == DT - 1): t.matmul(
                ps_f[p][0:r, c2:c2 + 1], f(d2), sa_bf[:, d2:d2 + 1],
                start=st, stop=sp),
                waits=([("Gw", wld), ("V", sabf_v)] + list(pg_b))
                if (ui == 0 and d == 0) else (),
                inc="P" if (ui, d) == lastb else None)
    S.emit("scalar", lambda s2, p=pi_b: s2.copy(sab[:, 0:14], ps_f[p][:, 0:14]),
           waits=[("P", S.val("P"))], inc="A")
    S.emit("scalar", lambda s2, p=pi_b: s2.copy(sab[0:HL, 14:15], ps_f[p][0:HL, 14:15]),
           inc="A")
    psum_rel(pi_b, "A")
    for wvv, wdim in ((wq_v, FQ), (wk_v, FQ), (wv_v, FQ),
                      (wad_v, DK), (wgd_v, DV), (wb_v, HL)):
        S.emit("vector", lambda v, w=wvv, n2=wdim: v.tensor_tensor(
            w[:, :, :], w[:, :, :],
            ca1_c.rearrange("p (a one) -> p a one", one=1).to_broadcast([128, DT, n2]),
            AL.mult),
            waits=[("P", S.val("P"))], inc="V")
    wscale_done = S.val("V")

    ldh_map = {}
    h_free_hist = [0] * NTC

    def emit_h(tc2, pwait):
        lh = "Lh%d" % (tc2 % 2)
        t02 = tc2 * TC
        for ft in range(DT):
            S.dma("sync", h_v[:, ft, :], xhat_d[t02:t02 + TC, ft * 128:(ft + 1) * 128],
                  lh, waits=([ln_ch[tc2], ("P", pwait)]) if ft == 0 else (),
                  transpose=True)
        ldh_map[tc2] = (lh, S.val(lh))

    h_free = [0]
    so_half1 = [0]
    emit_h(0, 0)
    for tc in range(NTC):
        t0 = tc * TC
        hw = ldh_map[tc]
        if DEBUG:
            for ft in range(DT):
                S.dma("sync", dbg_h[ft * 128:(ft + 1) * 128, t0:t0 + TC], h_v[:, ft, :],
                      "Sd", waits=[hw])

        if tc > 0:
            for n in "qkv":
                for hh in range(HL):
                    S.emit("vector", lambda v, n2=n, h3=hh: v.tensor_copy(
                        pre[n2][:, h3, 0:3], pre[n2][:, h3, TC:TC + 3]),
                        waits=[("A", S.val("A"))] if (n == "q" and hh == 0) else (), inc="V")
        halo_done = S.val("V")
        evac_a = {}
        for uo, (n, wvv) in enumerate((("q", wq_v), ("k", wk_v), ("v", wv_v))):
            for hh in range(HL):
                pi, pg = psum_acq()
                for d in range(DT):
                    S.emit("tensor", lambda t, w=wvv, d2=d, h3=hh, p=pi, h2=h_v: t.matmul(
                        ps_f[p][:, :], w[:, d2, h3 * 128:(h3 + 1) * 128], h2[:, d2, :],
                        start=(d2 == 0), stop=(d2 == DT - 1)),
                        waits=([hw, ("V", wscale_done)] + list(pg)) if d == 0 else (),
                        inc="P" if d == DT - 1 else None)
                S.emit("scalar", lambda s, n2=n, h3=hh, p=pi, c2=uo * 4 + hh: s.activation(
                    pre[n2][:, h3, 3:3 + TC], ps_f[p][:, :], AF.Identity,
                    bias=sab[:, c2:c2 + 1]),
                    waits=[("P", S.val("P")), ("V", halo_done)], inc="A")
                psum_rel(pi, "A")
                evac_a[(n, hh)] = S.val("A")
        for n in "qkv":
            for hh in range(HL):
                pi, pg = psum_acq()
                for j in range(CKW):
                    S.emit("tensor", lambda t, n2=n, h3=hh, j2=j, p=pi: t.matmul(
                        ps_f[p][:, :], cd_sb[n2][:, h3, j2, :], pre[n2][:, h3, j2:j2 + TC],
                        start=(j2 == 0), stop=(j2 == CKW - 1)),
                        waits=([("A", evac_a[(n, hh)])] + list(pg)) if j == 0 else (),
                        inc="P" if j == CKW - 1 else None)
                S.emit("scalar", lambda s, n2=n, h3=hh, p=pi: s.activation(
                    sil[n2][:, h3, :], ps_f[p][:, :], AF.Silu),
                    waits=[("P", S.val("P"))], inc="A")
                psum_rel(pi, "A")
        sil_done = S.val("A")
        for n in ("q", "k"):
            S.emit("scalar", lambda s, n2=n: s.activation(
                v2(sqs, HL, TC), sil[n2][:, :, :], AF.Square),
                waits=[("A", sil_done), ("P", S.val("P"))], inc="A")
            sqd = S.val("A")
            qb = (0, 1, 2, 3)
            pgq = psum_acq_set(qb)
            for hh in range(HL):
                S.emit("tensor", lambda t, h3=hh: t.matmul(
                    ps_f[h3][:, :], ones_bf, sqs[:, h3 * TC:(h3 + 1) * TC],
                    start=True, stop=True),
                    waits=([("A", sqd)] + list(pgq)) if hh == 0 else (),
                    inc="P" if hh == HL - 1 else None)
            S.emit("vector", lambda v: v.tensor_single_scalar(
                r_big, ps_big[:, 0:2048], 1e-12, AL.max),
                waits=[("P", S.val("P"))], inc="V")
            psum_rel_set(qb, "V")
            S.emit("vector", lambda v: v.reciprocal(r_big, r_big), inc="V")
            S.emit("scalar", lambda s: s.activation(r_big, r_big, AF.Sqrt),
                   waits=[("V", S.val("V"))], inc="A")
            S.emit("vector", lambda v, n2=n: v.tensor_tensor(
                sil[n2][:, :, :], sil[n2][:, :, :], v2(r_big, HL, TC), AL.mult),
                waits=[("A", S.val("A"))], inc="V")
        l2_done = S.val("V")
        if DEBUG:
            for hh in range(HL):
                S.dma("sync", dbg_qn[hh * 128:(hh + 1) * 128, t0:t0 + TC], qn_v[:, hh, :],
                      "Sd", waits=[("V", S.val("V"))])
                S.dma("sync", dbg_kn[hh * 128:(hh + 1) * 128, t0:t0 + TC], kn_v[:, hh, :],
                      "Sd", waits=[("V", S.val("V"))])

        def shared_proj(wtile, dst_sb, bcol):
            pi, pg = psum_acq()
            for d in range(DT):
                S.emit("tensor", lambda t, d2=d, p=pi, w=wtile, h2=h_v: t.matmul(
                    ps_f[p][:, :], w[:, d2, :], h2[:, d2, :],
                    start=(d2 == 0), stop=(d2 == DT - 1)),
                    waits=([hw] + list(pg)) if d == 0 else (),
                    inc="P" if d == DT - 1 else None)
            S.emit("scalar", lambda s, p=pi, o=dst_sb, c2=bcol: s.activation(
                o, ps_f[p][:, :], AF.Identity, bias=sab[:, c2:c2 + 1]),
                   waits=[("P", S.val("P"))], inc="A")
            psum_rel(pi, "A")
            return S.val("A")

        zd = shared_proj(wad_v, z_sb, 12)
        zgd = shared_proj(wgd_v, zg_sb, 13)
        pi, pg = psum_acq()
        for d in range(DT):
            S.emit("tensor", lambda t, d2=d, p=pi, h2=h_v: t.matmul(
                ps_f[p][0:HL, :], wb_v[:, d2, :], h2[:, d2, :],
                start=(d2 == 0), stop=(d2 == DT - 1)),
                waits=([hw] + list(pg)) if d == 0 else (),
                inc="P" if d == DT - 1 else None)
        h_free_hist[tc] = S.val("P")
        if tc + 1 < NTC:
            emit_h(tc + 1, h_free_hist[tc])
        S.emit("vector", lambda v, p=pi: v.tensor_scalar(
            bhr[0:HL, :], ps_f[p][0:HL, :], sab[0:HL, 14:15], -10.0, AL.add, AL.max),
            waits=[("P", S.val("P")), ("A", S.val("A"))], inc="V")
        psum_rel(pi, "V")
        S.emit("vector", lambda v: v.tensor_single_scalar(
            bhr[0:HL, :], bhr[0:HL, :], 10.0, AL.min), inc="V")
        S.emit("scalar", lambda s: s.activation(bhr[0:HL, :], bhr[0:HL, :], AF.Sigmoid),
               waits=[("V", S.val("V"))], inc="A")
        bhd = S.val("A")
        bi2, bg = psb_acq()
        for c8 in range(NCH_C):
            S.emit("tensor", lambda t, s2=c8, p=bi2: t.matmul(
                ps_b[p][0:64, s2 * HL:(s2 + 1) * HL], bhr[0:HL, s2 * 64:(s2 + 1) * 64],
                ident[0:HL, 0:HL], is_transpose=True,
                start=(s2 == 0), stop=(s2 == NCH_C - 1)),
                waits=([("A", bhd)] + list(bg)) if c8 == 0 else (),
                inc="P" if c8 == NCH_C - 1 else None)
        S.emit("vector", lambda v, p=bi2: v.tensor_copy(
            bhT_v[0:64, :, :], ps_b[p][0:64, 0:NCH_C * HL]),
            waits=[("P", S.val("P"))], inc="V")
        psb_rel(bi2, "V")
        qb = (0, 1, 2, 3)
        pgq = psum_acq_set(qb)
        for sub in range(4):
            S.emit("tensor", lambda t, s2=sub: t.matmul(
                ps_f[s2][:, :], zg_sb[:, s2 * 128:(s2 + 1) * 128], wgu_sb,
                start=True, stop=True),
                waits=([("A", zgd)] + list(pgq)) if sub == 0 else (),
                inc="P" if sub == 3 else None)
        S.emit("scalar", lambda s: s.activation(
            gate_v[:, :, :], ps_big[:, 0:2048], AF.Sigmoid),
            waits=[("P", S.val("P"))], inc="A")
        psum_rel_set(qb, "A")
        S.emit("vector", lambda v: v.tensor_tensor(
            gate_v[:, :, :], gate_v[:, :, :],
            v2(rms_sb, 1, FQ).to_broadcast([128, 4, FQ]), AL.mult),
            waits=[("A", S.val("A")), ("So", S.val("So"))], inc="V")

        qb = (0, 1, 2, 3)
        pgq = psum_acq_set(qb)
        for hh in range(HL):
            S.emit("tensor", lambda t, h3=hh: t.matmul(
                ps_f[h3][:, :], wau_sb[:, h3 * 128:(h3 + 1) * 128], z_sb,
                start=True, stop=True),
                waits=([("A", zd)] + list(pgq)) if hh == 0 else (),
                inc="P" if hh == HL - 1 else None)
        S.emit("vector", lambda v: v.tensor_scalar(
            sp_b.rearrange("p a b -> p (a b)"), ps_big[:, 0:2048], -10.0, 10.0,
            AL.max, AL.min),
            waits=[("P", S.val("P"))], inc="V")
        psum_rel_set(qb, "V")
        S.emit("scalar", lambda s: s.activation(sp_b[:, :, :], sp_b[:, :, :], AF.Exp),
               waits=[("V", S.val("V"))], inc="A")
        S.emit("scalar", lambda s: s.activation(sp_b[:, :, :], sp_b[:, :, :], AF.Ln, bias=1.0),
               inc="A")
        spd = S.val("A")
        for hh in range(HL):
            for c in range(NCH_C):
                S.emit("vector", lambda v, h3=hh, c2=c: v.tensor_tensor_scan(
                    cum_b[:, h3, c2 * CH:(c2 + 1) * CH], sp_b[:, h3, c2 * CH:(c2 + 1) * CH],
                    sp_b[:, h3, c2 * CH:(c2 + 1) * CH], 0.0, AL.add, AL.bypass),
                    waits=[("A", spd)] if (hh == 0 and c == 0) else (), inc="V")
        cumd = S.val("V")
        S.emit("scalar", lambda s: s.activation(eb_b[:, :, :], cum_b[:, :, :], AF.Exp, scale=-1.0),
               waits=[("V", cumd)], inc="A")
        S.emit("scalar", lambda s: s.activation(ebm_b[:, :, :], cum_b[:, :, :], AF.Exp), inc="A")
        S.emit("scalar", lambda s: s.activation(
            ebl_v[:, :, :], cum_b.rearrange("p a (c t) -> p a c t", t=CH)[:, :, :, CH - 1],
            AF.Exp, scale=-1.0), inc="A")
        ebld = S.val("A")
        S.emit("vector", lambda v: v.tensor_tensor(
            kt_v[:, :, :], kn_v[:, :, :], eb_b[:, :, :], AL.mult),
            waits=[("A", ebld), ("P", S.val("P"))], inc="V")
        S.emit("vector", lambda v: v.tensor_tensor(
            kb_v[:, :, :], kn_v[:, :, :], ebm_b[:, :, :], AL.mult), inc="V")
        S.emit("vector", lambda v: v.tensor_tensor(
            ebd_b.rearrange("p a (c t) -> p a c t", t=CH),
            ebm_b.rearrange("p a (c t) -> p a c t", t=CH),
            ebl_v.rearrange("p a (c o) -> p a c o", o=1).to_broadcast([128, HL, NCH_C, CH]),
            AL.mult), inc="V")
        S.emit("vector", lambda v: v.tensor_tensor(
            kdf_v[:, :, :], kn_v[:, :, :], ebd_b[:, :, :], AL.mult), inc="V")
        S.emit("vector", lambda v: v.tensor_tensor(
            qt_v[:, :, :], qn_v[:, :, :], eb_b[:, :, :], AL.mult), inc="V")
        scald = S.val("V")
        for h3c in range(HL):
            for src_t, dst_t, wsem in ((kdf_v, kdt_v3, ("V", scald)),
                                       (sil["v"], vt_v3, ("A", sil_done))):
                for cg in range(2):
                    bi2, bg = psb_acq()
                    for cc4 in range(4):
                        c8 = cg * 4 + cc4
                        S.emit("tensor", lambda t, h4=h3c, b2=c8, k4=cc4, p=bi2, st=src_t: t.matmul(
                            ps_b[p][0:64, k4 * 128:(k4 + 1) * 128],
                            st[:, h4, b2 * 64:(b2 + 1) * 64], ident, is_transpose=True,
                            start=(k4 == 0), stop=(k4 == 3)),
                            waits=([wsem] + list(bg)) if cc4 == 0 else (),
                            inc="P" if cc4 == 3 else None)
                    S.emit("scalar", lambda s, h4=h3c, g2=cg, p=bi2, dt2=dst_t: s.copy(
                        dt2[0:64, h4, g2 * 4:(g2 + 1) * 4, :], ps_b[p][0:64, :]),
                        waits=[("P", S.val("P"))], inc="A")
                    psb_rel(bi2, "A")
        kdvt_A = S.val("A")
        ebl_done = ebld
        ttv = [0, 0, 0, 0]
        seq_sv = [S.val("V")]

        def precompute_pair(p):
            st = {}
            thunks = []

            def stage_A():
                pga = psum_acq_set((0,))
                for u in range(8):
                    par, hh = u // 4, u % 4
                    c = 2 * p + par
                    csl = slice(c * CH, (c + 1) * CH)
                    S.emit("tensor", lambda t, h3=hh, cs=csl, u2=u: t.matmul(
                        ps_f[0][0:64, u2 * 64:(u2 + 1) * 64], kt_v[:, h3, cs], kb_v[:, h3, cs],
                        start=(u2 == 0), stop=(u2 == 7)),
                        waits=([("V", scald)] + list(pga)) if u == 0 else (),
                        inc="P" if u == 7 else None)
                st["amm"] = S.val("P")
            thunks.append(stage_A)

            def stage_N():
                S.emit("vector", lambda v: v.tensor_tensor(
                    Npr[p % 2][0:64, :].rearrange("q (a b) -> q a b", b=64),
                    ps_f[0][0:64, :].rearrange("q (a b) -> q a b", b=64),
                    bhT_v[0:64, 2 * p:2 * p + 2, :].rearrange("q a b -> q (a b)")
                    .rearrange("q (a o) -> q a o", o=1).to_broadcast([64, 8, 64]),
                    AL.mult), waits=[("P", st["amm"])], inc="V")
                S.emit("vector", lambda v: v.tensor_tensor(
                    Npr[p % 2][0:64, :], Npr[p % 2][0:64, :], nmsk[0:64, :], AL.mult), inc="V")
                psum_rel_set((0,), "V")
                st["n"] = S.val("V")
            thunks.append(stage_N)

            def stage_P0():
                bi2, bg = psb_acq()
                for u in range(8):
                    S.emit("tensor", lambda t, u2=u, pb=bi2: t.matmul(
                        ps_b[pb][0:64, u2 * 64:(u2 + 1) * 64],
                        Npr[p % 2][0:64, u2 * 64:(u2 + 1) * 64],
                        ident[0:64, 0:64], is_transpose=True,
                        start=(u2 == 0), stop=(u2 == 7)),
                        waits=([("V", st["n"])] + list(bg)) if u == 0 else (),
                        inc="P" if u == 7 else None)
                S.emit("scalar", lambda s2, pb=bi2: s2.copy(
                    Ppr[p % 2][0:64, :], ps_b[pb][0:64, :]),
                    waits=[("P", S.val("P"))], inc="A")
                psb_rel(bi2, "A")
                S.emit("vector", lambda v: v.tensor_copy(Qpr[p % 2][0:64, :], Npr[p % 2][0:64, :]),
                       inc="V")
                S.emit("vector", lambda v: v.tensor_tensor(
                    Rpr[p % 2][0:64, :], Ppr[p % 2][0:64, :], erep[0:64, :], AL.add),
                    waits=[("A", S.val("A"))], inc="V")
                st["ev"] = (S.val("V"), S.val("A"))
            thunks.append(stage_P0)

            def mk_lvl_mm(lev):
                def stage():
                    pga = psum_acq_set((0, 1, 2))
                    ev, ea = st["ev"]
                    for u in range(8):
                        usl = slice(u * 64, (u + 1) * 64)
                        S.emit("tensor", lambda t, us=usl: t.matmul(
                            ps_f[0][0:64, us], Ppr[p % 2][0:64, us], Qpr[p % 2][0:64, us],
                            start=(us.start == 0), stop=(us.start == 448)),
                            waits=([("V", ev), ("A", ea)] + list(pga)) if u == 0 else ())
                    for u in range(8):
                        usl = slice(u * 64, (u + 1) * 64)
                        S.emit("tensor", lambda t, us=usl: t.matmul(
                            ps_f[1][0:64, us], Qpr[p % 2][0:64, us], Ppr[p % 2][0:64, us],
                            start=(us.start == 0), stop=(us.start == 448)))
                    for u in range(8):
                        usl = slice(u * 64, (u + 1) * 64)
                        S.emit("tensor", lambda t, us=usl: t.matmul(
                            ps_f[2][0:64, us], Qpr[p % 2][0:64, us], Rpr[p % 2][0:64, us],
                            start=(us.start == 0), stop=(us.start == 448)),
                            inc="P" if u == 7 else None)
                    st["mm"] = S.val("P")
                return stage

            def mk_lvl_ev(lev):
                def stage():
                    S.emit("scalar", lambda s2: s2.copy(Qpr[p % 2][0:64, :], ps_f[0][0:64, :]),
                           waits=[("P", st["mm"])], inc="A")
                    S.emit("scalar", lambda s2: s2.copy(Ppr[p % 2][0:64, :], ps_f[1][0:64, :]),
                           inc="A")
                    S.emit("vector", lambda v: v.tensor_tensor(
                        Rpr[p % 2][0:64, :], ps_f[2][0:64, :], Rpr[p % 2][0:64, :], AL.add),
                        waits=[("P", st["mm"])], inc="V")
                    psum_rel_set((0, 1), "A")
                    psum_rel_set((2,), "V")
                    st["ev"] = (S.val("V"), S.val("A"))
                return stage

            for lev in range(5):
                thunks.append(mk_lvl_mm(lev))
                thunks.append(mk_lvl_ev(lev))

            def stage_TT():
                ev, ea = st["ev"]
                S.emit("vector", lambda v: v.tensor_tensor(
                    TT3[0:64, 2 * p:2 * p + 2, :, :],
                    Rpr[p % 2][0:64, :].rearrange("q (a b c) -> q a b c", a=2, c=64),
                    bhT_v[0:64, 2 * p:2 * p + 2, :].rearrange(
                        "q a (b o) -> q a b o", o=1).to_broadcast([64, 2, HL, 64]),
                    AL.mult), waits=[("V", ev), ("A", ea)], inc="V")
            thunks.append(stage_TT)

            def stage_attnT():
                pga = psum_acq_set((0,))
                for u in range(8):
                    par, hh = u // 4, u % 4
                    c = 2 * p + par
                    csl = slice(c * CH, (c + 1) * CH)
                    S.emit("tensor", lambda t, h3=hh, cs=csl, u2=u: t.matmul(
                        ps_f[0][0:64, u2 * 64:(u2 + 1) * 64], kb_v[:, h3, cs], qt_v[:, h3, cs],
                        start=(u2 == 0), stop=(u2 == 7)),
                        waits=(list(pga) + [("V", scald)]) if u == 0 else (),
                        inc="P" if u == 7 else None)
                amm = S.val("P")
                S.emit("vector", lambda v: v.tensor_tensor(
                    AT3[0:64, 2 * p:2 * p + 2, :, :],
                    ps_f[0][0:64, :].rearrange("q (a b c) -> q a b c", a=2, c=64),
                    umsk[0:64, :].rearrange("q (a b c) -> q a b c", a=2, c=64),
                    AL.mult), waits=[("P", amm)], inc="V")
                psum_rel_set((0,), "V")
                ttv[p] = S.val("V")
            thunks.append(stage_attnT)
            return thunks

        seq_rot = {"i": 0}

        def seq_acq():
            i = 3 + seq_rot["i"] % 3
            seq_rot["i"] += 1
            return i, ps_state["guard"][i]

        def scan_step_pieces(c):
            csl = slice(c * CH, (c + 1) * CH)
            ro = (c % 2) * 64
            rsl = slice(ro, ro + 64)
            pieces = []
            stx = {}

            def piece1():
              pia, pga = seq_acq()
              stx["pia"] = pia
              for hh in range(HL):
                S.emit("tensor", lambda t, h3=hh, cs=csl, p=pia: t.matmul(
                    ps_f[p][0:64, h3 * 128:(h3 + 1) * 128], kt_v[:, h3, cs], Sb_v[:, h3, :],
                    start=(h3 == 0), stop=(h3 == HL - 1)),
                    waits=([("V", max(seq_sv[0], scald)), ("A", kdvt_A)] + list(pga))
                    if hh == 0 else (),
                    inc="P" if hh == HL - 1 else None)
              mm1d = S.val("P")
              S.emit("vector", lambda v, p=pia, c2=c: v.tensor_tensor(
                rhsu[0:64, :], vt_v3[0:64, :, c2, :], ps_f[p][0:64, :], AL.subtract),
                waits=[("P", mm1d), ("A", kdvt_A)], inc="V")
              stx["rhv"] = S.val("V")
              psum_rel(pia, "V")
            pieces.append(piece1)

            def piece2():
              rhv = stx["rhv"]
              pib, pgb = seq_acq()
              for hh in range(HL):
                S.emit("tensor", lambda t, h3=hh, c2=c, p=pib: t.matmul(
                    ps_f[p][0:64, h3 * 128:(h3 + 1) * 128], TT3[0:64, c2, h3, :],
                    rhsu[0:64, h3 * 128:(h3 + 1) * 128],
                    start=(h3 == 0), stop=(h3 == HL - 1)),
                    waits=([("V", max(rhv, ttv[c // 2]))] + list(pgb)) if hh == 0 else (),
                    inc="P" if hh == HL - 1 else None)
              S.emit("scalar", lambda s2, p=pib: s2.copy(u_sb[0:64, :], ps_f[p][0:64, :]),
                     waits=[("P", S.val("P"))], inc="A")
              psum_rel(pib, "A")
              stx["ud"] = S.val("A")
            pieces.append(piece2)

            def piece3():
              ud = stx["ud"]
              pio, pgo = seq_acq()
              for hh in range(HL):
                  S.emit("tensor", lambda t, h3=hh, cs=csl, p=pio, r0=ro: t.matmul(
                      ps_f[p][r0:r0 + 64, h3 * 128:(h3 + 1) * 128], qt_v[:, h3, cs], Sb_v[:, h3, :],
                      start=(h3 == 0), stop=False),
                      waits=([("A", ud)] + list(pgo)) if hh == 0 else ())
              for hh in range(HL):
                  S.emit("tensor", lambda t, h3=hh, c2=c, p=pio, r0=ro: t.matmul(
                      ps_f[p][r0:r0 + 64, h3 * 128:(h3 + 1) * 128], AT3[0:64, c2, h3, :],
                      u_sb[0:64, h3 * 128:(h3 + 1) * 128],
                      start=False, stop=(h3 == HL - 1)),
                      inc="P" if hh == HL - 1 else None)
              od = S.val("P")
              pid, pgd = seq_acq()
              for hh in range(HL):
                  S.emit("tensor", lambda t, h3=hh, c2=c, p=pid: t.matmul(
                      ps_f[p][:, h3 * 128:(h3 + 1) * 128], kdt_v3[0:64, h3, c2, :],
                      u_sb[0:64, h3 * 128:(h3 + 1) * 128],
                      start=(h3 == 0), stop=(h3 == HL - 1)),
                      waits=(list(pgd)) if hh == 0 else (),
                      inc="P" if hh == HL - 1 else None)
              dsd = S.val("P")
              for hh in range(HL):
                  S.emit("vector", lambda v, h3=hh, c2=c, p=pid: v.scalar_tensor_tensor(
                      Sf_v[:, h3, :], Sf_v[:, h3, :], ebl_v[:, h3, c2:c2 + 1],
                      ps_f[p][:, h3 * 128:(h3 + 1) * 128], AL.mult, AL.add),
                      waits=[("P", dsd), ("A", ebl_done)] if hh == 0 else (), inc="V")
              S.emit("vector", lambda v: v.tensor_copy(Sb_v[:, :, :], Sf_v[:, :, :]), inc="V")
              seq_sv[0] = S.val("V")
              psum_rel(pid, "V")
              for hh in range(HL):
                  S.emit("scalar", lambda s2, h3=hh, p=pio, rs=rsl: s2.activation(
                      kdf_v.rearrange("p a b -> p (a b)")[rs, h3 * 128:(h3 + 1) * 128],
                      ps_f[p][rs, h3 * 128:(h3 + 1) * 128], AF.Square,
                      accum_out=ms_sb[rs, h3:h3 + 1]),
                      waits=[("P", od)] if hh == 0 else (), inc="A")
              S.emit("vector", lambda v, rs=rsl: v.tensor_scalar(
                  ms_sb[rs, :], ms_sb[rs, :], 1.0 / DV, 1e-6, AL.mult, AL.add),
                  waits=[("A", S.val("A"))], inc="V")
              S.emit("vector", lambda v, rs=rsl: v.reciprocal(rr_sb[rs, :], ms_sb[rs, :]), inc="V")
              S.emit("scalar", lambda s2, rs=rsl: s2.activation(rr_sb[rs, :], rr_sb[rs, :], AF.Sqrt),
                     waits=[("V", S.val("V"))], inc="A")
              rd = S.val("A")
              for hh in range(HL):
                  ogsl2 = og_sb[rsl, hh * 128:(hh + 1) * 128]
                  gsl = gate_v[rsl, c // 2, hh * 128:(hh + 1) * 128]
                  S.emit("vector", lambda v, h3=hh, p=pio, o=ogsl2, g2=gsl, rs=rsl: v.scalar_tensor_tensor(
                      o, ps_f[p][rs, h3 * 128:(h3 + 1) * 128], rr_sb[rs, h3:h3 + 1],
                      g2, AL.mult, AL.mult),
                      waits=([("A", rd)] + ([("So", S.val("So"))] if c % 2 == 0 else []))
                      if hh == 0 else (), inc="V")
              psum_rel(pio, "V")
              if c % 2 == 1:
                  S.dma("sync", og_d[t0 + (c - 1) * CH:t0 + (c + 1) * CH, :], og_sb,
                        "So", waits=[("V", S.val("V"))])
                  if c == 7 and tc == 5:
                      so_half1[0] = S.val("So")

            pieces.append(piece3)
            return pieces

        th0 = precompute_pair(0)
        for th in th0:
            th()
        for p in range(1, 4):
            ths = precompute_pair(p)
            pcs = scan_step_pieces(2 * (p - 1)) + scan_step_pieces(2 * (p - 1) + 1)
            slots = {1: 0, 3: 1, 5: 2, 7: 3, 9: 4, 11: 5}
            for i, th in enumerate(ths):
                th()
                if i in slots:
                    pcs[slots[i]]()
        for c in (6, 7):
            for pc in scan_step_pieces(c):
                pc()
    attn_st = S.val("So")
    for it in range(NIT):
        S.dma("gpsimd", wdb_d[it, :, :], wd_i[it, :, :], "Wd")
    for it in range(NIT):
        S.dma("gpsimd", wgb_d[it, :, :], wg_i[it, :, :], "Wd")
        S.dma("gpsimd", wub_d[it, :, :], wu_i[it, :, :], "Wd")
    S.emit("gpsimd", lambda g: g.collective_compute(
        "AllGather", AL.bypass, replica_groups=[[0, 1, 2, 3], [4, 5, 6, 7]],
        ins=[og_d[0:3 * T // 4, :]], outs=[ogg_d[0]]),
        waits=[("So", so_half1[0])], inc="C")
    cc_og1 = S.val("C")
    AB.pop(); AFP.pop()
    S.barrier()
    ps_stack.close()
    ps_f.append(nc.ctx.enter_context(nc.psum_tensor("psf6", [128, 512], F32)))
    ps_f.append(nc.ctx.enter_context(nc.psum_tensor("psf7", [128, 512], F32)))
    ps_state["n"] = 8

    AB.push(); AFP.push()
    wo_sb = [AB.alloc(D) for _ in range(16)]
    for ft in range(16):
        S.dma("gpsimd", wo_sb[ft], wo_i[ft * 128:(ft + 1) * 128, :], "Gwo")
    wo_ld = S.val("Gwo")
    gbuf = [AB.alloc(FQ) for _ in range(2)]
    gguard = [0, 0]

    def emit_gathers(kr, ccv):
        for k in kr:
            for r in range(4):
                j = k * 4 + r
                gb2 = gbuf[j % 2]
                gg = "Gg%d" % (j % 2)
                sg = "Sg%d" % (j % 2)
                S.emit("gpsimd", lambda g, o=gb2, j2=j, h=k // 6: g.indirect_dma_start(
                    o, None, ogg_d[h],
                    IndirectOffsetOnAxis(ogidx_s[:, j2:j2 + 1], 0)),
                    waits=[("C", ccv), (("Sg%d" % (j % 2)), gguard[j % 2]),
                           ("Lc", S.val("Lc"))],
                    inc=gg, inc_n=16)
                S.dma("sync", ogsl_d[(r * 8 + k) * 128:(r * 8 + k) * 128 + 128, :], gb2, sg,
                      waits=[(gg, S.val(gg))])
                gguard[j % 2] = S.val(sg)

    og_fm = [AB.alloc(TOK) for _ in range(16)]
    lo_half = [0, 0]

    def emit_ogfm(half):
        lsem = "Lo%d" % half
        c0 = 0 if half == 0 else 768
        c1 = 768 if half == 0 else 1024
        for r in range(4):
            for ftt in range(4):
                S.dma("sync", og_fm[r * 4 + ftt][:, c0:c1],
                      ogsl_d[r * 1024 + c0:r * 1024 + c1,
                             ftt * 128:(ftt + 1) * 128],
                      lsem, waits=[("Sg0", S.val("Sg0")), ("Sg1", S.val("Sg1"))]
                      if (r == 0 and ftt == 0) else (),
                      transpose=True)
        lo_half[half] = (lsem, S.val(lsem))

    emit_gathers(range(6), cc_og1)
    emit_ogfm(0)
    S.emit("gpsimd", lambda g: g.collective_compute(
        "AllGather", AL.bypass, replica_groups=[[0, 1, 2, 3], [4, 5, 6, 7]],
        ins=[og_d[3 * T // 4:T, :]], outs=[ogg_d[1]]),
        waits=[("So", attn_st)], inc="C")
    cc_og2 = S.val("C")
    emit_gathers(range(6, 8), cc_og2)
    emit_ogfm(1)

    xo_t = AFP.alloc(D)
    x1_t = AFP.alloc(D)
    r_scr = AFP.alloc(512)
    st2 = AFP.alloc(8)
    xh2 = AB.alloc(D)
    sq2 = AB.alloc(D)
    mu = st2[:, 0:1]; ssq = st2[:, 1:2]; var = st2[:, 2:3]; rstd = st2[:, 3:4]
    for tt in range(8):
        S.dma("sync", xo_t, x_own[tt * 128:(tt + 1) * 128, :], "Lxo",
              waits=[("V", S.val("V")), ("S1", S.val("S1"))])
        xol = S.val("Lxo")
        for dc in range(4):
            pi, pg = psum_acq()
            for ft in range(16):
                S.emit("tensor", lambda t, f=ft, t2=tt, d2=dc, p=pi: t.matmul(
                    ps_f[p][:, :], og_fm[f][:, t2 * 128:(t2 + 1) * 128],
                    wo_sb[f][:, d2 * 512:(d2 + 1) * 512],
                    start=(f == 0), stop=(f == 15)),
                    waits=([lo_half[0 if tt < 6 else 1], ("Gwo", wo_ld)] + list(pg))
                    if ft == 0 else (),
                    inc="P" if ft == 15 else None)
            S.emit("vector", lambda v, p=pi, d2=dc: v.tensor_tensor(
                r_scr, ps_f[p][:, :], ga_bc[:, d2 * 512:(d2 + 1) * 512], AL.mult),
                waits=[("P", S.val("P")), ("Lb", S.val("Lb"))], inc="V")
            psum_rel(pi, "V")
            S.emit("vector", lambda v, d2=dc: v.tensor_tensor(
                x1_t[:, d2 * 512:(d2 + 1) * 512], r_scr,
                xo_t[:, d2 * 512:(d2 + 1) * 512], AL.add),
                waits=[("Lxo", xol)], inc="V")
        S.dma("sync", x1_d[tt * 128:(tt + 1) * 128, :], x1_t, "S1",
              waits=[("V", S.val("V"))])
        S.emit("vector", lambda v, m=mu: v.reduce_sum(m, x1_t, AX.X), inc="V")
        S.emit("scalar", lambda s, q=ssq: s.activation(sq2, x1_t, AF.Square, accum_out=q),
               waits=[("V", S.val("V"))], inc="A")
        S.emit("vector", lambda v, m=mu: v.tensor_single_scalar(m, m, 1.0 / D, AL.mult),
               inc="V")
        S.emit("vector", lambda v, m=mu, va=var: v.tensor_tensor(va, m, m, AL.mult),
               inc="V")
        S.emit("vector", lambda v, q=ssq, va=var: v.scalar_tensor_tensor(
            va, q, 1.0 / D, va, AL.mult, AL.subtract),
               waits=[("A", S.val("A"))], inc="V")
        S.emit("vector", lambda v, va=var: v.tensor_single_scalar(va, va, 1e-5, AL.add),
               inc="V")
        S.emit("vector", lambda v, va=var, r=rstd: v.reciprocal(r, va), inc="V")
        S.emit("scalar", lambda s, r=rstd: s.activation(r, r, AF.Sqrt),
               waits=[("V", S.val("V"))], inc="A")
        S.emit("vector", lambda v, m=mu, r=rstd: v.tensor_scalar(
            xh2, x1_t, m, r, AL.subtract, AL.mult),
            waits=[("A", S.val("A")), ("S2", S.val("S2"))], inc="V")
        S.dma("sync", xhat_d[tt * 128:(tt + 1) * 128, :], xh2, "S2",
              waits=[("V", S.val("V"))])
    x1_done = S.val("S1")
    ln2_done = S.val("S2")
    AB.pop(); AFP.pop()
    S.barrier()

    AB.push(); AFP.push()
    h2_v = v2(AB.alloc(DT * TOK), DT, TOK)
    for ft in range(DT):
        S.dma("sync", h2_v[:, ft, :], xhat_d[0:TOK, ft * 128:(ft + 1) * 128],
              "Lh2", waits=[("S2", ln2_done)] if ft == 0 else (), transpose=True)
    ldh2 = S.val("Lh2")
    for ft in range(DT):
        S.emit("vector", lambda v, f=ft: v.tensor_scalar(
            h2_v[:, f, :], h2_v[:, f, :], cb1_c[:, f:f + 1], sb_c[:, f:f + 1],
            AL.mult, AL.add), waits=[("Lh2", ldh2)] if ft == 0 else (), inc="V")
    h2_done = S.val("V")
    if DEBUG:
        for ft in range(DT):
            S.dma("sync", dbg_h2[ft * 128:(ft + 1) * 128, :], h2_v[:, ft, :],
                  "Sd", waits=[("V", h2_done)])

    wgb = [v2(AB.alloc(DT * 128), DT, 128) for _ in range(2)]
    wub = [v2(AB.alloc(DT * 128), DT, 128) for _ in range(2)]
    wguard = [0, 0]
    hm_q = [AB.alloc(TOK) for _ in range(4)]
    hs_q = AB.alloc(TOK)
    m_q = [AFP.alloc(TOK) for _ in range(4)]
    cc6 = [AB.alloc(TOK) for _ in range(6)]
    um_ = AB.alloc(TOK)
    msk = AB.alloc(TOK)
    silu_t = AB.alloc(512)

    for j in range(16):
        for q4 in range(4):
            it = q4 * 16 + j
            bi = it % 2
            gf = "Gf%d" % bi
            uf = "Uf%d" % bi
            S.dma("gpsimd", wgb[bi][:, :, :],
                  wgb_d[it, :, :].rearrange("p (a b) -> p a b", a=DT, b=128),
                  gf, waits=[("P", wguard[bi])])
            S.dma("gpsimd", wub[bi][:, :, :],
                  wub_d[it, :, :].rearrange("p (a b) -> p a b", a=DT, b=128),
                  uf, waits=[("P", wguard[bi])])
            wldv = S.val(gf)
            uldv = S.val(uf)
            for th in range(2):
                pi, pg = psum_acq()
                for d in range(DT):
                    S.emit("tensor", lambda t, d2=d, b2=bi, t2=th, p=pi: t.matmul(
                        ps_f[p][:, :], wgb[b2][:, d2, :], h2_v[:, d2, t2 * 512:(t2 + 1) * 512],
                        start=(d2 == 0), stop=(d2 == DT - 1)),
                        waits=([(gf, wldv), ("V", h2_done)] + list(pg)) if d == 0 else (),
                        inc="P" if d == DT - 1 else None)
                pu, pgu = psum_acq()
                for d in range(DT):
                    S.emit("tensor", lambda t, d2=d, b2=bi, t2=th, p=pu: t.matmul(
                        ps_f[p][:, :], wub[b2][:, d2, :], h2_v[:, d2, t2 * 512:(t2 + 1) * 512],
                        start=(d2 == 0), stop=(d2 == DT - 1)),
                        waits=([(uf, uldv)] + list(pgu)) if d == 0 else (),
                        inc="P" if d == DT - 1 else None)
                gud = S.val("P")
                S.emit("scalar", lambda s, p=pi: s.activation(silu_t, ps_f[p][:, :], AF.Silu),
                       waits=[("P", gud), ("V", S.val("V"))], inc="A")
                S.emit("vector", lambda v, q=q4, t2=th, p=pu: v.tensor_tensor(
                    hm_q[q][:, t2 * 512:(t2 + 1) * 512], silu_t, ps_f[p][:, :], AL.mult),
                    waits=[("A", S.val("A"))], inc="V")
                psum_rel(pi, "A"); psum_rel(pu, "V")
            wguard[bi] = S.val("P")
        hmd = S.val("V")
        for q4 in range(4):
            S.emit("scalar", lambda s, q=q4: s.activation(m_q[q], hm_q[q], AF.Square),
                   waits=[("V", hmd)] if q4 == 0 else (), inc="A")
        sqd = S.val("A")
        pairs = [(0, 1), (0, 2), (0, 3), (1, 2), (1, 3), (2, 3)]
        for i6, (a, b2) in enumerate(pairs):
            S.emit("vector", lambda v, i=i6, a2=a, b3=b2: v.tensor_tensor(
                cc6[i], m_q[a2], m_q[b3], AL.is_gt),
                waits=[("A", sqd)] if i6 == 0 else (), inc="V")
        combos = [
            (0, 0, 1, AL.add, 2, AL.add, 2.0, AL.is_ge),
            (1, 3, 4, AL.add, 0, AL.subtract, 1.0, AL.is_ge),
            (2, 5, 1, AL.subtract, 3, AL.subtract, 0.0, AL.is_ge),
            (3, 2, 4, AL.add, 5, AL.add, 1.0, AL.is_le),
        ]
        for q4, i0, i1, op1, i2, op2, thr, opm in combos:
            S.emit("vector", lambda v, a=i0, b3=i1, o=op1: v.tensor_tensor(
                um_, cc6[a], cc6[b3], o),
                waits=[("Sh", S.val("Sh"))] if q4 == 0 else (), inc="V")
            S.emit("vector", lambda v, a=i2, o=op2: v.tensor_tensor(
                um_, um_, cc6[a], o), inc="V")
            S.emit("vector", lambda v, t2=thr, o=opm: v.tensor_single_scalar(
                msk, um_, t2, o), inc="V")
            S.emit("vector", lambda v, q=q4: v.tensor_tensor(
                hs_q, hm_q[q], msk, AL.mult),
                waits=[("Sh", S.val("Sh"))], inc="V")
            it = q4 * 16 + j
            S.dma("sync", hs_d[it, :, :], hs_q, "Sh", waits=[("V", S.val("V"))])
    hs_done = S.val("Sh")
    wd_done = S.val("Wd")
    AB.pop(); AFP.pop()
    S.barrier()

    AB.push(); AFP.push()
    hsr = v2(AB.alloc(NIT * TOK), NIT, TOK)
    wdb4 = [v2(AB.alloc(4 * 512), 4, 512) for _ in range(2)]
    dguard = [0, 0]
    o_t = AFP.alloc(512)
    x1e = AFP.alloc(512)
    ghv = []
    for g8 in range(8):
        gh = "Gh%d" % g8
        S.dma("gpsimd", hsr[:, g8 * 8:(g8 + 1) * 8, :],
              hs_d[g8 * 8:(g8 + 1) * 8, :, :].rearrange("a p t -> p a t"),
              gh, waits=[("Sh", hs_done), ("Wd", wd_done)] if g8 == 0 else ())
        ghv.append((gh, S.val(gh)))
    pis = []
    pgs = []
    for u in range(8):
        pi, pg = psum_acq()
        pis.append(pi)
        pgs.append(list(pg))
    pre_hld = {}

    def emit_wd_load(dq2, c):
        it0 = c * 4
        bi = c % 2
        eng = "sync" if bi == 0 else "gpsimd"
        gd = "Gd%d" % bi
        S.dma(eng, wdb4[bi][:, :, :],
              wdb_d[it0:it0 + 4, :, dq2 * 512:(dq2 + 1) * 512].rearrange("a p d -> p a d"),
              gd, waits=[("P", dguard[bi])])
        pre_hld[(dq2, c)] = S.val(gd)

    for dq in range(4):
        dco = dq * 512
        for c in range(16):
            it0 = c * 4
            bi = c % 2
            gd = "Gd%d" % bi
            if (dq, c) not in pre_hld:
                emit_wd_load(dq, c)
            hld = pre_hld[(dq, c)]
            for ii in range(4):
                it = it0 + ii
                for u in range(8):
                    pi = pis[u]
                    w_ = []
                    if ii == 0 and u == 0:
                        w_ += [(gd, hld), ghv[min(7, (it0 + 3) // 8)]]
                    if it == 0:
                        w_ += pgs[u]
                    S.emit("tensor", lambda t, i2=it, b2=bi, ii2=ii, u2=u, p=pi: t.matmul(
                        ps_f[p][:, :], hsr[:, i2, u2 * 128:u2 * 128 + 128],
                        wdb4[b2][:, ii2, :],
                        start=(i2 == 0), stop=(i2 == NIT - 1)),
                        waits=w_, inc="P" if (ii == 3 and u == 7) else None)
            dguard[bi] = S.val("P")
        dnd = S.val("P")
        if dq < 3:
            emit_wd_load(dq + 1, 0)
            emit_wd_load(dq + 1, 1)
        pgs = []
        new_pis = []
        for u in range(8):
            rows = slice(u * 128, u * 128 + 128)
            S.dma("gpsimd", x1e, x1_d[rows, dco:dco + 512], "Le",
                  waits=[("V", S.val("V"))])
            x1l = S.val("Le")
            pi = pis[u]
            S.emit("vector", lambda v, p=pi, d2=dco: v.tensor_tensor(
                o_t, ps_f[p][:, :], gb_bc[:, d2:d2 + 512], AL.mult),
                waits=[("P", dnd), ("Le", x1l), ("So2", S.val("So2")),
                       ("Lb", S.val("Lb"))], inc="V")
            S.emit("vector", lambda v: v.tensor_tensor(o_t, o_t, x1e, AL.add), inc="V")
            S.dma("sync", out_s[rows, dco:dco + 512], o_t, "So2",
                  waits=[("V", S.val("V"))])
            psum_rel(pi, "V")
            npi, npg = psum_acq()
            new_pis.append(npi)
            pgs.append(list(npg))
        pis = new_pis
    final_st = S.val("So2")
    S.emit("sync", lambda sy: sy.nop(), waits=[("So2", final_st)])
    AB.pop(); AFP.pop()

    return nc, S, dbg


def realize(nc, S):
    with contextlib.ExitStack() as es:
        for name in S.cnt:
            S.sems[name] = es.enter_context(nc.semaphore(name))
        with nc.Block() as block:
            for eng in S.ENGINES:
                if not S.ops[eng]:
                    continue
                getattr(block, eng)(lambda e, _eng=eng: [cl(e) for cl in S.ops[_eng]])
    return nc


_CACHE = {}


def get_nc():
    if "nc" not in _CACHE:
        nc, S, dbg = build_program()
        realize(nc, S)
        _CACHE["nc"] = nc
        _CACHE["dbg"] = dbg
    return _CACHE["nc"]


def _prep_inputs(inputs):
    f32 = np.float32
    x = np.ascontiguousarray(np.asarray(inputs["L0_attn_x"], f32))
    wq = np.asarray(inputs["L0_attn_Wq"], f32)
    wk = np.asarray(inputs["L0_attn_Wk"], f32)
    wv = np.asarray(inputs["L0_attn_Wv"], f32)
    wo = np.ascontiguousarray(np.asarray(inputs["L0_attn_Wo"], f32))
    wad = np.ascontiguousarray(np.asarray(inputs["L0_attn_Wad"], f32))
    wau = np.asarray(inputs["L0_attn_Wau"], f32)
    wb = np.asarray(inputs["L0_attn_Wb"], f32)
    wgd = np.ascontiguousarray(np.asarray(inputs["L0_attn_Wgd"], f32))
    wgu = np.asarray(inputs["L0_attn_Wgu"], f32)
    cq = np.asarray(inputs["L0_attn_conv_q_w"], f32)
    ck = np.asarray(inputs["L0_attn_conv_k_w"], f32)
    cv = np.asarray(inputs["L0_attn_conv_v_w"], f32)
    rms = np.asarray(inputs["L0_attn_rms_w"], f32)
    wg = np.asarray(inputs["L0_ffn_w_gate"], f32)
    wu = np.asarray(inputs["L0_ffn_w_up"], f32)
    wd = np.asarray(inputs["L0_ffn_w_down"], f32)
    cvec = np.asarray(inputs["L0_c"], f32)
    wmod = np.asarray(inputs["L0_wmod"], f32)

    wg_p = wg.reshape(D, FI // 4, 4).transpose(0, 2, 1).reshape(D, FI)
    wu_p = wu.reshape(D, FI // 4, 4).transpose(0, 2, 1).reshape(D, FI)
    wd_p = wd.reshape(FI // 4, 4, D).transpose(1, 0, 2).reshape(FI, D)
    wg_t = np.ascontiguousarray(
        wg_p.reshape(DT, 128, NIT, 128).transpose(2, 1, 0, 3).reshape(NIT, 128, DT * 128))
    wu_t = np.ascontiguousarray(
        wu_p.reshape(DT, 128, NIT, 128).transpose(2, 1, 0, 3).reshape(NIT, 128, DT * 128))
    wd_t = np.ascontiguousarray(wd_p.reshape(NIT, 128, D))

    ident = np.eye(128, dtype=f32)
    masks = np.zeros((64, 192), f32)
    masks[:, 0:64] = np.tril(np.ones((64, 64), f32), -1)
    masks[:, 64:128] = np.triu(np.ones((64, 64), f32))
    masks[:, 128:192] = np.eye(64, dtype=f32)
    mrep = np.zeros((64, 1536), f32)
    mrep[:, 0:512] = np.tile(-np.tril(np.ones((64, 64), f32), -1), (1, 8))
    mrep[:, 512:1024] = np.tile(np.triu(np.ones((64, 64), f32)), (1, 8))
    mrep[:, 1024:1536] = np.tile(np.eye(64, dtype=f32), (1, 8))
    ones1 = np.ones((1, 128), f32)
    rms_bc = np.ascontiguousarray(np.broadcast_to(np.tile(rms, HL), (128, FQ)))
    cT = np.ascontiguousarray(cvec.T)

    def diag_tiles(cw, lo):
        out = np.zeros((HL, CKW, 128, 128), f32)
        for hh in range(HL):
            for j in range(CKW):
                np.fill_diagonal(out[hh, j], cw[lo + hh * 128: lo + (hh + 1) * 128, j])
        return out

    in_maps = []
    for c in range(8):
        b = c // 4
        qd = c % 4
        fs = slice(qd * FQ, (qd + 1) * FQ)
        hsl = slice(qd * HL, (qd + 1) * HL)
        own_tiles = [k * 4 + qd for k in range(8)]
        ogidx = np.zeros((128, 32), np.int32)
        for jj in range(32):
            k, r = jj // 4, jj % 4
            g = own_tiles[k]
            if k < 6:
                ogidx[:, jj] = r * 3072 + g * 128 + np.arange(128)
            else:
                ogidx[:, jj] = r * 1024 + (g * 128 - 3072) + np.arange(128)
        m = {
            "x_b": x[b],
            "x_own": np.ascontiguousarray(
                x[b].reshape(32, 128, D)[own_tiles].reshape(TOK, D)),
            "wq": np.ascontiguousarray(wq[:, fs]),
            "wk": np.ascontiguousarray(wk[:, fs]),
            "wv": np.ascontiguousarray(wv[:, fs]),
            "cdq": diag_tiles(cq, qd * FQ),
            "cdk": diag_tiles(ck, qd * FQ),
            "cdv": diag_tiles(cv, qd * FQ),
            "wad": wad, "wgd": wgd,
            "wau": np.ascontiguousarray(wau[:, fs]),
            "wgu": np.ascontiguousarray(wgu[:, fs]),
            "wb": np.ascontiguousarray(wb[:, hsl]),
            "rms_bc": rms_bc,
            "wo": wo,
            "wmod_s": np.ascontiguousarray(wmod[:, c * 1536:(c + 1) * 1536]),
            "c_T": cT,
            "wg_t": wg_t, "wu_t": wu_t, "wd_t": wd_t,
            "ident_in": ident, "masks_in": masks, "mrep_in": mrep, "ones1_in": ones1,
            "bsel128": np.full((128, 1), float(b), f32),
            "ogidx": ogidx,
        }
        in_maps.append(m)
    return in_maps


def kernel(**inputs):
    nc = get_nc()
    in_maps = _prep_inputs(inputs)
    res = run_bass_kernel_spmd(nc, in_maps, core_ids=list(range(8)))
    out = np.empty((B, T, D), np.float32)
    for c in range(8):
        b, qd = c // 4, c % 4
        o = res.results[c]["out_s"].reshape(8, 128, D)
        for k in range(8):
            t = k * 4 + qd
            out[b, t * 128:(t + 1) * 128] = o[k]
    return out


# version 13
# speedup vs baseline: 1.1642x; 1.0939x over previous
import contextlib
import numpy as np
import ml_dtypes
import concourse.bass as bass
import concourse.mybir as mybir
from concourse.bass import IndirectOffsetOnAxis
from concourse.bass_utils import run_bass_kernel_spmd

F32 = mybir.dt.float32
BF16 = mybir.dt.bfloat16
I32 = mybir.dt.int32
AF = mybir.ActivationFunctionType
AL = mybir.AluOpType
AX = mybir.AxisListType

B, T, D = 2, 4096, 2048
H, DK, DV, CKW, CH = 16, 128, 128, 4, 64
HL, FQ, TOK = 4, 512, 1024
TC, NTC = 512, 8
NCH_C = TC // CH
DT = D // 128
FI = 8192
NIT = FI // 128

DEBUG = False


class Sched:
    ENGINES = ("tensor", "vector", "scalar", "gpsimd", "sync")

    def __init__(self):
        self.ops = {e: [] for e in self.ENGINES}
        self.cnt = {}
        self.sems = {}

    def _sem(self, name):
        self.cnt.setdefault(name, 0)
        return name

    def val(self, name):
        return self.cnt.get(name, 0)

    def allv(self, *names):
        return [(n, self.val(n)) for n in names]

    def emit(self, eng, fn, waits=(), inc=None, inc_n=1):
        waits = list(waits)
        if eng == "vector":
            waits.append(("V", self.val("V")))
            if inc is None:
                inc = "V"
        elif eng == "scalar":
            waits.append(("A", self.val("A")))
            if inc is None:
                inc = "A"
        if inc is not None:
            self._sem(inc)
            self.cnt[inc] += inc_n
        waits = tuple((self._sem(s), v) for s, v in waits if v > 0)
        incn = (inc, inc_n) if inc else None

        def closure(engine, _fn=fn, _w=waits, _i=incn):
            for s, v in _w:
                engine.wait_ge(self.sems[s], v)
            ins = _fn(engine)
            if _i is not None:
                ins.then_inc(self.sems[_i[0]], _i[1])
        self.ops[eng].append(closure)

    def barrier(self):
        self._sem("B")
        n0 = self.val("B")
        pre = [(k, v) for k, v in self.cnt.items() if k != "B" and v > 0]
        for eng in self.ENGINES:
            self.emit(eng, lambda e: e.nop(), inc="B",
                      waits=pre if eng == "sync" else ())
        tgt = n0 + len(self.ENGINES)
        assert self.cnt["B"] == tgt
        for eng in self.ENGINES:
            self.emit(eng, lambda e: e.nop(), waits=[("B", tgt)])

    def dma(self, eng, out, in_, sem="L", waits=(), transpose=False):
        if transpose:
            self.emit(eng, lambda e, o=out, i=in_: e.dma_start(out=o, in_=i, transpose=True),
                      waits=waits, inc=sem, inc_n=16)
        else:
            self.emit(eng, lambda e, o=out, i=in_: e.dma_start(out=o, in_=i),
                      waits=waits, inc=sem, inc_n=16)


class Arena:
    def __init__(self, nc, name, cols, dtype):
        self.t = nc.ctx.enter_context(nc.sbuf_tensor(name, [128, cols], dtype))
        self.cols = cols
        self.off = 0
        self.marks = []

    def alloc(self, n, align=8):
        o = (self.off + align - 1) // align * align
        assert o + n <= self.cols, f"arena overflow {o}+{n}>{self.cols}"
        self.off = o + n
        return self.t[:, o:o + n]

    def push(self):
        self.marks.append(self.off)

    def pop(self):
        self.off = self.marks.pop()


def v2(t, a, b2):
    return t.rearrange("p (a b) -> p a b", a=a, b=b2)


def v3(t, a, b2, c3):
    return t.rearrange("p (a b c) -> p a b c", a=a, b=b2, c=c3)


def build_program():
    nc = bass.Bass("TRN2", target_bir_lowering=False)
    S = Sched()

    def din(name, shape, dt=F32):
        return nc.dram_tensor(name, shape, dt, kind="ExternalInput").ap()

    x_b = din("x_b", [T, D])
    x_own = din("x_own", [TOK, D])
    wq_i = din("wq", [D, FQ]); wk_i = din("wk", [D, FQ]); wv_i = din("wv", [D, FQ])
    cd_i = {n: din("cd" + n, [HL, CKW, 128, 128]) for n in "qkv"}
    wad_i = din("wad", [D, DK]); wau_i = din("wau", [DK, FQ])
    wgd_i = din("wgd", [D, DV]); wgu_i = din("wgu", [DV, FQ])
    wb_i = din("wb", [D, HL])
    rms_i = din("rms_bc", [128, FQ])
    wo_i = din("wo", [D, D])
    wmod_i = din("wmod_s", [D, 1536])
    cT_i = din("c_T", [D, B])
    wg_i = din("wg_t", [NIT, 128, DT * 128])
    wu_i = din("wu_t", [NIT, 128, DT * 128])
    wd_i = din("wd_t", [NIT, 128, D])
    ident_i = din("ident_in", [128, 128])
    masks_i = din("masks_in", [64, 192])
    mrep_i = din("mrep_in", [64, 1536])
    ones1_i = din("ones1_in", [1, 128])
    bsel_i = din("bsel128", [128, 1])
    ogidx_i = din("ogidx", [128, 32], dt=I32)
    out_s = nc.dram_tensor("out_s", [TOK, D], F32, kind="ExternalOutput").ap()

    xhat_d = nc.dram_tensor("xhat_d", [T, D], BF16).ap()
    modp_d = nc.dram_tensor("modp_d", [1536, B], F32).ap()
    modf_d = nc.dram_tensor("modf_d", [12288, B], F32).ap()
    og_d = nc.dram_tensor("og_d", [T, FQ], BF16).ap()
    ogg_d = [nc.dram_tensor("ogg_d1", [3 * T, FQ], BF16).ap(),
             nc.dram_tensor("ogg_d2", [T, FQ], BF16).ap()]
    ogsl_d = nc.dram_tensor("ogsl_d", [4 * TOK, FQ], BF16).ap()
    x1_d = nc.dram_tensor("x1_d", [TOK, D], F32).ap()
    grow_d = nc.dram_tensor("grow_d", [2, D], F32).ap()
    hs_d = nc.dram_tensor("hs_d", [NIT, 128, TOK], BF16).ap()
    wdb_d = nc.dram_tensor("wdb_d", [NIT, 128, D], BF16).ap()
    wgb_d = nc.dram_tensor("wgb_d", [NIT, 128, DT * 128], BF16).ap()
    wub_d = nc.dram_tensor("wub_d", [NIT, 128, DT * 128], BF16).ap()

    dbg = {}

    def ddbg(name, shape, dt):
        if DEBUG:
            dbg[name] = nc.dram_tensor(name, shape, dt, kind="ExternalOutput").ap()
            return dbg[name]
        return None
    dbg_h = ddbg("dbg_h", [D, T], BF16)
    dbg_qn = ddbg("dbg_qn", [FQ, T], BF16)
    dbg_kn = ddbg("dbg_kn", [FQ, T], BF16)
    dbg_sp = ddbg("dbg_sp", [FQ, T], F32)
    dbg_h2 = ddbg("dbg_h2", [D, TOK], BF16)

    AB = Arena(nc, "ab", 87136, BF16)
    AFP = Arena(nc, "af", 9608, F32)
    ogidx_s = nc.ctx.enter_context(nc.sbuf_tensor("ogidx_s", [128, 32], I32))

    ps_stack = contextlib.ExitStack()
    ps_big = nc.ctx.enter_context(nc.psum_tensor("psbig", [128, 6 * 512], F32))
    ps_f = [ps_big[:, i * 512:(i + 1) * 512] for i in range(6)]
    ps_b = [ps_stack.enter_context(nc.psum_tensor(f"psb{i}", [64, 512], BF16)) for i in range(2)]
    ps_state = {"i": 0, "guard": [()] * 8, "bi": 0, "bguard": [()] * 2, "n": 6}

    def psum_acq():
        i = ps_state["i"] % ps_state["n"]
        ps_state["i"] += 1
        return i, ps_state["guard"][i]

    def psum_rel(i, sem):
        ps_state["guard"][i] = ((sem, S.val(sem)),)

    def psb_acq():
        i = ps_state["bi"] % 2
        ps_state["bi"] += 1
        return i, ps_state["bguard"][i]

    def psb_rel(i, sem):
        ps_state["bguard"][i] = ((sem, S.val(sem)),)

    def psum_acq_set(idxs):
        g = []
        for i in idxs:
            g.extend(ps_state["guard"][i])
        return tuple(g)

    def psum_rel_set(idxs, sem):
        for i in idxs:
            ps_state["guard"][i] = ((sem, S.val(sem)),)

    ident = AB.alloc(128)
    mask_sl = AB.alloc(64)
    mask_ui = AB.alloc(64)
    eye64 = AB.alloc(64)
    ones_bf = AB.alloc(128)
    nmsk = AB.alloc(512)
    umsk = AB.alloc(512)
    erep = AB.alloc(512)
    ones1_f = AFP.alloc(128)
    identF = AFP.alloc(128)
    rms_sb = AFP.alloc(FQ)
    sa_c = AFP.alloc(DT); ca1_c = AFP.alloc(DT)
    sb_c = AFP.alloc(DT); cb1_c = AFP.alloc(DT)
    ga_bc = AFP.alloc(D); gb_bc = AFP.alloc(D)
    bsel_s = AFP.alloc(1)

    S.dma("gpsimd", ident, ident_i[:, :], "Gc")
    S.dma("gpsimd", mask_sl[0:64, :], masks_i[:, 0:64], "Gc")
    S.dma("gpsimd", mask_ui[0:64, :], masks_i[:, 64:128], "Gc")
    S.dma("gpsimd", eye64[0:64, :], masks_i[:, 128:192], "Gc")
    S.dma("gpsimd", nmsk[0:64, :], mrep_i[:, 0:512], "Gc")
    S.dma("gpsimd", umsk[0:64, :], mrep_i[:, 512:1024], "Gc")
    S.dma("gpsimd", erep[0:64, :], mrep_i[:, 1024:1536], "Gc")
    S.emit("vector", lambda v: v.memset(ones_bf, 1.0), inc="V")
    S.dma("sync", ones1_f[0:1, :], ones1_i[:, :], "Lc")
    S.dma("sync", identF, ident_i[:, :], "Lc")
    S.dma("sync", rms_sb, rms_i[:, :], "Lc")
    S.dma("sync", bsel_s, bsel_i[:, :], "Lc")
    S.dma("sync", ogidx_s[:, :], ogidx_i[:, :], "Lc")

    AB.push(); AFP.push()
    csil_v = v2(AB.alloc(DT * B), DT, B)
    modp_v = v2(AFP.alloc(12 * B), 12, B)
    S.dma("gpsimd", csil_v[:, :, :], cT_i.rearrange("(a p) b -> p a b", p=128), "Gc")
    S.emit("scalar", lambda s: s.activation(csil_v[:, :, :], csil_v[:, :, :], AF.Silu),
           waits=[("Gc", S.val("Gc"))], inc="A")
    csil_done = S.val("A")

    wmod_v = [v2(AB.alloc(128 * DT), DT, 128) for _ in range(2)]
    for jt in range(12):
        buf = wmod_v[jt % 2]
        gm = "Gm%d" % (jt % 2)
        wpe = [("P", S.val("P"))] if jt >= 2 else []
        S.dma("gpsimd", buf[:, :, :],
              wmod_i[:, jt * 128:(jt + 1) * 128].rearrange("(a p) j -> p a j", p=128),
              gm, waits=wpe)
        pi, pg = psum_acq()
        ldv = S.val(gm)
        for d in range(DT):
            S.emit("tensor", lambda t, b2=buf, d2=d, p=pi: t.matmul(
                ps_f[p][0:128, 0:B], b2[:, d2, :], csil_v[:, d2, :],
                start=(d2 == 0), stop=(d2 == DT - 1)),
                waits=([(gm, ldv), ("A", csil_done)] + list(pg)) if d == 0 else (),
                inc="P" if d == DT - 1 else None)
        S.emit("scalar", lambda s, p=pi, j=jt: s.copy(modp_v[:, j, :], ps_f[p][0:128, 0:B]),
               waits=[("P", S.val("P"))], inc="A")
        psum_rel(pi, "A")
    S.dma("sync", modp_d.rearrange("(a p) b -> p a b", p=128), modp_v[:, :, :], "Sm",
          waits=[("A", S.val("A"))])
    S.emit("gpsimd", lambda g: g.collective_compute(
        "AllGather", AL.bypass, replica_groups=[list(range(8))],
        ins=[modp_d], outs=[modf_d]),
        waits=[("Sm", S.val("Sm"))], inc="C")
    cc_mod = S.val("C")

    m2 = {}
    for grp, nm in ((0, "sa"), (1, "ca"), (2, "ga"), (3, "sb"), (4, "cb"), (5, "gb")):
        m2[nm] = v2(AFP.alloc(DT * B), DT, B)
        S.dma("sync", m2[nm],
              modf_d[grp * D:(grp + 1) * D, :].rearrange("(a p) b -> p a b", p=128),
              "Lm", waits=[("C", cc_mod)])
    mod_ld = S.val("Lm")
    ga_c = AFP.alloc(DT); gb_c = AFP.alloc(DT)

    def select2(dst, key):
        src = m2[key]
        S.emit("vector", lambda v: v.tensor_tensor(dst, src[:, :, 1], src[:, :, 0], AL.subtract),
               waits=[("Lm", mod_ld)], inc="V")
        S.emit("vector", lambda v: v.scalar_tensor_tensor(
            dst, dst, bsel_s, src[:, :, 0], AL.mult, AL.add),
            waits=[("Lc", S.val("Lc"))], inc="V")

    select2(sa_c, "sa"); select2(ca1_c, "ca"); select2(sb_c, "sb")
    select2(cb1_c, "cb"); select2(ga_c, "ga"); select2(gb_c, "gb")
    S.emit("vector", lambda v: v.tensor_single_scalar(ca1_c, ca1_c, 1.0, AL.add), inc="V")
    S.emit("vector", lambda v: v.tensor_single_scalar(cb1_c, cb1_c, 1.0, AL.add), inc="V")

    sgn = AFP.alloc(DT)
    growsb = AFP.alloc(D)
    for gi, (gc, bcd) in enumerate(((ga_c, ga_bc), (gb_c, gb_bc))):
        S.emit("vector", lambda v, g=gc: v.tensor_single_scalar(sgn, g, 0.0, AL.is_ge),
               waits=[("P", S.val("P"))], inc="V")
        S.emit("vector", lambda v: v.tensor_scalar(sgn, sgn, 2.0, -1.0, AL.mult, AL.add), inc="V")
        S.emit("scalar", lambda s, g=gc: s.activation(g, g, AF.Abs),
               waits=[("V", S.val("V"))], inc="A")
        S.emit("vector", lambda v, g=gc: v.tensor_single_scalar(g, g, 0.25, AL.max),
               waits=[("A", S.val("A"))], inc="V")
        S.emit("vector", lambda v, g=gc: v.tensor_tensor(g, g, sgn, AL.mult), inc="V")
        gd = S.val("V")
        S.emit("sync", lambda e, g=gc, gi2=gi: e.dma_start(
            out=grow_d[gi2:gi2 + 1, :].rearrange("one (a p) -> p a one", p=128),
            in_=g.rearrange("p (a one) -> p a one", one=1),
            allow_slow_non_contiguous=True),
            waits=[("V", gd)], inc="Sm", inc_n=16)
        S.dma("sync", bcd.rearrange("p (one d) -> p one d", one=1),
              grow_d[gi:gi + 1, :].partition_broadcast(128), "Lb",
              waits=[("Sm", S.val("Sm"))])
    AB.push(); AFP.push()
    NLB = 4
    x_t4 = [[AB.alloc(D) for _ in range(NLB)] for _ in range(2)]
    xh_t4 = [[AB.alloc(D) for _ in range(NLB)] for _ in range(2)]
    sq_scr = AB.alloc(D)
    stat4 = AFP.alloc(4 * NLB)
    ln_guard = [(0, 0), (0, 0)]
    ln_ch = []

    for ib in range(T // 128 // NLB):
        bi = ib % 2
        lx = "Gx%d" % bi
        sx = "Sx%d" % bi
        gv, gst = ln_guard[bi]
        for k in range(NLB):
            i = ib * NLB + k
            S.dma("gpsimd", x_t4[bi][k], x_b[i * 128:(i + 1) * 128, :], lx,
                  waits=[("V", gv), (sx, gst)] if k == 0 else ())
        ldv = S.val(lx)
        mu4 = stat4[:, 0:NLB]; ssq4 = stat4[:, NLB:2 * NLB]
        var4 = stat4[:, 2 * NLB:3 * NLB]; rstd4 = stat4[:, 3 * NLB:4 * NLB]
        for k in range(NLB):
            S.emit("vector", lambda v, x=x_t4[bi][k], k2=k: v.reduce_sum(
                stat4[:, k2:k2 + 1], x, AX.X),
                waits=[(lx, ldv), ("A", S.val("A"))] if k == 0 else (), inc="V")
        for k in range(NLB):
            S.emit("scalar", lambda st, x=x_t4[bi][k], k2=k: st.activation(
                sq_scr, x, AF.Square, accum_out=stat4[:, NLB + k2:NLB + k2 + 1]),
                waits=[(lx, ldv), ("V", S.val("V"))] if k == 0 else (), inc="A")
        S.emit("vector", lambda v: v.tensor_single_scalar(mu4, mu4, 1.0 / D, AL.mult), inc="V")
        S.emit("vector", lambda v: v.tensor_tensor(var4, mu4, mu4, AL.mult), inc="V")
        S.emit("vector", lambda v: v.scalar_tensor_tensor(var4, ssq4, 1.0 / D, var4,
                                                          AL.mult, AL.subtract),
               waits=[("A", S.val("A"))], inc="V")
        S.emit("vector", lambda v: v.tensor_single_scalar(var4, var4, 1e-5, AL.add), inc="V")
        S.emit("vector", lambda v: v.reciprocal(rstd4, var4), inc="V")
        S.emit("scalar", lambda st: st.activation(rstd4, rstd4, AF.Sqrt),
               waits=[("V", S.val("V"))], inc="A")
        for k in range(NLB):
            S.emit("vector", lambda v, x=x_t4[bi][k], o=xh_t4[bi][k], k2=k: v.tensor_scalar(
                o, x, stat4[:, k2:k2 + 1], stat4[:, 3 * NLB + k2:3 * NLB + k2 + 1],
                AL.subtract, AL.mult),
                waits=[("A", S.val("A"))] if k == 0 else (), inc="V")
        xv = S.val("V")
        for k in range(NLB):
            i = ib * NLB + k
            S.dma("sync", xhat_d[i * 128:(i + 1) * 128, :], xh_t4[bi][k], sx,
                  waits=[("V", xv)] if k == 0 else ())
        ln_guard[bi] = (xv, S.val(sx))
        ln_ch.append((sx, S.val(sx)))

    ln_done = [("Sx0", S.val("Sx0")), ("Sx1", S.val("Sx1"))]
    AB.pop(); AFP.pop()
    AB.pop(); AFP.pop()
    S.barrier()

    AB.push(); AFP.push()
    wq_v = v2(AB.alloc(FQ * DT), DT, FQ)
    wk_v = v2(AB.alloc(FQ * DT), DT, FQ)
    wv_v = v2(AB.alloc(FQ * DT), DT, FQ)
    wad_v = v2(AB.alloc(DK * DT), DT, DK)
    wgd_v = v2(AB.alloc(DV * DT), DT, DV)
    wb_v = v2(AB.alloc(HL * DT), DT, HL)
    wau_sb = AB.alloc(FQ); wgu_sb = AB.alloc(FQ)
    cd_sb = {n: v3(AB.alloc(HL * CKW * 128), HL, CKW, 128) for n in "qkv"}
    for dst, src in ((wq_v, wq_i), (wk_v, wk_i), (wv_v, wv_i),
                     (wad_v, wad_i), (wgd_v, wgd_i), (wb_v, wb_i)):
        for d in range(DT):
            S.dma("gpsimd", dst[:, d, :], src[d * 128:(d + 1) * 128, :], "Gw")
    S.dma("gpsimd", wau_sb, wau_i[:, :], "Gw")
    S.dma("gpsimd", wgu_sb, wgu_i[:, :], "Gw")
    for n in "qkv":
        for hh in range(HL):
            for j in range(CKW):
                S.dma("gpsimd", cd_sb[n][:, hh, j, :], cd_i[n][hh, j, :, :], "Gw")
    wld = S.val("Gw")


    h_v = v2(AB.alloc(DT * TC), DT, TC)
    pre = {n: v2(AB.alloc(HL * (TC + 8)), HL, TC + 8) for n in "qkv"}
    sil = {n: pre[n][:, :, 0:TC] for n in "qkv"}
    qn_v = sil["q"]
    kn_v = sil["k"]
    qt_v = sil["q"]
    kt_v = v2(AB.alloc(HL * TC), HL, TC)
    kb_v = v2(AB.alloc(HL * TC), HL, TC)
    kdf_v = v2(AB.alloc(HL * TC), HL, TC)
    kdt_v3 = v3(AB.alloc(HL * NCH_C * 128), HL, NCH_C, 128)
    vt_v3 = v3(AB.alloc(HL * NCH_C * 128), HL, NCH_C, 128)
    sqs = AB.alloc(HL * TC)
    z_sb = sqs[:, 512:1024]; zg_sb = sqs[:, 0:512]
    eb_raw = AB.alloc(HL * TC)
    eb_b = v2(eb_raw, HL, TC)
    ebm_b = v2(AB.alloc(HL * TC), HL, TC)
    ebd_b = ebm_b
    sp_b = v2(AFP.alloc(HL * TC), HL, TC)
    cum_b = v2(AFP.alloc(HL * TC), HL, TC)
    r_big = sp_b.rearrange("p a b -> p (a b)")
    ebl_v = v2(AFP.alloc(HL * NCH_C), HL, NCH_C)
    bhr = AB.alloc(TC)
    bhT_v = v2(AB.alloc(NCH_C * HL), NCH_C, HL)
    gate_v = v2(AB.alloc(4 * TC), 4, TC)
    TT3 = v3(eb_raw, NCH_C, HL, 64)
    AT3 = v3(sqs, NCH_C, HL, 64)
    Npr = [AB.alloc(512) for _ in range(2)]
    Ppr = [AB.alloc(512) for _ in range(2)]
    Qpr = [AB.alloc(512) for _ in range(2)]
    Rpr = [AB.alloc(512) for _ in range(2)]
    og_sb = AB.alloc(FQ)
    Sf_v = v2(AFP.alloc(HL * DV), HL, DV)
    Sb_v = v2(AB.alloc(HL * DV), HL, DV)
    rhsu = AB.alloc(FQ)
    u_sb = rhsu
    ms_sb = AFP.alloc(HL)
    rr_sb = AFP.alloc(HL)
    S.emit("vector", lambda v: v.memset(Sf_v[:, :, :], 0.0), inc="V")
    S.emit("vector", lambda v: v.memset(Sb_v[:, :, :], 0.0), inc="V")
    for n in "qkv":
        S.emit("vector", lambda v, n2=n: v.memset(pre[n2][:, :, 0:3], 0.0), inc="V")

    sab = AFP.alloc(16)
    sa_bf = AB.alloc(DT)
    S.emit("vector", lambda v: v.tensor_copy(sa_bf, sa_c), inc="V")
    sabf_v = S.val("V")
    blist = []
    for base, wvv in ((0, wq_v), (4, wk_v), (8, wv_v)):
        for hh in range(HL):
            blist.append((lambda d, w=wvv, h3=hh: w[:, d, h3 * 128:(h3 + 1) * 128],
                          base + hh, 128))
    blist.append(((lambda d: wad_v[:, d, :]), 12, 128))
    blist.append(((lambda d: wgd_v[:, d, :]), 13, 128))
    blist.append(((lambda d: wb_v[:, d, :]), 14, HL))
    pi_b, pg_b = psum_acq()
    lastb = (len(blist) - 1, DT - 1)
    for ui, (wf, col, rows) in enumerate(blist):
        for d in range(DT):
            S.emit("tensor", lambda t, f=wf, d2=d, c2=col, r=rows, p=pi_b,
                   st=(d == 0), sp=(d == DT - 1): t.matmul(
                ps_f[p][0:r, c2:c2 + 1], f(d2), sa_bf[:, d2:d2 + 1],
                start=st, stop=sp),
                waits=([("Gw", wld), ("V", sabf_v)] + list(pg_b))
                if (ui == 0 and d == 0) else (),
                inc="P" if (ui, d) == lastb else None)
    S.emit("scalar", lambda s2, p=pi_b: s2.copy(sab[:, 0:14], ps_f[p][:, 0:14]),
           waits=[("P", S.val("P"))], inc="A")
    S.emit("scalar", lambda s2, p=pi_b: s2.copy(sab[0:HL, 14:15], ps_f[p][0:HL, 14:15]),
           inc="A")
    psum_rel(pi_b, "A")
    for wvv, wdim in ((wq_v, FQ), (wk_v, FQ), (wv_v, FQ),
                      (wad_v, DK), (wgd_v, DV), (wb_v, HL)):
        S.emit("vector", lambda v, w=wvv, n2=wdim: v.tensor_tensor(
            w[:, :, :], w[:, :, :],
            ca1_c.rearrange("p (a one) -> p a one", one=1).to_broadcast([128, DT, n2]),
            AL.mult),
            waits=[("P", S.val("P"))], inc="V")
    wscale_done = S.val("V")

    ldh_map = {}
    h_free_hist = [0] * NTC

    def emit_h(tc2, pwait):
        lh = "Lh%d" % (tc2 % 2)
        t02 = tc2 * TC
        for ft in range(DT):
            S.dma("sync", h_v[:, ft, :], xhat_d[t02:t02 + TC, ft * 128:(ft + 1) * 128],
                  lh, waits=([ln_ch[tc2], ("P", pwait)]) if ft == 0 else (),
                  transpose=True)
        ldh_map[tc2] = (lh, S.val(lh))

    gbuf = [AB.alloc(FQ) for _ in range(2)]
    gguard = [0, 0]

    def emit_gathers(kr, ccv):
        for k in kr:
            for r in range(4):
                j = k * 4 + r
                gb2 = gbuf[j % 2]
                gg = "Gg%d" % (j % 2)
                sg = "Sg%d" % (j % 2)
                S.emit("gpsimd", lambda g, o=gb2, j2=j, h=k // 6: g.indirect_dma_start(
                    o, None, ogg_d[h],
                    IndirectOffsetOnAxis(ogidx_s[:, j2:j2 + 1], 0)),
                    waits=[("C", ccv), (("Sg%d" % (j % 2)), gguard[j % 2]),
                           ("Lc", S.val("Lc"))],
                    inc=gg, inc_n=16)
                S.dma("sync", ogsl_d[(r * 8 + k) * 128:(r * 8 + k) * 128 + 128, :], gb2, sg,
                      waits=[(gg, S.val(gg))])
                gguard[j % 2] = S.val(sg)

    h_free = [0]
    so_prog = [0] * NTC
    emit_h(0, 0)
    for tc in range(NTC):
        t0 = tc * TC
        hw = ldh_map[tc]
        if DEBUG:
            for ft in range(DT):
                S.dma("sync", dbg_h[ft * 128:(ft + 1) * 128, t0:t0 + TC], h_v[:, ft, :],
                      "Sd", waits=[hw])

        if tc > 0:
            for n in "qkv":
                for hh in range(HL):
                    S.emit("vector", lambda v, n2=n, h3=hh: v.tensor_copy(
                        pre[n2][:, h3, 0:3], pre[n2][:, h3, TC:TC + 3]),
                        waits=[("A", S.val("A"))] if (n == "q" and hh == 0) else (), inc="V")
        halo_done = S.val("V")
        evac_a = {}
        for uo, (n, wvv) in enumerate((("q", wq_v), ("k", wk_v), ("v", wv_v))):
            for hh in range(HL):
                pi, pg = psum_acq()
                for d in range(DT):
                    S.emit("tensor", lambda t, w=wvv, d2=d, h3=hh, p=pi, h2=h_v: t.matmul(
                        ps_f[p][:, :], w[:, d2, h3 * 128:(h3 + 1) * 128], h2[:, d2, :],
                        start=(d2 == 0), stop=(d2 == DT - 1)),
                        waits=([hw, ("V", wscale_done)] + list(pg)) if d == 0 else (),
                        inc="P" if d == DT - 1 else None)
                S.emit("scalar", lambda s, n2=n, h3=hh, p=pi, c2=uo * 4 + hh: s.activation(
                    pre[n2][:, h3, 3:3 + TC], ps_f[p][:, :], AF.Identity,
                    bias=sab[:, c2:c2 + 1]),
                    waits=[("P", S.val("P")), ("V", halo_done)], inc="A")
                psum_rel(pi, "A")
                evac_a[(n, hh)] = S.val("A")
        for n in "qkv":
            for hh in range(HL):
                pi, pg = psum_acq()
                for j in range(CKW):
                    S.emit("tensor", lambda t, n2=n, h3=hh, j2=j, p=pi: t.matmul(
                        ps_f[p][:, :], cd_sb[n2][:, h3, j2, :], pre[n2][:, h3, j2:j2 + TC],
                        start=(j2 == 0), stop=(j2 == CKW - 1)),
                        waits=([("A", evac_a[(n, hh)])] + list(pg)) if j == 0 else (),
                        inc="P" if j == CKW - 1 else None)
                S.emit("scalar", lambda s, n2=n, h3=hh, p=pi: s.activation(
                    sil[n2][:, h3, :], ps_f[p][:, :], AF.Silu),
                    waits=[("P", S.val("P"))], inc="A")
                psum_rel(pi, "A")
        sil_done = S.val("A")
        for n in ("q", "k"):
            S.emit("scalar", lambda s, n2=n: s.activation(
                v2(sqs, HL, TC), sil[n2][:, :, :], AF.Square),
                waits=[("A", sil_done), ("P", S.val("P"))], inc="A")
            sqd = S.val("A")
            qb = (0, 1, 2, 3)
            pgq = psum_acq_set(qb)
            for hh in range(HL):
                S.emit("tensor", lambda t, h3=hh: t.matmul(
                    ps_f[h3][:, :], ones_bf, sqs[:, h3 * TC:(h3 + 1) * TC],
                    start=True, stop=True),
                    waits=([("A", sqd)] + list(pgq)) if hh == 0 else (),
                    inc="P" if hh == HL - 1 else None)
            S.emit("vector", lambda v: v.tensor_single_scalar(
                r_big, ps_big[:, 0:2048], 1e-12, AL.max),
                waits=[("P", S.val("P"))], inc="V")
            psum_rel_set(qb, "V")
            S.emit("vector", lambda v: v.reciprocal(r_big, r_big), inc="V")
            S.emit("scalar", lambda s: s.activation(r_big, r_big, AF.Sqrt),
                   waits=[("V", S.val("V"))], inc="A")
            S.emit("vector", lambda v, n2=n: v.tensor_tensor(
                sil[n2][:, :, :], sil[n2][:, :, :], v2(r_big, HL, TC), AL.mult),
                waits=[("A", S.val("A"))], inc="V")
        l2_done = S.val("V")
        if DEBUG:
            for hh in range(HL):
                S.dma("sync", dbg_qn[hh * 128:(hh + 1) * 128, t0:t0 + TC], qn_v[:, hh, :],
                      "Sd", waits=[("V", S.val("V"))])
                S.dma("sync", dbg_kn[hh * 128:(hh + 1) * 128, t0:t0 + TC], kn_v[:, hh, :],
                      "Sd", waits=[("V", S.val("V"))])

        def shared_proj(wtile, dst_sb, bcol):
            pi, pg = psum_acq()
            for d in range(DT):
                S.emit("tensor", lambda t, d2=d, p=pi, w=wtile, h2=h_v: t.matmul(
                    ps_f[p][:, :], w[:, d2, :], h2[:, d2, :],
                    start=(d2 == 0), stop=(d2 == DT - 1)),
                    waits=([hw] + list(pg)) if d == 0 else (),
                    inc="P" if d == DT - 1 else None)
            S.emit("scalar", lambda s, p=pi, o=dst_sb, c2=bcol: s.activation(
                o, ps_f[p][:, :], AF.Identity, bias=sab[:, c2:c2 + 1]),
                   waits=[("P", S.val("P"))], inc="A")
            psum_rel(pi, "A")
            return S.val("A")

        zd = shared_proj(wad_v, z_sb, 12)
        zgd = shared_proj(wgd_v, zg_sb, 13)
        pi, pg = psum_acq()
        for d in range(DT):
            S.emit("tensor", lambda t, d2=d, p=pi, h2=h_v: t.matmul(
                ps_f[p][0:HL, :], wb_v[:, d2, :], h2[:, d2, :],
                start=(d2 == 0), stop=(d2 == DT - 1)),
                waits=([hw] + list(pg)) if d == 0 else (),
                inc="P" if d == DT - 1 else None)
        h_free_hist[tc] = S.val("P")
        if tc + 1 < NTC:
            emit_h(tc + 1, h_free_hist[tc])
        S.emit("vector", lambda v, p=pi: v.tensor_scalar(
            bhr[0:HL, :], ps_f[p][0:HL, :], sab[0:HL, 14:15], -10.0, AL.add, AL.max),
            waits=[("P", S.val("P")), ("A", S.val("A"))], inc="V")
        psum_rel(pi, "V")
        S.emit("vector", lambda v: v.tensor_single_scalar(
            bhr[0:HL, :], bhr[0:HL, :], 10.0, AL.min), inc="V")
        S.emit("scalar", lambda s: s.activation(bhr[0:HL, :], bhr[0:HL, :], AF.Sigmoid),
               waits=[("V", S.val("V"))], inc="A")
        bhd = S.val("A")
        bi2, bg = psb_acq()
        for c8 in range(NCH_C):
            S.emit("tensor", lambda t, s2=c8, p=bi2: t.matmul(
                ps_b[p][0:64, s2 * HL:(s2 + 1) * HL], bhr[0:HL, s2 * 64:(s2 + 1) * 64],
                ident[0:HL, 0:HL], is_transpose=True,
                start=(s2 == 0), stop=(s2 == NCH_C - 1)),
                waits=([("A", bhd)] + list(bg)) if c8 == 0 else (),
                inc="P" if c8 == NCH_C - 1 else None)
        S.emit("vector", lambda v, p=bi2: v.tensor_copy(
            bhT_v[0:64, :, :], ps_b[p][0:64, 0:NCH_C * HL]),
            waits=[("P", S.val("P"))], inc="V")
        psb_rel(bi2, "V")
        qb = (0, 1, 2, 3)
        pgq = psum_acq_set(qb)
        for sub in range(4):
            S.emit("tensor", lambda t, s2=sub: t.matmul(
                ps_f[s2][:, :], zg_sb[:, s2 * 128:(s2 + 1) * 128], wgu_sb,
                start=True, stop=True),
                waits=([("A", zgd)] + list(pgq)) if sub == 0 else (),
                inc="P" if sub == 3 else None)
        S.emit("scalar", lambda s: s.activation(
            gate_v[:, :, :], ps_big[:, 0:2048], AF.Sigmoid),
            waits=[("P", S.val("P"))], inc="A")
        psum_rel_set(qb, "A")
        S.emit("vector", lambda v: v.tensor_tensor(
            gate_v[:, :, :], gate_v[:, :, :],
            v2(rms_sb, 1, FQ).to_broadcast([128, 4, FQ]), AL.mult),
            waits=[("A", S.val("A")), ("So", S.val("So"))], inc="V")

        qb = (0, 1, 2, 3)
        pgq = psum_acq_set(qb)
        for hh in range(HL):
            S.emit("tensor", lambda t, h3=hh: t.matmul(
                ps_f[h3][:, :], wau_sb[:, h3 * 128:(h3 + 1) * 128], z_sb,
                start=True, stop=True),
                waits=([("A", zd)] + list(pgq)) if hh == 0 else (),
                inc="P" if hh == HL - 1 else None)
        S.emit("vector", lambda v: v.tensor_scalar(
            sp_b.rearrange("p a b -> p (a b)"), ps_big[:, 0:2048], -10.0, 10.0,
            AL.max, AL.min),
            waits=[("P", S.val("P"))], inc="V")
        psum_rel_set(qb, "V")
        S.emit("scalar", lambda s: s.activation(sp_b[:, :, :], sp_b[:, :, :], AF.Exp),
               waits=[("V", S.val("V"))], inc="A")
        S.emit("scalar", lambda s: s.activation(sp_b[:, :, :], sp_b[:, :, :], AF.Ln, bias=1.0),
               inc="A")
        spd = S.val("A")
        for hh in range(HL):
            for c in range(NCH_C):
                S.emit("vector", lambda v, h3=hh, c2=c: v.tensor_tensor_scan(
                    cum_b[:, h3, c2 * CH:(c2 + 1) * CH], sp_b[:, h3, c2 * CH:(c2 + 1) * CH],
                    sp_b[:, h3, c2 * CH:(c2 + 1) * CH], 0.0, AL.add, AL.bypass),
                    waits=[("A", spd)] if (hh == 0 and c == 0) else (), inc="V")
        cumd = S.val("V")
        S.emit("scalar", lambda s: s.activation(eb_b[:, :, :], cum_b[:, :, :], AF.Exp, scale=-1.0),
               waits=[("V", cumd)], inc="A")
        S.emit("scalar", lambda s: s.activation(ebm_b[:, :, :], cum_b[:, :, :], AF.Exp), inc="A")
        S.emit("scalar", lambda s: s.activation(
            ebl_v[:, :, :], cum_b.rearrange("p a (c t) -> p a c t", t=CH)[:, :, :, CH - 1],
            AF.Exp, scale=-1.0), inc="A")
        ebld = S.val("A")
        S.emit("vector", lambda v: v.tensor_tensor(
            kt_v[:, :, :], kn_v[:, :, :], eb_b[:, :, :], AL.mult),
            waits=[("A", ebld), ("P", S.val("P"))], inc="V")
        S.emit("vector", lambda v: v.tensor_tensor(
            kb_v[:, :, :], kn_v[:, :, :], ebm_b[:, :, :], AL.mult), inc="V")
        S.emit("vector", lambda v: v.tensor_tensor(
            ebd_b.rearrange("p a (c t) -> p a c t", t=CH),
            ebm_b.rearrange("p a (c t) -> p a c t", t=CH),
            ebl_v.rearrange("p a (c o) -> p a c o", o=1).to_broadcast([128, HL, NCH_C, CH]),
            AL.mult), inc="V")
        S.emit("vector", lambda v: v.tensor_tensor(
            kdf_v[:, :, :], kn_v[:, :, :], ebd_b[:, :, :], AL.mult), inc="V")
        S.emit("vector", lambda v: v.tensor_tensor(
            qt_v[:, :, :], qn_v[:, :, :], eb_b[:, :, :], AL.mult), inc="V")
        scald = S.val("V")
        for h3c in range(HL):
            for src_t, dst_t, wsem in ((kdf_v, kdt_v3, ("V", scald)),
                                       (sil["v"], vt_v3, ("A", sil_done))):
                for cg in range(2):
                    bi2, bg = psb_acq()
                    for cc4 in range(4):
                        c8 = cg * 4 + cc4
                        S.emit("tensor", lambda t, h4=h3c, b2=c8, k4=cc4, p=bi2, st=src_t: t.matmul(
                            ps_b[p][0:64, k4 * 128:(k4 + 1) * 128],
                            st[:, h4, b2 * 64:(b2 + 1) * 64], ident, is_transpose=True,
                            start=(k4 == 0), stop=(k4 == 3)),
                            waits=([wsem] + list(bg)) if cc4 == 0 else (),
                            inc="P" if cc4 == 3 else None)
                    S.emit("scalar", lambda s, h4=h3c, g2=cg, p=bi2, dt2=dst_t: s.copy(
                        dt2[0:64, h4, g2 * 4:(g2 + 1) * 4, :], ps_b[p][0:64, :]),
                        waits=[("P", S.val("P"))], inc="A")
                    psb_rel(bi2, "A")
        kdvt_A = S.val("A")
        ebl_done = ebld
        ttv = [0, 0, 0, 0]
        seq_sv = [S.val("V")]

        def precompute_pair(p):
            st = {}
            thunks = []

            def stage_A():
                pga = psum_acq_set((0,))
                for u in range(8):
                    par, hh = u // 4, u % 4
                    c = 2 * p + par
                    csl = slice(c * CH, (c + 1) * CH)
                    S.emit("tensor", lambda t, h3=hh, cs=csl, u2=u: t.matmul(
                        ps_f[0][0:64, u2 * 64:(u2 + 1) * 64], kt_v[:, h3, cs], kb_v[:, h3, cs],
                        start=(u2 == 0), stop=(u2 == 7)),
                        waits=([("V", scald)] + list(pga)) if u == 0 else (),
                        inc="P" if u == 7 else None)
                st["amm"] = S.val("P")
            thunks.append(stage_A)

            def stage_N():
                S.emit("vector", lambda v: v.tensor_tensor(
                    Npr[p % 2][0:64, :].rearrange("q (a b) -> q a b", b=64),
                    ps_f[0][0:64, :].rearrange("q (a b) -> q a b", b=64),
                    bhT_v[0:64, 2 * p:2 * p + 2, :].rearrange("q a b -> q (a b)")
                    .rearrange("q (a o) -> q a o", o=1).to_broadcast([64, 8, 64]),
                    AL.mult), waits=[("P", st["amm"])], inc="V")
                S.emit("vector", lambda v: v.tensor_tensor(
                    Npr[p % 2][0:64, :], Npr[p % 2][0:64, :], nmsk[0:64, :], AL.mult), inc="V")
                psum_rel_set((0,), "V")
                st["n"] = S.val("V")
            thunks.append(stage_N)

            def stage_P0():
                bi2, bg = psb_acq()
                for u in range(8):
                    S.emit("tensor", lambda t, u2=u, pb=bi2: t.matmul(
                        ps_b[pb][0:64, u2 * 64:(u2 + 1) * 64],
                        Npr[p % 2][0:64, u2 * 64:(u2 + 1) * 64],
                        ident[0:64, 0:64], is_transpose=True,
                        start=(u2 == 0), stop=(u2 == 7)),
                        waits=([("V", st["n"])] + list(bg)) if u == 0 else (),
                        inc="P" if u == 7 else None)
                S.emit("scalar", lambda s2, pb=bi2: s2.copy(
                    Ppr[p % 2][0:64, :], ps_b[pb][0:64, :]),
                    waits=[("P", S.val("P"))], inc="A")
                psb_rel(bi2, "A")
                S.emit("vector", lambda v: v.tensor_copy(Qpr[p % 2][0:64, :], Npr[p % 2][0:64, :]),
                       inc="V")
                S.emit("vector", lambda v: v.tensor_tensor(
                    Rpr[p % 2][0:64, :], Ppr[p % 2][0:64, :], erep[0:64, :], AL.add),
                    waits=[("A", S.val("A"))], inc="V")
                st["ev"] = (S.val("V"), S.val("A"))
            thunks.append(stage_P0)

            def mk_lvl_mm(lev):
                def stage():
                    pga = psum_acq_set((0, 1, 2))
                    ev, ea = st["ev"]
                    for u in range(8):
                        usl = slice(u * 64, (u + 1) * 64)
                        S.emit("tensor", lambda t, us=usl: t.matmul(
                            ps_f[0][0:64, us], Ppr[p % 2][0:64, us], Qpr[p % 2][0:64, us],
                            start=(us.start == 0), stop=(us.start == 448)),
                            waits=([("V", ev), ("A", ea)] + list(pga)) if u == 0 else ())
                    for u in range(8):
                        usl = slice(u * 64, (u + 1) * 64)
                        S.emit("tensor", lambda t, us=usl: t.matmul(
                            ps_f[1][0:64, us], Qpr[p % 2][0:64, us], Ppr[p % 2][0:64, us],
                            start=(us.start == 0), stop=(us.start == 448)))
                    for u in range(8):
                        usl = slice(u * 64, (u + 1) * 64)
                        S.emit("tensor", lambda t, us=usl: t.matmul(
                            ps_f[2][0:64, us], Qpr[p % 2][0:64, us], Rpr[p % 2][0:64, us],
                            start=(us.start == 0), stop=(us.start == 448)),
                            inc="P" if u == 7 else None)
                    st["mm"] = S.val("P")
                return stage

            def mk_lvl_ev(lev):
                def stage():
                    S.emit("scalar", lambda s2: s2.copy(Qpr[p % 2][0:64, :], ps_f[0][0:64, :]),
                           waits=[("P", st["mm"])], inc="A")
                    S.emit("scalar", lambda s2: s2.copy(Ppr[p % 2][0:64, :], ps_f[1][0:64, :]),
                           inc="A")
                    S.emit("vector", lambda v: v.tensor_tensor(
                        Rpr[p % 2][0:64, :], ps_f[2][0:64, :], Rpr[p % 2][0:64, :], AL.add),
                        waits=[("P", st["mm"])], inc="V")
                    psum_rel_set((0, 1), "A")
                    psum_rel_set((2,), "V")
                    st["ev"] = (S.val("V"), S.val("A"))
                return stage

            for lev in range(5):
                thunks.append(mk_lvl_mm(lev))
                thunks.append(mk_lvl_ev(lev))

            def stage_TT():
                ev, ea = st["ev"]
                S.emit("vector", lambda v: v.tensor_tensor(
                    TT3[0:64, 2 * p:2 * p + 2, :, :],
                    Rpr[p % 2][0:64, :].rearrange("q (a b c) -> q a b c", a=2, c=64),
                    bhT_v[0:64, 2 * p:2 * p + 2, :].rearrange(
                        "q a (b o) -> q a b o", o=1).to_broadcast([64, 2, HL, 64]),
                    AL.mult), waits=[("V", ev), ("A", ea)], inc="V")
            thunks.append(stage_TT)

            def stage_attnT():
                pga = psum_acq_set((0,))
                for u in range(8):
                    par, hh = u // 4, u % 4
                    c = 2 * p + par
                    csl = slice(c * CH, (c + 1) * CH)
                    S.emit("tensor", lambda t, h3=hh, cs=csl, u2=u: t.matmul(
                        ps_f[0][0:64, u2 * 64:(u2 + 1) * 64], kb_v[:, h3, cs], qt_v[:, h3, cs],
                        start=(u2 == 0), stop=(u2 == 7)),
                        waits=(list(pga) + [("V", scald)]) if u == 0 else (),
                        inc="P" if u == 7 else None)
                amm = S.val("P")
                S.emit("vector", lambda v: v.tensor_tensor(
                    AT3[0:64, 2 * p:2 * p + 2, :, :],
                    ps_f[0][0:64, :].rearrange("q (a b c) -> q a b c", a=2, c=64),
                    umsk[0:64, :].rearrange("q (a b c) -> q a b c", a=2, c=64),
                    AL.mult), waits=[("P", amm)], inc="V")
                psum_rel_set((0,), "V")
                ttv[p] = S.val("V")
            thunks.append(stage_attnT)
            return thunks

        seq_rot = {"i": 0}

        def seq_acq():
            i = 3 + seq_rot["i"] % 3
            seq_rot["i"] += 1
            return i, ps_state["guard"][i]

        def scan_step_pieces(c):
            csl = slice(c * CH, (c + 1) * CH)
            ro = (c % 2) * 64
            rsl = slice(ro, ro + 64)
            pieces = []
            stx = {}

            def piece1():
              pia, pga = seq_acq()
              stx["pia"] = pia
              for hh in range(HL):
                S.emit("tensor", lambda t, h3=hh, cs=csl, p=pia: t.matmul(
                    ps_f[p][0:64, h3 * 128:(h3 + 1) * 128], kt_v[:, h3, cs], Sb_v[:, h3, :],
                    start=(h3 == 0), stop=(h3 == HL - 1)),
                    waits=([("V", max(seq_sv[0], scald)), ("A", kdvt_A)] + list(pga))
                    if hh == 0 else (),
                    inc="P" if hh == HL - 1 else None)
              mm1d = S.val("P")
              S.emit("vector", lambda v, p=pia, c2=c: v.tensor_tensor(
                rhsu[0:64, :], vt_v3[0:64, :, c2, :], ps_f[p][0:64, :], AL.subtract),
                waits=[("P", mm1d), ("A", kdvt_A)], inc="V")
              stx["rhv"] = S.val("V")
              psum_rel(pia, "V")
            pieces.append(piece1)

            def piece2():
              rhv = stx["rhv"]
              pib, pgb = seq_acq()
              for hh in range(HL):
                S.emit("tensor", lambda t, h3=hh, c2=c, p=pib: t.matmul(
                    ps_f[p][0:64, h3 * 128:(h3 + 1) * 128], TT3[0:64, c2, h3, :],
                    rhsu[0:64, h3 * 128:(h3 + 1) * 128],
                    start=(h3 == 0), stop=(h3 == HL - 1)),
                    waits=([("V", max(rhv, ttv[c // 2]))] + list(pgb)) if hh == 0 else (),
                    inc="P" if hh == HL - 1 else None)
              S.emit("scalar", lambda s2, p=pib: s2.copy(u_sb[0:64, :], ps_f[p][0:64, :]),
                     waits=[("P", S.val("P"))], inc="A")
              psum_rel(pib, "A")
              stx["ud"] = S.val("A")
            pieces.append(piece2)

            def piece3():
              ud = stx["ud"]
              pio, pgo = seq_acq()
              for hh in range(HL):
                  S.emit("tensor", lambda t, h3=hh, cs=csl, p=pio, r0=ro: t.matmul(
                      ps_f[p][r0:r0 + 64, h3 * 128:(h3 + 1) * 128], qt_v[:, h3, cs], Sb_v[:, h3, :],
                      start=(h3 == 0), stop=False),
                      waits=([("A", ud)] + list(pgo)) if hh == 0 else ())
              for hh in range(HL):
                  S.emit("tensor", lambda t, h3=hh, c2=c, p=pio, r0=ro: t.matmul(
                      ps_f[p][r0:r0 + 64, h3 * 128:(h3 + 1) * 128], AT3[0:64, c2, h3, :],
                      u_sb[0:64, h3 * 128:(h3 + 1) * 128],
                      start=False, stop=(h3 == HL - 1)),
                      inc="P" if hh == HL - 1 else None)
              od = S.val("P")
              pid, pgd = seq_acq()
              for hh in range(HL):
                  S.emit("tensor", lambda t, h3=hh, c2=c, p=pid: t.matmul(
                      ps_f[p][:, h3 * 128:(h3 + 1) * 128], kdt_v3[0:64, h3, c2, :],
                      u_sb[0:64, h3 * 128:(h3 + 1) * 128],
                      start=(h3 == 0), stop=(h3 == HL - 1)),
                      waits=(list(pgd)) if hh == 0 else (),
                      inc="P" if hh == HL - 1 else None)
              dsd = S.val("P")
              for hh in range(HL):
                  S.emit("vector", lambda v, h3=hh, c2=c, p=pid: v.scalar_tensor_tensor(
                      Sf_v[:, h3, :], Sf_v[:, h3, :], ebl_v[:, h3, c2:c2 + 1],
                      ps_f[p][:, h3 * 128:(h3 + 1) * 128], AL.mult, AL.add),
                      waits=[("P", dsd), ("A", ebl_done)] if hh == 0 else (), inc="V")
              S.emit("vector", lambda v: v.tensor_copy(Sb_v[:, :, :], Sf_v[:, :, :]), inc="V")
              seq_sv[0] = S.val("V")
              psum_rel(pid, "V")
              for hh in range(HL):
                  S.emit("scalar", lambda s2, h3=hh, p=pio, rs=rsl: s2.activation(
                      kdf_v.rearrange("p a b -> p (a b)")[rs, h3 * 128:(h3 + 1) * 128],
                      ps_f[p][rs, h3 * 128:(h3 + 1) * 128], AF.Square,
                      accum_out=ms_sb[rs, h3:h3 + 1]),
                      waits=[("P", od)] if hh == 0 else (), inc="A")
              S.emit("vector", lambda v, rs=rsl: v.tensor_scalar(
                  ms_sb[rs, :], ms_sb[rs, :], 1.0 / DV, 1e-6, AL.mult, AL.add),
                  waits=[("A", S.val("A"))], inc="V")
              S.emit("vector", lambda v, rs=rsl: v.reciprocal(rr_sb[rs, :], ms_sb[rs, :]), inc="V")
              S.emit("scalar", lambda s2, rs=rsl: s2.activation(rr_sb[rs, :], rr_sb[rs, :], AF.Sqrt),
                     waits=[("V", S.val("V"))], inc="A")
              rd = S.val("A")
              for hh in range(HL):
                  ogsl2 = og_sb[rsl, hh * 128:(hh + 1) * 128]
                  gsl = gate_v[rsl, c // 2, hh * 128:(hh + 1) * 128]
                  S.emit("vector", lambda v, h3=hh, p=pio, o=ogsl2, g2=gsl, rs=rsl: v.scalar_tensor_tensor(
                      o, ps_f[p][rs, h3 * 128:(h3 + 1) * 128], rr_sb[rs, h3:h3 + 1],
                      g2, AL.mult, AL.mult),
                      waits=([("A", rd)] + ([("So", S.val("So"))] if c % 2 == 0 else []))
                      if hh == 0 else (), inc="V")
              psum_rel(pio, "V")
              if c % 2 == 1:
                  S.dma("sync", og_d[t0 + (c - 1) * CH:t0 + (c + 1) * CH, :], og_sb,
                        "So", waits=[("V", S.val("V"))])
                  if c == 7:
                      so_prog[tc] = S.val("So")

            pieces.append(piece3)
            return pieces

        th0 = precompute_pair(0)
        for th in th0:
            th()
        for p in range(1, 4):
            ths = precompute_pair(p)
            pcs = scan_step_pieces(2 * (p - 1)) + scan_step_pieces(2 * (p - 1) + 1)
            slots = {1: 0, 3: 1, 5: 2, 7: 3, 9: 4, 11: 5}
            for i, th in enumerate(ths):
                th()
                if i in slots:
                    pcs[slots[i]]()
        for c in (6, 7):
            for pc in scan_step_pieces(c):
                pc()
    attn_st = S.val("So")
    for g8 in range(8):
        w_ = [("So", so_prog[min(g8, 5)])]
        if g8 < 6:
            for it in range(g8 * 11, min(NIT, (g8 + 1) * 11)):
                S.dma("gpsimd", wdb_d[it, :, :], wd_i[it, :, :], "Wd",
                      waits=w_ if it == g8 * 11 else ())
                S.dma("gpsimd", wgb_d[it, :, :], wg_i[it, :, :], "Wd")
                S.dma("gpsimd", wub_d[it, :, :], wu_i[it, :, :], "Wd")
    S.emit("gpsimd", lambda g: g.collective_compute(
        "AllGather", AL.bypass, replica_groups=[[0, 1, 2, 3], [4, 5, 6, 7]],
        ins=[og_d[0:3 * T // 4, :]], outs=[ogg_d[0]]),
        waits=[("So", so_prog[5])], inc="C")
    cc_og1 = S.val("C")
    emit_gathers(range(6), cc_og1)
    AB.pop(); AFP.pop()
    S.barrier()
    ps_stack.close()
    ps_f.append(nc.ctx.enter_context(nc.psum_tensor("psf6", [128, 512], F32)))
    ps_f.append(nc.ctx.enter_context(nc.psum_tensor("psf7", [128, 512], F32)))
    ps_state["n"] = 8

    AB.push(); AFP.push()
    wo_sb = [AB.alloc(D) for _ in range(16)]
    for ft in range(16):
        S.dma("gpsimd", wo_sb[ft], wo_i[ft * 128:(ft + 1) * 128, :], "Gwo")
    wo_ld = S.val("Gwo")
    og_fm = [AB.alloc(TOK) for _ in range(16)]
    lo_half = [0, 0]

    def emit_ogfm(half):
        lsem = "Lo%d" % half
        c0 = 0 if half == 0 else 768
        c1 = 768 if half == 0 else 1024
        for r in range(4):
            for ftt in range(4):
                S.dma("sync", og_fm[r * 4 + ftt][:, c0:c1],
                      ogsl_d[r * 1024 + c0:r * 1024 + c1,
                             ftt * 128:(ftt + 1) * 128],
                      lsem, waits=[("Sg0", S.val("Sg0")), ("Sg1", S.val("Sg1"))]
                      if (r == 0 and ftt == 0) else (),
                      transpose=True)
        lo_half[half] = (lsem, S.val(lsem))

    emit_ogfm(0)
    S.emit("gpsimd", lambda g: g.collective_compute(
        "AllGather", AL.bypass, replica_groups=[[0, 1, 2, 3], [4, 5, 6, 7]],
        ins=[og_d[3 * T // 4:T, :]], outs=[ogg_d[1]]),
        waits=[("So", attn_st)], inc="C")
    cc_og2 = S.val("C")
    emit_gathers(range(6, 8), cc_og2)
    emit_ogfm(1)

    xo_t = AFP.alloc(D)
    x1_t = AFP.alloc(D)
    r_scr = AFP.alloc(512)
    st2 = AFP.alloc(8)
    xh2 = AB.alloc(D)
    sq2 = AB.alloc(D)
    mu = st2[:, 0:1]; ssq = st2[:, 1:2]; var = st2[:, 2:3]; rstd = st2[:, 3:4]
    for tt in range(8):
        S.dma("sync", xo_t, x_own[tt * 128:(tt + 1) * 128, :], "Lxo",
              waits=[("V", S.val("V")), ("S1", S.val("S1"))])
        xol = S.val("Lxo")
        for dc in range(4):
            pi, pg = psum_acq()
            for ft in range(16):
                S.emit("tensor", lambda t, f=ft, t2=tt, d2=dc, p=pi: t.matmul(
                    ps_f[p][:, :], og_fm[f][:, t2 * 128:(t2 + 1) * 128],
                    wo_sb[f][:, d2 * 512:(d2 + 1) * 512],
                    start=(f == 0), stop=(f == 15)),
                    waits=([lo_half[0 if tt < 6 else 1], ("Gwo", wo_ld)] + list(pg))
                    if ft == 0 else (),
                    inc="P" if ft == 15 else None)
            S.emit("vector", lambda v, p=pi, d2=dc: v.tensor_tensor(
                r_scr, ps_f[p][:, :], ga_bc[:, d2 * 512:(d2 + 1) * 512], AL.mult),
                waits=[("P", S.val("P")), ("Lb", S.val("Lb"))], inc="V")
            psum_rel(pi, "V")
            S.emit("vector", lambda v, d2=dc: v.tensor_tensor(
                x1_t[:, d2 * 512:(d2 + 1) * 512], r_scr,
                xo_t[:, d2 * 512:(d2 + 1) * 512], AL.add),
                waits=[("Lxo", xol)], inc="V")
        S.dma("sync", x1_d[tt * 128:(tt + 1) * 128, :], x1_t, "S1",
              waits=[("V", S.val("V"))])
        S.emit("vector", lambda v, m=mu: v.reduce_sum(m, x1_t, AX.X), inc="V")
        S.emit("scalar", lambda s, q=ssq: s.activation(sq2, x1_t, AF.Square, accum_out=q),
               waits=[("V", S.val("V"))], inc="A")
        S.emit("vector", lambda v, m=mu: v.tensor_single_scalar(m, m, 1.0 / D, AL.mult),
               inc="V")
        S.emit("vector", lambda v, m=mu, va=var: v.tensor_tensor(va, m, m, AL.mult),
               inc="V")
        S.emit("vector", lambda v, q=ssq, va=var: v.scalar_tensor_tensor(
            va, q, 1.0 / D, va, AL.mult, AL.subtract),
               waits=[("A", S.val("A"))], inc="V")
        S.emit("vector", lambda v, va=var: v.tensor_single_scalar(va, va, 1e-5, AL.add),
               inc="V")
        S.emit("vector", lambda v, va=var, r=rstd: v.reciprocal(r, va), inc="V")
        S.emit("scalar", lambda s, r=rstd: s.activation(r, r, AF.Sqrt),
               waits=[("V", S.val("V"))], inc="A")
        S.emit("vector", lambda v, m=mu, r=rstd: v.tensor_scalar(
            xh2, x1_t, m, r, AL.subtract, AL.mult),
            waits=[("A", S.val("A")), ("S2", S.val("S2"))], inc="V")
        S.dma("sync", xhat_d[tt * 128:(tt + 1) * 128, :], xh2, "S2",
              waits=[("V", S.val("V"))])
    x1_done = S.val("S1")
    ln2_done = S.val("S2")
    AB.pop(); AFP.pop()
    S.barrier()

    AB.push(); AFP.push()
    h2_v = v2(AB.alloc(DT * TOK), DT, TOK)
    for ft in range(DT):
        S.dma("sync", h2_v[:, ft, :], xhat_d[0:TOK, ft * 128:(ft + 1) * 128],
              "Lh2", waits=[("S2", ln2_done)] if ft == 0 else (), transpose=True)
    ldh2 = S.val("Lh2")
    for ft in range(DT):
        S.emit("vector", lambda v, f=ft: v.tensor_scalar(
            h2_v[:, f, :], h2_v[:, f, :], cb1_c[:, f:f + 1], sb_c[:, f:f + 1],
            AL.mult, AL.add), waits=[("Lh2", ldh2)] if ft == 0 else (), inc="V")
    h2_done = S.val("V")
    if DEBUG:
        for ft in range(DT):
            S.dma("sync", dbg_h2[ft * 128:(ft + 1) * 128, :], h2_v[:, ft, :],
                  "Sd", waits=[("V", h2_done)])

    wgb = [v2(AB.alloc(DT * 128), DT, 128) for _ in range(2)]
    wub = [v2(AB.alloc(DT * 128), DT, 128) for _ in range(2)]
    wguard = [0, 0]
    hm_q = [AB.alloc(TOK) for _ in range(4)]
    hs_q = AB.alloc(TOK)
    m_q = [AFP.alloc(TOK) for _ in range(4)]
    cc6 = [AB.alloc(TOK) for _ in range(6)]
    um_ = AB.alloc(TOK)
    msk = AB.alloc(TOK)
    silu_t = AB.alloc(512)

    for j in range(16):
        for q4 in range(4):
            it = q4 * 16 + j
            bi = it % 2
            gf = "Gf%d" % bi
            uf = "Uf%d" % bi
            S.dma("gpsimd", wgb[bi][:, :, :],
                  wgb_d[it, :, :].rearrange("p (a b) -> p a b", a=DT, b=128),
                  gf, waits=[("P", wguard[bi])])
            S.dma("gpsimd", wub[bi][:, :, :],
                  wub_d[it, :, :].rearrange("p (a b) -> p a b", a=DT, b=128),
                  uf, waits=[("P", wguard[bi])])
            wldv = S.val(gf)
            uldv = S.val(uf)
            for th in range(2):
                pi, pg = psum_acq()
                for d in range(DT):
                    S.emit("tensor", lambda t, d2=d, b2=bi, t2=th, p=pi: t.matmul(
                        ps_f[p][:, :], wgb[b2][:, d2, :], h2_v[:, d2, t2 * 512:(t2 + 1) * 512],
                        start=(d2 == 0), stop=(d2 == DT - 1)),
                        waits=([(gf, wldv), ("V", h2_done)] + list(pg)) if d == 0 else (),
                        inc="P" if d == DT - 1 else None)
                pu, pgu = psum_acq()
                for d in range(DT):
                    S.emit("tensor", lambda t, d2=d, b2=bi, t2=th, p=pu: t.matmul(
                        ps_f[p][:, :], wub[b2][:, d2, :], h2_v[:, d2, t2 * 512:(t2 + 1) * 512],
                        start=(d2 == 0), stop=(d2 == DT - 1)),
                        waits=([(uf, uldv)] + list(pgu)) if d == 0 else (),
                        inc="P" if d == DT - 1 else None)
                gud = S.val("P")
                S.emit("scalar", lambda s, p=pi: s.activation(silu_t, ps_f[p][:, :], AF.Silu),
                       waits=[("P", gud), ("V", S.val("V"))], inc="A")
                S.emit("vector", lambda v, q=q4, t2=th, p=pu: v.tensor_tensor(
                    hm_q[q][:, t2 * 512:(t2 + 1) * 512], silu_t, ps_f[p][:, :], AL.mult),
                    waits=[("A", S.val("A"))], inc="V")
                psum_rel(pi, "A"); psum_rel(pu, "V")
            wguard[bi] = S.val("P")
        hmd = S.val("V")
        for q4 in range(4):
            S.emit("scalar", lambda s, q=q4: s.activation(m_q[q], hm_q[q], AF.Square),
                   waits=[("V", hmd)] if q4 == 0 else (), inc="A")
        sqd = S.val("A")
        pairs = [(0, 1), (0, 2), (0, 3), (1, 2), (1, 3), (2, 3)]
        for i6, (a, b2) in enumerate(pairs):
            S.emit("vector", lambda v, i=i6, a2=a, b3=b2: v.tensor_tensor(
                cc6[i], m_q[a2], m_q[b3], AL.is_gt),
                waits=[("A", sqd)] if i6 == 0 else (), inc="V")
        combos = [
            (0, 0, 1, AL.add, 2, AL.add, 2.0, AL.is_ge),
            (1, 3, 4, AL.add, 0, AL.subtract, 1.0, AL.is_ge),
            (2, 5, 1, AL.subtract, 3, AL.subtract, 0.0, AL.is_ge),
            (3, 2, 4, AL.add, 5, AL.add, 1.0, AL.is_le),
        ]
        for q4, i0, i1, op1, i2, op2, thr, opm in combos:
            S.emit("vector", lambda v, a=i0, b3=i1, o=op1: v.tensor_tensor(
                um_, cc6[a], cc6[b3], o),
                waits=[("Sh", S.val("Sh"))] if q4 == 0 else (), inc="V")
            S.emit("vector", lambda v, a=i2, o=op2: v.tensor_tensor(
                um_, um_, cc6[a], o), inc="V")
            S.emit("vector", lambda v, t2=thr, o=opm: v.tensor_single_scalar(
                msk, um_, t2, o), inc="V")
            S.emit("vector", lambda v, q=q4: v.tensor_tensor(
                hs_q, hm_q[q], msk, AL.mult),
                waits=[("Sh", S.val("Sh"))], inc="V")
            it = q4 * 16 + j
            S.dma("sync", hs_d[it, :, :], hs_q, "Sh", waits=[("V", S.val("V"))])
    hs_done = S.val("Sh")
    wd_done = S.val("Wd")
    AB.pop(); AFP.pop()
    S.barrier()

    AB.push(); AFP.push()
    hsr = v2(AB.alloc(NIT * TOK), NIT, TOK)
    wdb4 = [v2(AB.alloc(4 * 512), 4, 512) for _ in range(2)]
    dguard = [0, 0]
    o_t = AFP.alloc(512)
    x1e = AFP.alloc(512)
    ghv = []
    for g8 in range(8):
        gh = "Gh%d" % g8
        S.dma("gpsimd", hsr[:, g8 * 8:(g8 + 1) * 8, :],
              hs_d[g8 * 8:(g8 + 1) * 8, :, :].rearrange("a p t -> p a t"),
              gh, waits=[("Sh", hs_done), ("Wd", wd_done)] if g8 == 0 else ())
        ghv.append((gh, S.val(gh)))
    pis = []
    pgs = []
    for u in range(8):
        pi, pg = psum_acq()
        pis.append(pi)
        pgs.append(list(pg))
    pre_hld = {}

    def emit_wd_load(dq2, c):
        it0 = c * 4
        bi = c % 2
        eng = "sync" if bi == 0 else "gpsimd"
        gd = "Gd%d" % bi
        S.dma(eng, wdb4[bi][:, :, :],
              wdb_d[it0:it0 + 4, :, dq2 * 512:(dq2 + 1) * 512].rearrange("a p d -> p a d"),
              gd, waits=[("P", dguard[bi])])
        pre_hld[(dq2, c)] = S.val(gd)

    for dq in range(4):
        dco = dq * 512
        for c in range(16):
            it0 = c * 4
            bi = c % 2
            gd = "Gd%d" % bi
            if (dq, c) not in pre_hld:
                emit_wd_load(dq, c)
            hld = pre_hld[(dq, c)]
            for ii in range(4):
                it = it0 + ii
                for u in range(8):
                    pi = pis[u]
                    w_ = []
                    if ii == 0 and u == 0:
                        w_ += [(gd, hld), ghv[min(7, (it0 + 3) // 8)]]
                    if it == 0:
                        w_ += pgs[u]
                    S.emit("tensor", lambda t, i2=it, b2=bi, ii2=ii, u2=u, p=pi: t.matmul(
                        ps_f[p][:, :], hsr[:, i2, u2 * 128:u2 * 128 + 128],
                        wdb4[b2][:, ii2, :],
                        start=(i2 == 0), stop=(i2 == NIT - 1)),
                        waits=w_, inc="P" if (ii == 3 and u == 7) else None)
            dguard[bi] = S.val("P")
        dnd = S.val("P")
        if dq < 3:
            emit_wd_load(dq + 1, 0)
            emit_wd_load(dq + 1, 1)
        pgs = []
        new_pis = []
        for u in range(8):
            rows = slice(u * 128, u * 128 + 128)
            S.dma("gpsimd", x1e, x1_d[rows, dco:dco + 512], "Le",
                  waits=[("V", S.val("V"))])
            x1l = S.val("Le")
            pi = pis[u]
            S.emit("vector", lambda v, p=pi, d2=dco: v.tensor_tensor(
                o_t, ps_f[p][:, :], gb_bc[:, d2:d2 + 512], AL.mult),
                waits=[("P", dnd), ("Le", x1l), ("So2", S.val("So2")),
                       ("Lb", S.val("Lb"))], inc="V")
            S.emit("vector", lambda v: v.tensor_tensor(o_t, o_t, x1e, AL.add), inc="V")
            S.dma("sync", out_s[rows, dco:dco + 512], o_t, "So2",
                  waits=[("V", S.val("V"))])
            psum_rel(pi, "V")
            npi, npg = psum_acq()
            new_pis.append(npi)
            pgs.append(list(npg))
        pis = new_pis
    final_st = S.val("So2")
    S.emit("sync", lambda sy: sy.nop(), waits=[("So2", final_st)])
    AB.pop(); AFP.pop()

    return nc, S, dbg


def realize(nc, S):
    with contextlib.ExitStack() as es:
        for name in S.cnt:
            S.sems[name] = es.enter_context(nc.semaphore(name))
        with nc.Block() as block:
            for eng in S.ENGINES:
                if not S.ops[eng]:
                    continue
                getattr(block, eng)(lambda e, _eng=eng: [cl(e) for cl in S.ops[_eng]])
    return nc


_CACHE = {}


def get_nc():
    if "nc" not in _CACHE:
        nc, S, dbg = build_program()
        realize(nc, S)
        _CACHE["nc"] = nc
        _CACHE["dbg"] = dbg
    return _CACHE["nc"]


def _prep_inputs(inputs):
    f32 = np.float32
    x = np.ascontiguousarray(np.asarray(inputs["L0_attn_x"], f32))
    wq = np.asarray(inputs["L0_attn_Wq"], f32)
    wk = np.asarray(inputs["L0_attn_Wk"], f32)
    wv = np.asarray(inputs["L0_attn_Wv"], f32)
    wo = np.ascontiguousarray(np.asarray(inputs["L0_attn_Wo"], f32))
    wad = np.ascontiguousarray(np.asarray(inputs["L0_attn_Wad"], f32))
    wau = np.asarray(inputs["L0_attn_Wau"], f32)
    wb = np.asarray(inputs["L0_attn_Wb"], f32)
    wgd = np.ascontiguousarray(np.asarray(inputs["L0_attn_Wgd"], f32))
    wgu = np.asarray(inputs["L0_attn_Wgu"], f32)
    cq = np.asarray(inputs["L0_attn_conv_q_w"], f32)
    ck = np.asarray(inputs["L0_attn_conv_k_w"], f32)
    cv = np.asarray(inputs["L0_attn_conv_v_w"], f32)
    rms = np.asarray(inputs["L0_attn_rms_w"], f32)
    wg = np.asarray(inputs["L0_ffn_w_gate"], f32)
    wu = np.asarray(inputs["L0_ffn_w_up"], f32)
    wd = np.asarray(inputs["L0_ffn_w_down"], f32)
    cvec = np.asarray(inputs["L0_c"], f32)
    wmod = np.asarray(inputs["L0_wmod"], f32)

    wg_p = wg.reshape(D, FI // 4, 4).transpose(0, 2, 1).reshape(D, FI)
    wu_p = wu.reshape(D, FI // 4, 4).transpose(0, 2, 1).reshape(D, FI)
    wd_p = wd.reshape(FI // 4, 4, D).transpose(1, 0, 2).reshape(FI, D)
    wg_t = np.ascontiguousarray(
        wg_p.reshape(DT, 128, NIT, 128).transpose(2, 1, 0, 3).reshape(NIT, 128, DT * 128))
    wu_t = np.ascontiguousarray(
        wu_p.reshape(DT, 128, NIT, 128).transpose(2, 1, 0, 3).reshape(NIT, 128, DT * 128))
    wd_t = np.ascontiguousarray(wd_p.reshape(NIT, 128, D))

    ident = np.eye(128, dtype=f32)
    masks = np.zeros((64, 192), f32)
    masks[:, 0:64] = np.tril(np.ones((64, 64), f32), -1)
    masks[:, 64:128] = np.triu(np.ones((64, 64), f32))
    masks[:, 128:192] = np.eye(64, dtype=f32)
    mrep = np.zeros((64, 1536), f32)
    mrep[:, 0:512] = np.tile(-np.tril(np.ones((64, 64), f32), -1), (1, 8))
    mrep[:, 512:1024] = np.tile(np.triu(np.ones((64, 64), f32)), (1, 8))
    mrep[:, 1024:1536] = np.tile(np.eye(64, dtype=f32), (1, 8))
    ones1 = np.ones((1, 128), f32)
    rms_bc = np.ascontiguousarray(np.broadcast_to(np.tile(rms, HL), (128, FQ)))
    cT = np.ascontiguousarray(cvec.T)

    def diag_tiles(cw, lo):
        out = np.zeros((HL, CKW, 128, 128), f32)
        for hh in range(HL):
            for j in range(CKW):
                np.fill_diagonal(out[hh, j], cw[lo + hh * 128: lo + (hh + 1) * 128, j])
        return out

    in_maps = []
    for c in range(8):
        b = c // 4
        qd = c % 4
        fs = slice(qd * FQ, (qd + 1) * FQ)
        hsl = slice(qd * HL, (qd + 1) * HL)
        own_tiles = [k * 4 + qd for k in range(8)]
        ogidx = np.zeros((128, 32), np.int32)
        for jj in range(32):
            k, r = jj // 4, jj % 4
            g = own_tiles[k]
            if k < 6:
                ogidx[:, jj] = r * 3072 + g * 128 + np.arange(128)
            else:
                ogidx[:, jj] = r * 1024 + (g * 128 - 3072) + np.arange(128)
        m = {
            "x_b": x[b],
            "x_own": np.ascontiguousarray(
                x[b].reshape(32, 128, D)[own_tiles].reshape(TOK, D)),
            "wq": np.ascontiguousarray(wq[:, fs]),
            "wk": np.ascontiguousarray(wk[:, fs]),
            "wv": np.ascontiguousarray(wv[:, fs]),
            "cdq": diag_tiles(cq, qd * FQ),
            "cdk": diag_tiles(ck, qd * FQ),
            "cdv": diag_tiles(cv, qd * FQ),
            "wad": wad, "wgd": wgd,
            "wau": np.ascontiguousarray(wau[:, fs]),
            "wgu": np.ascontiguousarray(wgu[:, fs]),
            "wb": np.ascontiguousarray(wb[:, hsl]),
            "rms_bc": rms_bc,
            "wo": wo,
            "wmod_s": np.ascontiguousarray(wmod[:, c * 1536:(c + 1) * 1536]),
            "c_T": cT,
            "wg_t": wg_t, "wu_t": wu_t, "wd_t": wd_t,
            "ident_in": ident, "masks_in": masks, "mrep_in": mrep, "ones1_in": ones1,
            "bsel128": np.full((128, 1), float(b), f32),
            "ogidx": ogidx,
        }
        in_maps.append(m)
    return in_maps


def kernel(**inputs):
    nc = get_nc()
    in_maps = _prep_inputs(inputs)
    res = run_bass_kernel_spmd(nc, in_maps, core_ids=list(range(8)))
    out = np.empty((B, T, D), np.float32)
    for c in range(8):
        b, qd = c // 4, c % 4
        o = res.results[c]["out_s"].reshape(8, 128, D)
        for k in range(8):
            t = k * 4 + qd
            out[b, t * 128:(t + 1) * 128] = o[k]
    return out


# version 14 (final)
# speedup vs baseline: 1.1731x; 1.0076x over previous
import contextlib
import numpy as np
import ml_dtypes
import concourse.bass as bass
import concourse.mybir as mybir
from concourse.bass import IndirectOffsetOnAxis
from concourse.bass_utils import run_bass_kernel_spmd

F32 = mybir.dt.float32
BF16 = mybir.dt.bfloat16
I32 = mybir.dt.int32
AF = mybir.ActivationFunctionType
AL = mybir.AluOpType
AX = mybir.AxisListType

B, T, D = 2, 4096, 2048
H, DK, DV, CKW, CH = 16, 128, 128, 4, 64
HL, FQ, TOK = 4, 512, 1024
TC, NTC = 512, 8
NCH_C = TC // CH
DT = D // 128
FI = 8192
NIT = FI // 128

DEBUG = False


class Sched:
    ENGINES = ("tensor", "vector", "scalar", "gpsimd", "sync")

    def __init__(self):
        self.ops = {e: [] for e in self.ENGINES}
        self.cnt = {}
        self.sems = {}

    def _sem(self, name):
        self.cnt.setdefault(name, 0)
        return name

    def val(self, name):
        return self.cnt.get(name, 0)

    def allv(self, *names):
        return [(n, self.val(n)) for n in names]

    def emit(self, eng, fn, waits=(), inc=None, inc_n=1):
        waits = list(waits)
        if eng == "vector":
            waits.append(("V", self.val("V")))
            if inc is None:
                inc = "V"
        elif eng == "scalar":
            waits.append(("A", self.val("A")))
            if inc is None:
                inc = "A"
        if inc is not None:
            self._sem(inc)
            self.cnt[inc] += inc_n
        waits = tuple((self._sem(s), v) for s, v in waits if v > 0)
        incn = (inc, inc_n) if inc else None

        def closure(engine, _fn=fn, _w=waits, _i=incn):
            for s, v in _w:
                engine.wait_ge(self.sems[s], v)
            ins = _fn(engine)
            if _i is not None:
                ins.then_inc(self.sems[_i[0]], _i[1])
        self.ops[eng].append(closure)

    def barrier(self):
        self._sem("B")
        n0 = self.val("B")
        pre = [(k, v) for k, v in self.cnt.items() if k != "B" and v > 0]
        for eng in self.ENGINES:
            self.emit(eng, lambda e: e.nop(), inc="B",
                      waits=pre if eng == "sync" else ())
        tgt = n0 + len(self.ENGINES)
        assert self.cnt["B"] == tgt
        for eng in self.ENGINES:
            self.emit(eng, lambda e: e.nop(), waits=[("B", tgt)])

    def dma(self, eng, out, in_, sem="L", waits=(), transpose=False):
        if transpose:
            self.emit(eng, lambda e, o=out, i=in_: e.dma_start(out=o, in_=i, transpose=True),
                      waits=waits, inc=sem, inc_n=16)
        else:
            self.emit(eng, lambda e, o=out, i=in_: e.dma_start(out=o, in_=i),
                      waits=waits, inc=sem, inc_n=16)


class Arena:
    def __init__(self, nc, name, cols, dtype):
        self.t = nc.ctx.enter_context(nc.sbuf_tensor(name, [128, cols], dtype))
        self.cols = cols
        self.off = 0
        self.marks = []

    def alloc(self, n, align=8):
        o = (self.off + align - 1) // align * align
        assert o + n <= self.cols, f"arena overflow {o}+{n}>{self.cols}"
        self.off = o + n
        return self.t[:, o:o + n]

    def push(self):
        self.marks.append(self.off)

    def pop(self):
        self.off = self.marks.pop()


def v2(t, a, b2):
    return t.rearrange("p (a b) -> p a b", a=a, b=b2)


def v3(t, a, b2, c3):
    return t.rearrange("p (a b c) -> p a b c", a=a, b=b2, c=c3)


def build_program():
    nc = bass.Bass("TRN2", target_bir_lowering=False)
    S = Sched()

    def din(name, shape, dt=F32):
        return nc.dram_tensor(name, shape, dt, kind="ExternalInput").ap()

    x_b = din("x_b", [T, D])
    x_own = din("x_own", [TOK, D])
    wq_i = din("wq", [D, FQ]); wk_i = din("wk", [D, FQ]); wv_i = din("wv", [D, FQ])
    cd_i = {n: din("cd" + n, [HL, CKW, 128, 128]) for n in "qkv"}
    wad_i = din("wad", [D, DK]); wau_i = din("wau", [DK, FQ])
    wgd_i = din("wgd", [D, DV]); wgu_i = din("wgu", [DV, FQ])
    wb_i = din("wb", [D, HL])
    rms_i = din("rms_bc", [128, FQ])
    wo_i = din("wo", [D, D])
    wmod_i = din("wmod_s", [D, 1536])
    cT_i = din("c_T", [D, B])
    wg_i = din("wg_t", [NIT, 128, DT * 128])
    wu_i = din("wu_t", [NIT, 128, DT * 128])
    wd_i = din("wd_t", [NIT, 128, D])
    ident_i = din("ident_in", [128, 128])
    masks_i = din("masks_in", [64, 192])
    mrep_i = din("mrep_in", [64, 1536])
    ones1_i = din("ones1_in", [1, 128])
    bsel_i = din("bsel128", [128, 1])
    ogidx_i = din("ogidx", [128, 32], dt=I32)
    out_s = nc.dram_tensor("out_s", [TOK, D], F32, kind="ExternalOutput").ap()

    xhat_d = nc.dram_tensor("xhat_d", [T, D], BF16).ap()
    modp_d = nc.dram_tensor("modp_d", [1536, B], F32).ap()
    modf_d = nc.dram_tensor("modf_d", [12288, B], F32).ap()
    og_d = nc.dram_tensor("og_d", [T, FQ], BF16).ap()
    ogg_d = [nc.dram_tensor("ogg_d1", [3 * T, FQ], BF16).ap(),
             nc.dram_tensor("ogg_d2", [T, FQ], BF16).ap()]
    ogsl_d = nc.dram_tensor("ogsl_d", [4 * TOK, FQ], BF16).ap()
    x1_d = nc.dram_tensor("x1_d", [TOK, D], F32).ap()
    grow_d = nc.dram_tensor("grow_d", [2, D], F32).ap()
    hs_d = nc.dram_tensor("hs_d", [NIT, 128, TOK], BF16).ap()
    wdb_d = nc.dram_tensor("wdb_d", [NIT, 128, D], BF16).ap()
    wgb_d = nc.dram_tensor("wgb_d", [NIT, 128, DT * 128], BF16).ap()
    wub_d = nc.dram_tensor("wub_d", [NIT, 128, DT * 128], BF16).ap()

    dbg = {}

    def ddbg(name, shape, dt):
        if DEBUG:
            dbg[name] = nc.dram_tensor(name, shape, dt, kind="ExternalOutput").ap()
            return dbg[name]
        return None
    dbg_h = ddbg("dbg_h", [D, T], BF16)
    dbg_qn = ddbg("dbg_qn", [FQ, T], BF16)
    dbg_kn = ddbg("dbg_kn", [FQ, T], BF16)
    dbg_sp = ddbg("dbg_sp", [FQ, T], F32)
    dbg_h2 = ddbg("dbg_h2", [D, TOK], BF16)

    AB = Arena(nc, "ab", 87136, BF16)
    AFP = Arena(nc, "af", 9608, F32)
    ogidx_s = nc.ctx.enter_context(nc.sbuf_tensor("ogidx_s", [128, 32], I32))

    ps_stack = contextlib.ExitStack()
    ps_big = nc.ctx.enter_context(nc.psum_tensor("psbig", [128, 6 * 512], F32))
    ps_f = [ps_big[:, i * 512:(i + 1) * 512] for i in range(6)]
    ps_b = [ps_stack.enter_context(nc.psum_tensor(f"psb{i}", [64, 512], BF16)) for i in range(2)]
    ps_state = {"i": 0, "guard": [()] * 8, "bi": 0, "bguard": [()] * 2, "n": 6}

    def psum_acq():
        i = ps_state["i"] % ps_state["n"]
        ps_state["i"] += 1
        return i, ps_state["guard"][i]

    def psum_rel(i, sem):
        ps_state["guard"][i] = ((sem, S.val(sem)),)

    def psb_acq():
        i = ps_state["bi"] % 2
        ps_state["bi"] += 1
        return i, ps_state["bguard"][i]

    def psb_rel(i, sem):
        ps_state["bguard"][i] = ((sem, S.val(sem)),)

    def psum_acq_set(idxs):
        g = []
        for i in idxs:
            g.extend(ps_state["guard"][i])
        return tuple(g)

    def psum_rel_set(idxs, sem):
        for i in idxs:
            ps_state["guard"][i] = ((sem, S.val(sem)),)

    ident = AB.alloc(128)
    mask_sl = AB.alloc(64)
    mask_ui = AB.alloc(64)
    eye64 = AB.alloc(64)
    ones_bf = AB.alloc(128)
    nmsk = AB.alloc(512)
    umsk = AB.alloc(512)
    erep = AB.alloc(512)
    ones1_f = AFP.alloc(128)
    identF = AFP.alloc(128)
    rms_sb = AFP.alloc(FQ)
    sa_c = AFP.alloc(DT); ca1_c = AFP.alloc(DT)
    sb_c = AFP.alloc(DT); cb1_c = AFP.alloc(DT)
    ga_bc = AFP.alloc(D); gb_bc = AFP.alloc(D)
    bsel_s = AFP.alloc(1)

    S.dma("gpsimd", ident, ident_i[:, :], "Gc")
    S.dma("gpsimd", mask_sl[0:64, :], masks_i[:, 0:64], "Gc")
    S.dma("gpsimd", mask_ui[0:64, :], masks_i[:, 64:128], "Gc")
    S.dma("gpsimd", eye64[0:64, :], masks_i[:, 128:192], "Gc")
    S.dma("gpsimd", nmsk[0:64, :], mrep_i[:, 0:512], "Gc")
    S.dma("gpsimd", umsk[0:64, :], mrep_i[:, 512:1024], "Gc")
    S.dma("gpsimd", erep[0:64, :], mrep_i[:, 1024:1536], "Gc")
    S.emit("vector", lambda v: v.memset(ones_bf, 1.0), inc="V")
    S.dma("sync", ones1_f[0:1, :], ones1_i[:, :], "Lc")
    S.dma("sync", identF, ident_i[:, :], "Lc")
    S.dma("sync", rms_sb, rms_i[:, :], "Lc")
    S.dma("sync", bsel_s, bsel_i[:, :], "Lc")
    S.dma("sync", ogidx_s[:, :], ogidx_i[:, :], "Lc")

    AB.push(); AFP.push()
    csil_v = v2(AB.alloc(DT * B), DT, B)
    modp_v = v2(AFP.alloc(12 * B), 12, B)
    S.dma("gpsimd", csil_v[:, :, :], cT_i.rearrange("(a p) b -> p a b", p=128), "Gc")
    S.emit("scalar", lambda s: s.activation(csil_v[:, :, :], csil_v[:, :, :], AF.Silu),
           waits=[("Gc", S.val("Gc"))], inc="A")
    csil_done = S.val("A")

    wmod_v = [v2(AB.alloc(128 * DT), DT, 128) for _ in range(2)]
    for jt in range(12):
        buf = wmod_v[jt % 2]
        gm = "Gm%d" % (jt % 2)
        wpe = [("P", S.val("P"))] if jt >= 2 else []
        S.dma("gpsimd", buf[:, :, :],
              wmod_i[:, jt * 128:(jt + 1) * 128].rearrange("(a p) j -> p a j", p=128),
              gm, waits=wpe)
        pi, pg = psum_acq()
        ldv = S.val(gm)
        for d in range(DT):
            S.emit("tensor", lambda t, b2=buf, d2=d, p=pi: t.matmul(
                ps_f[p][0:128, 0:B], b2[:, d2, :], csil_v[:, d2, :],
                start=(d2 == 0), stop=(d2 == DT - 1)),
                waits=([(gm, ldv), ("A", csil_done)] + list(pg)) if d == 0 else (),
                inc="P" if d == DT - 1 else None)
        S.emit("scalar", lambda s, p=pi, j=jt: s.copy(modp_v[:, j, :], ps_f[p][0:128, 0:B]),
               waits=[("P", S.val("P"))], inc="A")
        psum_rel(pi, "A")
    S.dma("sync", modp_d.rearrange("(a p) b -> p a b", p=128), modp_v[:, :, :], "Sm",
          waits=[("A", S.val("A"))])
    S.emit("gpsimd", lambda g: g.collective_compute(
        "AllGather", AL.bypass, replica_groups=[list(range(8))],
        ins=[modp_d], outs=[modf_d]),
        waits=[("Sm", S.val("Sm"))], inc="C")
    cc_mod = S.val("C")

    m2 = {}
    for grp, nm in ((0, "sa"), (1, "ca"), (2, "ga"), (3, "sb"), (4, "cb"), (5, "gb")):
        m2[nm] = v2(AFP.alloc(DT * B), DT, B)
        S.dma("sync", m2[nm],
              modf_d[grp * D:(grp + 1) * D, :].rearrange("(a p) b -> p a b", p=128),
              "Lm", waits=[("C", cc_mod)])
    mod_ld = S.val("Lm")
    ga_c = AFP.alloc(DT); gb_c = AFP.alloc(DT)

    def select2(dst, key):
        src = m2[key]
        S.emit("vector", lambda v: v.tensor_tensor(dst, src[:, :, 1], src[:, :, 0], AL.subtract),
               waits=[("Lm", mod_ld)], inc="V")
        S.emit("vector", lambda v: v.scalar_tensor_tensor(
            dst, dst, bsel_s, src[:, :, 0], AL.mult, AL.add),
            waits=[("Lc", S.val("Lc"))], inc="V")

    select2(sa_c, "sa"); select2(ca1_c, "ca"); select2(sb_c, "sb")
    select2(cb1_c, "cb"); select2(ga_c, "ga"); select2(gb_c, "gb")
    S.emit("vector", lambda v: v.tensor_single_scalar(ca1_c, ca1_c, 1.0, AL.add), inc="V")
    S.emit("vector", lambda v: v.tensor_single_scalar(cb1_c, cb1_c, 1.0, AL.add), inc="V")

    sgn = AFP.alloc(DT)
    growsb = AFP.alloc(D)
    for gi, (gc, bcd) in enumerate(((ga_c, ga_bc), (gb_c, gb_bc))):
        S.emit("vector", lambda v, g=gc: v.tensor_single_scalar(sgn, g, 0.0, AL.is_ge),
               waits=[("P", S.val("P"))], inc="V")
        S.emit("vector", lambda v: v.tensor_scalar(sgn, sgn, 2.0, -1.0, AL.mult, AL.add), inc="V")
        S.emit("scalar", lambda s, g=gc: s.activation(g, g, AF.Abs),
               waits=[("V", S.val("V"))], inc="A")
        S.emit("vector", lambda v, g=gc: v.tensor_single_scalar(g, g, 0.25, AL.max),
               waits=[("A", S.val("A"))], inc="V")
        S.emit("vector", lambda v, g=gc: v.tensor_tensor(g, g, sgn, AL.mult), inc="V")
        gd = S.val("V")
        S.emit("sync", lambda e, g=gc, gi2=gi: e.dma_start(
            out=grow_d[gi2:gi2 + 1, :].rearrange("one (a p) -> p a one", p=128),
            in_=g.rearrange("p (a one) -> p a one", one=1),
            allow_slow_non_contiguous=True),
            waits=[("V", gd)], inc="Sm", inc_n=16)
        S.dma("sync", bcd.rearrange("p (one d) -> p one d", one=1),
              grow_d[gi:gi + 1, :].partition_broadcast(128), "Lb",
              waits=[("Sm", S.val("Sm"))])
    AB.push(); AFP.push()
    NLB = 4
    x_t4 = [[AB.alloc(D) for _ in range(NLB)] for _ in range(2)]
    xh_t4 = [[AB.alloc(D) for _ in range(NLB)] for _ in range(2)]
    sq_scr = AB.alloc(D)
    stat4 = AFP.alloc(4 * NLB)
    ln_guard = [(0, 0), (0, 0)]
    ln_ch = []

    for ib in range(T // 128 // NLB):
        bi = ib % 2
        lx = "Gx%d" % bi
        sx = "Sx%d" % bi
        gv, gst = ln_guard[bi]
        for k in range(NLB):
            i = ib * NLB + k
            S.dma("gpsimd", x_t4[bi][k], x_b[i * 128:(i + 1) * 128, :], lx,
                  waits=[("V", gv), (sx, gst)] if k == 0 else ())
        ldv = S.val(lx)
        mu4 = stat4[:, 0:NLB]; ssq4 = stat4[:, NLB:2 * NLB]
        var4 = stat4[:, 2 * NLB:3 * NLB]; rstd4 = stat4[:, 3 * NLB:4 * NLB]
        for k in range(NLB):
            S.emit("vector", lambda v, x=x_t4[bi][k], k2=k: v.reduce_sum(
                stat4[:, k2:k2 + 1], x, AX.X),
                waits=[(lx, ldv), ("A", S.val("A"))] if k == 0 else (), inc="V")
        for k in range(NLB):
            S.emit("scalar", lambda st, x=x_t4[bi][k], k2=k: st.activation(
                sq_scr, x, AF.Square, accum_out=stat4[:, NLB + k2:NLB + k2 + 1]),
                waits=[(lx, ldv), ("V", S.val("V"))] if k == 0 else (), inc="A")
        S.emit("vector", lambda v: v.tensor_single_scalar(mu4, mu4, 1.0 / D, AL.mult), inc="V")
        S.emit("vector", lambda v: v.tensor_tensor(var4, mu4, mu4, AL.mult), inc="V")
        S.emit("vector", lambda v: v.scalar_tensor_tensor(var4, ssq4, 1.0 / D, var4,
                                                          AL.mult, AL.subtract),
               waits=[("A", S.val("A"))], inc="V")
        S.emit("vector", lambda v: v.tensor_single_scalar(var4, var4, 1e-5, AL.add), inc="V")
        S.emit("vector", lambda v: v.reciprocal(rstd4, var4), inc="V")
        S.emit("scalar", lambda st: st.activation(rstd4, rstd4, AF.Sqrt),
               waits=[("V", S.val("V"))], inc="A")
        for k in range(NLB):
            S.emit("vector", lambda v, x=x_t4[bi][k], o=xh_t4[bi][k], k2=k: v.tensor_scalar(
                o, x, stat4[:, k2:k2 + 1], stat4[:, 3 * NLB + k2:3 * NLB + k2 + 1],
                AL.subtract, AL.mult),
                waits=[("A", S.val("A"))] if k == 0 else (), inc="V")
        xv = S.val("V")
        for k in range(NLB):
            i = ib * NLB + k
            S.dma("sync", xhat_d[i * 128:(i + 1) * 128, :], xh_t4[bi][k], sx,
                  waits=[("V", xv)] if k == 0 else ())
        ln_guard[bi] = (xv, S.val(sx))
        ln_ch.append((sx, S.val(sx)))

    ln_done = [("Sx0", S.val("Sx0")), ("Sx1", S.val("Sx1"))]
    AB.pop(); AFP.pop()
    AB.pop(); AFP.pop()
    S.barrier()

    AB.push(); AFP.push()
    wq_v = v2(AB.alloc(FQ * DT), DT, FQ)
    wk_v = v2(AB.alloc(FQ * DT), DT, FQ)
    wv_v = v2(AB.alloc(FQ * DT), DT, FQ)
    wad_v = v2(AB.alloc(DK * DT), DT, DK)
    wgd_v = v2(AB.alloc(DV * DT), DT, DV)
    wb_v = v2(AB.alloc(HL * DT), DT, HL)
    wau_sb = AB.alloc(FQ); wgu_sb = AB.alloc(FQ)
    cd_sb = {n: v3(AB.alloc(HL * CKW * 128), HL, CKW, 128) for n in "qkv"}
    for dst, src in ((wq_v, wq_i), (wk_v, wk_i), (wv_v, wv_i),
                     (wad_v, wad_i), (wgd_v, wgd_i), (wb_v, wb_i)):
        for d in range(DT):
            S.dma("gpsimd", dst[:, d, :], src[d * 128:(d + 1) * 128, :], "Gw")
    S.dma("gpsimd", wau_sb, wau_i[:, :], "Gw")
    S.dma("gpsimd", wgu_sb, wgu_i[:, :], "Gw")
    for n in "qkv":
        for hh in range(HL):
            for j in range(CKW):
                S.dma("gpsimd", cd_sb[n][:, hh, j, :], cd_i[n][hh, j, :, :], "Gw")
    wld = S.val("Gw")


    h_v = v2(AB.alloc(DT * TC), DT, TC)
    pre = {n: v2(AB.alloc(HL * (TC + 8)), HL, TC + 8) for n in "qkv"}
    sil = {n: pre[n][:, :, 0:TC] for n in "qkv"}
    qn_v = sil["q"]
    kn_v = sil["k"]
    qt_v = sil["q"]
    kt_v = v2(AB.alloc(HL * TC), HL, TC)
    kb_v = v2(AB.alloc(HL * TC), HL, TC)
    kdf_v = v2(AB.alloc(HL * TC), HL, TC)
    kdt_v3 = v3(AB.alloc(HL * NCH_C * 128), HL, NCH_C, 128)
    vt_v3 = v3(AB.alloc(HL * NCH_C * 128), HL, NCH_C, 128)
    sqs = AB.alloc(HL * TC)
    z_sb = sqs[:, 512:1024]; zg_sb = sqs[:, 0:512]
    eb_raw = AB.alloc(HL * TC)
    eb_b = v2(eb_raw, HL, TC)
    ebm_b = v2(AB.alloc(HL * TC), HL, TC)
    ebd_b = ebm_b
    sp_b = v2(AFP.alloc(HL * TC), HL, TC)
    cum_b = v2(AFP.alloc(HL * TC), HL, TC)
    r_big = sp_b.rearrange("p a b -> p (a b)")
    ebl_v = v2(AFP.alloc(HL * NCH_C), HL, NCH_C)
    bhr = AB.alloc(TC)
    bhT_v = v2(AB.alloc(NCH_C * HL), NCH_C, HL)
    gate_v = v2(AB.alloc(4 * TC), 4, TC)
    TT3 = v3(eb_raw, NCH_C, HL, 64)
    AT3 = v3(sqs, NCH_C, HL, 64)
    Npr = [AB.alloc(512) for _ in range(2)]
    Ppr = [AB.alloc(512) for _ in range(2)]
    Qpr = [AB.alloc(512) for _ in range(2)]
    Rpr = [AB.alloc(512) for _ in range(2)]
    og_sb = AB.alloc(FQ)
    Sf_v = v2(AFP.alloc(HL * DV), HL, DV)
    Sb_v = v2(AB.alloc(HL * DV), HL, DV)
    rhsu = AB.alloc(FQ)
    u_sb = rhsu
    ms_sb = AFP.alloc(HL)
    rr_sb = AFP.alloc(HL)
    S.emit("vector", lambda v: v.memset(Sf_v[:, :, :], 0.0), inc="V")
    S.emit("vector", lambda v: v.memset(Sb_v[:, :, :], 0.0), inc="V")
    for n in "qkv":
        S.emit("vector", lambda v, n2=n: v.memset(pre[n2][:, :, 0:3], 0.0), inc="V")

    sab = AFP.alloc(16)
    sa_bf = AB.alloc(DT)
    S.emit("vector", lambda v: v.tensor_copy(sa_bf, sa_c), inc="V")
    sabf_v = S.val("V")
    blist = []
    for base, wvv in ((0, wq_v), (4, wk_v), (8, wv_v)):
        for hh in range(HL):
            blist.append((lambda d, w=wvv, h3=hh: w[:, d, h3 * 128:(h3 + 1) * 128],
                          base + hh, 128))
    blist.append(((lambda d: wad_v[:, d, :]), 12, 128))
    blist.append(((lambda d: wgd_v[:, d, :]), 13, 128))
    blist.append(((lambda d: wb_v[:, d, :]), 14, HL))
    pi_b, pg_b = psum_acq()
    lastb = (len(blist) - 1, DT - 1)
    for ui, (wf, col, rows) in enumerate(blist):
        for d in range(DT):
            S.emit("tensor", lambda t, f=wf, d2=d, c2=col, r=rows, p=pi_b,
                   st=(d == 0), sp=(d == DT - 1): t.matmul(
                ps_f[p][0:r, c2:c2 + 1], f(d2), sa_bf[:, d2:d2 + 1],
                start=st, stop=sp),
                waits=([("Gw", wld), ("V", sabf_v)] + list(pg_b))
                if (ui == 0 and d == 0) else (),
                inc="P" if (ui, d) == lastb else None)
    S.emit("scalar", lambda s2, p=pi_b: s2.copy(sab[:, 0:14], ps_f[p][:, 0:14]),
           waits=[("P", S.val("P"))], inc="A")
    S.emit("scalar", lambda s2, p=pi_b: s2.copy(sab[0:HL, 14:15], ps_f[p][0:HL, 14:15]),
           inc="A")
    psum_rel(pi_b, "A")
    for wvv, wdim in ((wq_v, FQ), (wk_v, FQ), (wv_v, FQ),
                      (wad_v, DK), (wgd_v, DV), (wb_v, HL)):
        S.emit("vector", lambda v, w=wvv, n2=wdim: v.tensor_tensor(
            w[:, :, :], w[:, :, :],
            ca1_c.rearrange("p (a one) -> p a one", one=1).to_broadcast([128, DT, n2]),
            AL.mult),
            waits=[("P", S.val("P"))], inc="V")
    wscale_done = S.val("V")

    ldh_map = {}
    h_free_hist = [0] * NTC

    def emit_h(tc2, pwait):
        lh = "Lh%d" % (tc2 % 2)
        t02 = tc2 * TC
        for ft in range(DT):
            S.dma("sync", h_v[:, ft, :], xhat_d[t02:t02 + TC, ft * 128:(ft + 1) * 128],
                  lh, waits=([ln_ch[tc2], ("P", pwait)]) if ft == 0 else (),
                  transpose=True)
        ldh_map[tc2] = (lh, S.val(lh))

    gbuf = [AB.alloc(FQ) for _ in range(2)]
    gguard = [0, 0]

    def emit_gathers(kr, ccv):
        for k in kr:
            for r in range(4):
                j = k * 4 + r
                gb2 = gbuf[j % 2]
                gg = "Gg%d" % (j % 2)
                sg = "Sg%d" % (j % 2)
                S.emit("gpsimd", lambda g, o=gb2, j2=j, h=k // 6: g.indirect_dma_start(
                    o, None, ogg_d[h],
                    IndirectOffsetOnAxis(ogidx_s[:, j2:j2 + 1], 0)),
                    waits=[("C", ccv), (("Sg%d" % (j % 2)), gguard[j % 2]),
                           ("Lc", S.val("Lc"))],
                    inc=gg, inc_n=16)
                S.dma("sync", ogsl_d[(r * 8 + k) * 128:(r * 8 + k) * 128 + 128, :], gb2, sg,
                      waits=[(gg, S.val(gg))])
                gguard[j % 2] = S.val(sg)

    h_free = [0]
    so_prog = [0] * NTC
    emit_h(0, 0)
    for tc in range(NTC):
        t0 = tc * TC
        hw = ldh_map[tc]
        if DEBUG:
            for ft in range(DT):
                S.dma("sync", dbg_h[ft * 128:(ft + 1) * 128, t0:t0 + TC], h_v[:, ft, :],
                      "Sd", waits=[hw])

        if tc > 0:
            for n in "qkv":
                for hh in range(HL):
                    S.emit("vector", lambda v, n2=n, h3=hh: v.tensor_copy(
                        pre[n2][:, h3, 0:3], pre[n2][:, h3, TC:TC + 3]),
                        waits=[("A", S.val("A"))] if (n == "q" and hh == 0) else (), inc="V")
        halo_done = S.val("V")
        evac_a = {}
        for uo, (n, wvv) in enumerate((("q", wq_v), ("k", wk_v), ("v", wv_v))):
            for hh in range(HL):
                pi, pg = psum_acq()
                for d in range(DT):
                    S.emit("tensor", lambda t, w=wvv, d2=d, h3=hh, p=pi, h2=h_v: t.matmul(
                        ps_f[p][:, :], w[:, d2, h3 * 128:(h3 + 1) * 128], h2[:, d2, :],
                        start=(d2 == 0), stop=(d2 == DT - 1)),
                        waits=([hw, ("V", wscale_done)] + list(pg)) if d == 0 else (),
                        inc="P" if d == DT - 1 else None)
                S.emit("scalar", lambda s, n2=n, h3=hh, p=pi, c2=uo * 4 + hh: s.activation(
                    pre[n2][:, h3, 3:3 + TC], ps_f[p][:, :], AF.Identity,
                    bias=sab[:, c2:c2 + 1]),
                    waits=[("P", S.val("P")), ("V", halo_done)], inc="A")
                psum_rel(pi, "A")
                evac_a[(n, hh)] = S.val("A")
        for n in "qkv":
            for hh in range(HL):
                pi, pg = psum_acq()
                for j in range(CKW):
                    S.emit("tensor", lambda t, n2=n, h3=hh, j2=j, p=pi: t.matmul(
                        ps_f[p][:, :], cd_sb[n2][:, h3, j2, :], pre[n2][:, h3, j2:j2 + TC],
                        start=(j2 == 0), stop=(j2 == CKW - 1)),
                        waits=([("A", evac_a[(n, hh)])] + list(pg)) if j == 0 else (),
                        inc="P" if j == CKW - 1 else None)
                S.emit("scalar", lambda s, n2=n, h3=hh, p=pi: s.activation(
                    sil[n2][:, h3, :], ps_f[p][:, :], AF.Silu),
                    waits=[("P", S.val("P"))], inc="A")
                psum_rel(pi, "A")
        sil_done = S.val("A")
        for n in ("q", "k"):
            S.emit("scalar", lambda s, n2=n: s.activation(
                v2(sqs, HL, TC), sil[n2][:, :, :], AF.Square),
                waits=[("A", sil_done), ("P", S.val("P"))], inc="A")
            sqd = S.val("A")
            qb = (0, 1, 2, 3)
            pgq = psum_acq_set(qb)
            for hh in range(HL):
                S.emit("tensor", lambda t, h3=hh: t.matmul(
                    ps_f[h3][:, :], ones_bf, sqs[:, h3 * TC:(h3 + 1) * TC],
                    start=True, stop=True),
                    waits=([("A", sqd)] + list(pgq)) if hh == 0 else (),
                    inc="P" if hh == HL - 1 else None)
            S.emit("vector", lambda v: v.tensor_single_scalar(
                r_big, ps_big[:, 0:2048], 1e-12, AL.max),
                waits=[("P", S.val("P"))], inc="V")
            psum_rel_set(qb, "V")
            S.emit("vector", lambda v: v.reciprocal(r_big, r_big), inc="V")
            S.emit("scalar", lambda s: s.activation(r_big, r_big, AF.Sqrt),
                   waits=[("V", S.val("V"))], inc="A")
            S.emit("vector", lambda v, n2=n: v.tensor_tensor(
                sil[n2][:, :, :], sil[n2][:, :, :], v2(r_big, HL, TC), AL.mult),
                waits=[("A", S.val("A"))], inc="V")
        l2_done = S.val("V")
        if DEBUG:
            for hh in range(HL):
                S.dma("sync", dbg_qn[hh * 128:(hh + 1) * 128, t0:t0 + TC], qn_v[:, hh, :],
                      "Sd", waits=[("V", S.val("V"))])
                S.dma("sync", dbg_kn[hh * 128:(hh + 1) * 128, t0:t0 + TC], kn_v[:, hh, :],
                      "Sd", waits=[("V", S.val("V"))])

        def shared_proj(wtile, dst_sb, bcol):
            pi, pg = psum_acq()
            for d in range(DT):
                S.emit("tensor", lambda t, d2=d, p=pi, w=wtile, h2=h_v: t.matmul(
                    ps_f[p][:, :], w[:, d2, :], h2[:, d2, :],
                    start=(d2 == 0), stop=(d2 == DT - 1)),
                    waits=([hw] + list(pg)) if d == 0 else (),
                    inc="P" if d == DT - 1 else None)
            S.emit("scalar", lambda s, p=pi, o=dst_sb, c2=bcol: s.activation(
                o, ps_f[p][:, :], AF.Identity, bias=sab[:, c2:c2 + 1]),
                   waits=[("P", S.val("P"))], inc="A")
            psum_rel(pi, "A")
            return S.val("A")

        zd = shared_proj(wad_v, z_sb, 12)
        zgd = shared_proj(wgd_v, zg_sb, 13)
        pi, pg = psum_acq()
        for d in range(DT):
            S.emit("tensor", lambda t, d2=d, p=pi, h2=h_v: t.matmul(
                ps_f[p][0:HL, :], wb_v[:, d2, :], h2[:, d2, :],
                start=(d2 == 0), stop=(d2 == DT - 1)),
                waits=([hw] + list(pg)) if d == 0 else (),
                inc="P" if d == DT - 1 else None)
        h_free_hist[tc] = S.val("P")
        if tc + 1 < NTC:
            emit_h(tc + 1, h_free_hist[tc])
        S.emit("vector", lambda v, p=pi: v.tensor_scalar(
            bhr[0:HL, :], ps_f[p][0:HL, :], sab[0:HL, 14:15], -10.0, AL.add, AL.max),
            waits=[("P", S.val("P")), ("A", S.val("A"))], inc="V")
        psum_rel(pi, "V")
        S.emit("vector", lambda v: v.tensor_single_scalar(
            bhr[0:HL, :], bhr[0:HL, :], 10.0, AL.min), inc="V")
        S.emit("scalar", lambda s: s.activation(bhr[0:HL, :], bhr[0:HL, :], AF.Sigmoid),
               waits=[("V", S.val("V"))], inc="A")
        bhd = S.val("A")
        bi2, bg = psb_acq()
        for c8 in range(NCH_C):
            S.emit("tensor", lambda t, s2=c8, p=bi2: t.matmul(
                ps_b[p][0:64, s2 * HL:(s2 + 1) * HL], bhr[0:HL, s2 * 64:(s2 + 1) * 64],
                ident[0:HL, 0:HL], is_transpose=True,
                start=(s2 == 0), stop=(s2 == NCH_C - 1)),
                waits=([("A", bhd)] + list(bg)) if c8 == 0 else (),
                inc="P" if c8 == NCH_C - 1 else None)
        S.emit("vector", lambda v, p=bi2: v.tensor_copy(
            bhT_v[0:64, :, :], ps_b[p][0:64, 0:NCH_C * HL]),
            waits=[("P", S.val("P"))], inc="V")
        psb_rel(bi2, "V")
        qb = (0, 1, 2, 3)
        pgq = psum_acq_set(qb)
        for sub in range(4):
            S.emit("tensor", lambda t, s2=sub: t.matmul(
                ps_f[s2][:, :], zg_sb[:, s2 * 128:(s2 + 1) * 128], wgu_sb,
                start=True, stop=True),
                waits=([("A", zgd)] + list(pgq)) if sub == 0 else (),
                inc="P" if sub == 3 else None)
        S.emit("scalar", lambda s: s.activation(
            gate_v[:, :, :], ps_big[:, 0:2048], AF.Sigmoid),
            waits=[("P", S.val("P"))], inc="A")
        psum_rel_set(qb, "A")
        S.emit("vector", lambda v: v.tensor_tensor(
            gate_v[:, :, :], gate_v[:, :, :],
            v2(rms_sb, 1, FQ).to_broadcast([128, 4, FQ]), AL.mult),
            waits=[("A", S.val("A")), ("So", S.val("So"))], inc="V")

        qb = (0, 1, 2, 3)
        pgq = psum_acq_set(qb)
        for hh in range(HL):
            S.emit("tensor", lambda t, h3=hh: t.matmul(
                ps_f[h3][:, :], wau_sb[:, h3 * 128:(h3 + 1) * 128], z_sb,
                start=True, stop=True),
                waits=([("A", zd)] + list(pgq)) if hh == 0 else (),
                inc="P" if hh == HL - 1 else None)
        S.emit("vector", lambda v: v.tensor_scalar(
            sp_b.rearrange("p a b -> p (a b)"), ps_big[:, 0:2048], -10.0, 10.0,
            AL.max, AL.min),
            waits=[("P", S.val("P"))], inc="V")
        psum_rel_set(qb, "V")
        S.emit("scalar", lambda s: s.activation(sp_b[:, :, :], sp_b[:, :, :], AF.Exp),
               waits=[("V", S.val("V"))], inc="A")
        S.emit("scalar", lambda s: s.activation(sp_b[:, :, :], sp_b[:, :, :], AF.Ln, bias=1.0),
               inc="A")
        spd = S.val("A")
        for hh in range(HL):
            for c in range(NCH_C):
                S.emit("vector", lambda v, h3=hh, c2=c: v.tensor_tensor_scan(
                    cum_b[:, h3, c2 * CH:(c2 + 1) * CH], sp_b[:, h3, c2 * CH:(c2 + 1) * CH],
                    sp_b[:, h3, c2 * CH:(c2 + 1) * CH], 0.0, AL.add, AL.bypass),
                    waits=[("A", spd)] if (hh == 0 and c == 0) else (), inc="V")
        cumd = S.val("V")
        S.emit("scalar", lambda s: s.activation(eb_b[:, :, :], cum_b[:, :, :], AF.Exp, scale=-1.0),
               waits=[("V", cumd)], inc="A")
        S.emit("scalar", lambda s: s.activation(ebm_b[:, :, :], cum_b[:, :, :], AF.Exp), inc="A")
        S.emit("scalar", lambda s: s.activation(
            ebl_v[:, :, :], cum_b.rearrange("p a (c t) -> p a c t", t=CH)[:, :, :, CH - 1],
            AF.Exp, scale=-1.0), inc="A")
        ebld = S.val("A")
        S.emit("vector", lambda v: v.tensor_tensor(
            kt_v[:, :, :], kn_v[:, :, :], eb_b[:, :, :], AL.mult),
            waits=[("A", ebld), ("P", S.val("P"))], inc="V")
        S.emit("vector", lambda v: v.tensor_tensor(
            kb_v[:, :, :], kn_v[:, :, :], ebm_b[:, :, :], AL.mult), inc="V")
        S.emit("vector", lambda v: v.tensor_tensor(
            ebd_b.rearrange("p a (c t) -> p a c t", t=CH),
            ebm_b.rearrange("p a (c t) -> p a c t", t=CH),
            ebl_v.rearrange("p a (c o) -> p a c o", o=1).to_broadcast([128, HL, NCH_C, CH]),
            AL.mult), inc="V")
        S.emit("vector", lambda v: v.tensor_tensor(
            kdf_v[:, :, :], kn_v[:, :, :], ebd_b[:, :, :], AL.mult), inc="V")
        S.emit("vector", lambda v: v.tensor_tensor(
            qt_v[:, :, :], qn_v[:, :, :], eb_b[:, :, :], AL.mult), inc="V")
        scald = S.val("V")
        for h3c in range(HL):
            for src_t, dst_t, wsem in ((kdf_v, kdt_v3, ("V", scald)),
                                       (sil["v"], vt_v3, ("A", sil_done))):
                for cg in range(2):
                    bi2, bg = psb_acq()
                    for cc4 in range(4):
                        c8 = cg * 4 + cc4
                        S.emit("tensor", lambda t, h4=h3c, b2=c8, k4=cc4, p=bi2, st=src_t: t.matmul(
                            ps_b[p][0:64, k4 * 128:(k4 + 1) * 128],
                            st[:, h4, b2 * 64:(b2 + 1) * 64], ident, is_transpose=True,
                            start=(k4 == 0), stop=(k4 == 3)),
                            waits=([wsem] + list(bg)) if cc4 == 0 else (),
                            inc="P" if cc4 == 3 else None)
                    S.emit("scalar", lambda s, h4=h3c, g2=cg, p=bi2, dt2=dst_t: s.copy(
                        dt2[0:64, h4, g2 * 4:(g2 + 1) * 4, :], ps_b[p][0:64, :]),
                        waits=[("P", S.val("P"))], inc="A")
                    psb_rel(bi2, "A")
        kdvt_A = S.val("A")
        ebl_done = ebld
        ttv = [0, 0, 0, 0]
        seq_sv = [S.val("V")]

        def precompute_pair(p):
            st = {}
            thunks = []

            def stage_A():
                pga = psum_acq_set((0,))
                for u in range(8):
                    par, hh = u // 4, u % 4
                    c = 2 * p + par
                    csl = slice(c * CH, (c + 1) * CH)
                    S.emit("tensor", lambda t, h3=hh, cs=csl, u2=u: t.matmul(
                        ps_f[0][0:64, u2 * 64:(u2 + 1) * 64], kt_v[:, h3, cs], kb_v[:, h3, cs],
                        start=(u2 == 0), stop=(u2 == 7)),
                        waits=([("V", scald)] + list(pga)) if u == 0 else (),
                        inc="P" if u == 7 else None)
                st["amm"] = S.val("P")
            thunks.append(stage_A)

            def stage_N():
                S.emit("vector", lambda v: v.tensor_tensor(
                    Npr[p % 2][0:64, :].rearrange("q (a b) -> q a b", b=64),
                    ps_f[0][0:64, :].rearrange("q (a b) -> q a b", b=64),
                    bhT_v[0:64, 2 * p:2 * p + 2, :].rearrange("q a b -> q (a b)")
                    .rearrange("q (a o) -> q a o", o=1).to_broadcast([64, 8, 64]),
                    AL.mult), waits=[("P", st["amm"])], inc="V")
                S.emit("vector", lambda v: v.tensor_tensor(
                    Npr[p % 2][0:64, :], Npr[p % 2][0:64, :], nmsk[0:64, :], AL.mult), inc="V")
                psum_rel_set((0,), "V")
                st["n"] = S.val("V")
            thunks.append(stage_N)

            def stage_P0():
                bi2, bg = psb_acq()
                for u in range(8):
                    S.emit("tensor", lambda t, u2=u, pb=bi2: t.matmul(
                        ps_b[pb][0:64, u2 * 64:(u2 + 1) * 64],
                        Npr[p % 2][0:64, u2 * 64:(u2 + 1) * 64],
                        ident[0:64, 0:64], is_transpose=True,
                        start=(u2 == 0), stop=(u2 == 7)),
                        waits=([("V", st["n"])] + list(bg)) if u == 0 else (),
                        inc="P" if u == 7 else None)
                S.emit("scalar", lambda s2, pb=bi2: s2.copy(
                    Ppr[p % 2][0:64, :], ps_b[pb][0:64, :]),
                    waits=[("P", S.val("P"))], inc="A")
                psb_rel(bi2, "A")
                S.emit("vector", lambda v: v.tensor_copy(Qpr[p % 2][0:64, :], Npr[p % 2][0:64, :]),
                       inc="V")
                S.emit("vector", lambda v: v.tensor_tensor(
                    Rpr[p % 2][0:64, :], Ppr[p % 2][0:64, :], erep[0:64, :], AL.add),
                    waits=[("A", S.val("A"))], inc="V")
                st["ev"] = (S.val("V"), S.val("A"))
            thunks.append(stage_P0)

            def mk_lvl_mm(lev):
                def stage():
                    pga = psum_acq_set((0, 1, 2))
                    ev, ea = st["ev"]
                    for u in range(8):
                        usl = slice(u * 64, (u + 1) * 64)
                        S.emit("tensor", lambda t, us=usl: t.matmul(
                            ps_f[0][0:64, us], Ppr[p % 2][0:64, us], Qpr[p % 2][0:64, us],
                            start=(us.start == 0), stop=(us.start == 448)),
                            waits=([("V", ev), ("A", ea)] + list(pga)) if u == 0 else ())
                    for u in range(8):
                        usl = slice(u * 64, (u + 1) * 64)
                        S.emit("tensor", lambda t, us=usl: t.matmul(
                            ps_f[1][0:64, us], Qpr[p % 2][0:64, us], Ppr[p % 2][0:64, us],
                            start=(us.start == 0), stop=(us.start == 448)))
                    for u in range(8):
                        usl = slice(u * 64, (u + 1) * 64)
                        S.emit("tensor", lambda t, us=usl: t.matmul(
                            ps_f[2][0:64, us], Qpr[p % 2][0:64, us], Rpr[p % 2][0:64, us],
                            start=(us.start == 0), stop=(us.start == 448)),
                            inc="P" if u == 7 else None)
                    st["mm"] = S.val("P")
                return stage

            def mk_lvl_ev(lev):
                def stage():
                    S.emit("scalar", lambda s2: s2.copy(Qpr[p % 2][0:64, :], ps_f[0][0:64, :]),
                           waits=[("P", st["mm"])], inc="A")
                    S.emit("scalar", lambda s2: s2.copy(Ppr[p % 2][0:64, :], ps_f[1][0:64, :]),
                           inc="A")
                    S.emit("vector", lambda v: v.tensor_tensor(
                        Rpr[p % 2][0:64, :], ps_f[2][0:64, :], Rpr[p % 2][0:64, :], AL.add),
                        waits=[("P", st["mm"])], inc="V")
                    psum_rel_set((0, 1), "A")
                    psum_rel_set((2,), "V")
                    st["ev"] = (S.val("V"), S.val("A"))
                return stage

            for lev in range(5):
                thunks.append(mk_lvl_mm(lev))
                thunks.append(mk_lvl_ev(lev))

            def stage_TT():
                ev, ea = st["ev"]
                S.emit("vector", lambda v: v.tensor_tensor(
                    TT3[0:64, 2 * p:2 * p + 2, :, :],
                    Rpr[p % 2][0:64, :].rearrange("q (a b c) -> q a b c", a=2, c=64),
                    bhT_v[0:64, 2 * p:2 * p + 2, :].rearrange(
                        "q a (b o) -> q a b o", o=1).to_broadcast([64, 2, HL, 64]),
                    AL.mult), waits=[("V", ev), ("A", ea)], inc="V")
            thunks.append(stage_TT)

            def stage_attnT():
                pga = psum_acq_set((0,))
                for u in range(8):
                    par, hh = u // 4, u % 4
                    c = 2 * p + par
                    csl = slice(c * CH, (c + 1) * CH)
                    S.emit("tensor", lambda t, h3=hh, cs=csl, u2=u: t.matmul(
                        ps_f[0][0:64, u2 * 64:(u2 + 1) * 64], kb_v[:, h3, cs], qt_v[:, h3, cs],
                        start=(u2 == 0), stop=(u2 == 7)),
                        waits=(list(pga) + [("V", scald)]) if u == 0 else (),
                        inc="P" if u == 7 else None)
                amm = S.val("P")
                S.emit("vector", lambda v: v.tensor_tensor(
                    AT3[0:64, 2 * p:2 * p + 2, :, :],
                    ps_f[0][0:64, :].rearrange("q (a b c) -> q a b c", a=2, c=64),
                    umsk[0:64, :].rearrange("q (a b c) -> q a b c", a=2, c=64),
                    AL.mult), waits=[("P", amm)], inc="V")
                psum_rel_set((0,), "V")
                ttv[p] = S.val("V")
            thunks.append(stage_attnT)
            return thunks

        seq_rot = {"i": 0}

        def seq_acq():
            i = 3 + seq_rot["i"] % 3
            seq_rot["i"] += 1
            return i, ps_state["guard"][i]

        def scan_step_pieces(c):
            csl = slice(c * CH, (c + 1) * CH)
            ro = (c % 2) * 64
            rsl = slice(ro, ro + 64)
            pieces = []
            stx = {}

            def piece1():
              pia, pga = seq_acq()
              stx["pia"] = pia
              for hh in range(HL):
                S.emit("tensor", lambda t, h3=hh, cs=csl, p=pia: t.matmul(
                    ps_f[p][0:64, h3 * 128:(h3 + 1) * 128], kt_v[:, h3, cs], Sb_v[:, h3, :],
                    start=(h3 == 0), stop=(h3 == HL - 1)),
                    waits=([("V", max(seq_sv[0], scald)), ("A", kdvt_A)] + list(pga))
                    if hh == 0 else (),
                    inc="P" if hh == HL - 1 else None)
              mm1d = S.val("P")
              S.emit("vector", lambda v, p=pia, c2=c: v.tensor_tensor(
                rhsu[0:64, :], vt_v3[0:64, :, c2, :], ps_f[p][0:64, :], AL.subtract),
                waits=[("P", mm1d), ("A", kdvt_A)], inc="V")
              stx["rhv"] = S.val("V")
              psum_rel(pia, "V")
            pieces.append(piece1)

            def piece2():
              rhv = stx["rhv"]
              pib, pgb = seq_acq()
              for hh in range(HL):
                S.emit("tensor", lambda t, h3=hh, c2=c, p=pib: t.matmul(
                    ps_f[p][0:64, h3 * 128:(h3 + 1) * 128], TT3[0:64, c2, h3, :],
                    rhsu[0:64, h3 * 128:(h3 + 1) * 128],
                    start=(h3 == 0), stop=(h3 == HL - 1)),
                    waits=([("V", max(rhv, ttv[c // 2]))] + list(pgb)) if hh == 0 else (),
                    inc="P" if hh == HL - 1 else None)
              S.emit("scalar", lambda s2, p=pib: s2.copy(u_sb[0:64, :], ps_f[p][0:64, :]),
                     waits=[("P", S.val("P"))], inc="A")
              psum_rel(pib, "A")
              stx["ud"] = S.val("A")
            pieces.append(piece2)

            def piece3():
              ud = stx["ud"]
              pio, pgo = seq_acq()
              for hh in range(HL):
                  S.emit("tensor", lambda t, h3=hh, cs=csl, p=pio, r0=ro: t.matmul(
                      ps_f[p][r0:r0 + 64, h3 * 128:(h3 + 1) * 128], qt_v[:, h3, cs], Sb_v[:, h3, :],
                      start=(h3 == 0), stop=False),
                      waits=([("A", ud)] + list(pgo)) if hh == 0 else ())
              for hh in range(HL):
                  S.emit("tensor", lambda t, h3=hh, c2=c, p=pio, r0=ro: t.matmul(
                      ps_f[p][r0:r0 + 64, h3 * 128:(h3 + 1) * 128], AT3[0:64, c2, h3, :],
                      u_sb[0:64, h3 * 128:(h3 + 1) * 128],
                      start=False, stop=(h3 == HL - 1)),
                      inc="P" if hh == HL - 1 else None)
              od = S.val("P")
              pid, pgd = seq_acq()
              for hh in range(HL):
                  S.emit("tensor", lambda t, h3=hh, c2=c, p=pid: t.matmul(
                      ps_f[p][:, h3 * 128:(h3 + 1) * 128], kdt_v3[0:64, h3, c2, :],
                      u_sb[0:64, h3 * 128:(h3 + 1) * 128],
                      start=(h3 == 0), stop=(h3 == HL - 1)),
                      waits=(list(pgd)) if hh == 0 else (),
                      inc="P" if hh == HL - 1 else None)
              dsd = S.val("P")
              for hh in range(HL):
                  S.emit("vector", lambda v, h3=hh, c2=c, p=pid: v.scalar_tensor_tensor(
                      Sf_v[:, h3, :], Sf_v[:, h3, :], ebl_v[:, h3, c2:c2 + 1],
                      ps_f[p][:, h3 * 128:(h3 + 1) * 128], AL.mult, AL.add),
                      waits=[("P", dsd), ("A", ebl_done)] if hh == 0 else (), inc="V")
              S.emit("vector", lambda v: v.tensor_copy(Sb_v[:, :, :], Sf_v[:, :, :]), inc="V")
              seq_sv[0] = S.val("V")
              psum_rel(pid, "V")
              for hh in range(HL):
                  S.emit("scalar", lambda s2, h3=hh, p=pio, rs=rsl: s2.activation(
                      kdf_v.rearrange("p a b -> p (a b)")[rs, h3 * 128:(h3 + 1) * 128],
                      ps_f[p][rs, h3 * 128:(h3 + 1) * 128], AF.Square,
                      accum_out=ms_sb[rs, h3:h3 + 1]),
                      waits=[("P", od)] if hh == 0 else (), inc="A")
              S.emit("vector", lambda v, rs=rsl: v.tensor_scalar(
                  ms_sb[rs, :], ms_sb[rs, :], 1.0 / DV, 1e-6, AL.mult, AL.add),
                  waits=[("A", S.val("A"))], inc="V")
              S.emit("vector", lambda v, rs=rsl: v.reciprocal(rr_sb[rs, :], ms_sb[rs, :]), inc="V")
              S.emit("scalar", lambda s2, rs=rsl: s2.activation(rr_sb[rs, :], rr_sb[rs, :], AF.Sqrt),
                     waits=[("V", S.val("V"))], inc="A")
              rd = S.val("A")
              for hh in range(HL):
                  ogsl2 = og_sb[rsl, hh * 128:(hh + 1) * 128]
                  gsl = gate_v[rsl, c // 2, hh * 128:(hh + 1) * 128]
                  S.emit("vector", lambda v, h3=hh, p=pio, o=ogsl2, g2=gsl, rs=rsl: v.scalar_tensor_tensor(
                      o, ps_f[p][rs, h3 * 128:(h3 + 1) * 128], rr_sb[rs, h3:h3 + 1],
                      g2, AL.mult, AL.mult),
                      waits=([("A", rd)] + ([("So", S.val("So"))] if c % 2 == 0 else []))
                      if hh == 0 else (), inc="V")
              psum_rel(pio, "V")
              if c % 2 == 1:
                  S.dma("sync", og_d[t0 + (c - 1) * CH:t0 + (c + 1) * CH, :], og_sb,
                        "So", waits=[("V", S.val("V"))])
                  if c == 7:
                      so_prog[tc] = S.val("So")

            pieces.append(piece3)
            return pieces

        th0 = precompute_pair(0)
        for th in th0:
            th()
        for p in range(1, 4):
            ths = precompute_pair(p)
            pcs = scan_step_pieces(2 * (p - 1)) + scan_step_pieces(2 * (p - 1) + 1)
            slots = {1: 0, 3: 1, 5: 2, 7: 3, 9: 4, 11: 5}
            for i, th in enumerate(ths):
                th()
                if i in slots:
                    pcs[slots[i]]()
        for c in (6, 7):
            for pc in scan_step_pieces(c):
                pc()
    attn_st = S.val("So")
    for g8 in range(8):
        w_ = [("So", so_prog[min(g8, 5)])]
        if g8 < 6:
            for it in range(g8 * 11, min(NIT, (g8 + 1) * 11)):
                S.dma("gpsimd", wdb_d[it, :, :], wd_i[it, :, :], "Wd",
                      waits=w_ if it == g8 * 11 else ())
                S.dma("gpsimd", wgb_d[it, :, :], wg_i[it, :, :], "Wd")
                S.dma("gpsimd", wub_d[it, :, :], wu_i[it, :, :], "Wd")
    S.emit("gpsimd", lambda g: g.collective_compute(
        "AllGather", AL.bypass, replica_groups=[[0, 1, 2, 3], [4, 5, 6, 7]],
        ins=[og_d[0:3 * T // 4, :]], outs=[ogg_d[0]]),
        waits=[("So", so_prog[5])], inc="C")
    cc_og1 = S.val("C")
    emit_gathers(range(6), cc_og1)
    AB.pop(); AFP.pop()
    S.barrier()
    ps_stack.close()
    ps_f.append(nc.ctx.enter_context(nc.psum_tensor("psf6", [128, 512], F32)))
    ps_f.append(nc.ctx.enter_context(nc.psum_tensor("psf7", [128, 512], F32)))
    ps_state["n"] = 8

    AB.push(); AFP.push()
    wo_sb = [AB.alloc(D) for _ in range(16)]
    for ft in range(16):
        S.dma("gpsimd", wo_sb[ft], wo_i[ft * 128:(ft + 1) * 128, :], "Gwo")
    wo_ld = S.val("Gwo")
    og_fm = [AB.alloc(TOK) for _ in range(16)]
    lo_half = [0, 0]

    def emit_ogfm(half):
        lsem = "Lo%d" % half
        c0 = 0 if half == 0 else 768
        c1 = 768 if half == 0 else 1024
        for r in range(4):
            for ftt in range(4):
                S.dma("sync", og_fm[r * 4 + ftt][:, c0:c1],
                      ogsl_d[r * 1024 + c0:r * 1024 + c1,
                             ftt * 128:(ftt + 1) * 128],
                      lsem, waits=[("Sg0", S.val("Sg0")), ("Sg1", S.val("Sg1"))]
                      if (r == 0 and ftt == 0) else (),
                      transpose=True)
        lo_half[half] = (lsem, S.val(lsem))

    emit_ogfm(0)
    S.emit("gpsimd", lambda g: g.collective_compute(
        "AllGather", AL.bypass, replica_groups=[[0, 1, 2, 3], [4, 5, 6, 7]],
        ins=[og_d[3 * T // 4:T, :]], outs=[ogg_d[1]]),
        waits=[("So", attn_st)], inc="C")
    cc_og2 = S.val("C")
    emit_gathers(range(6, 8), cc_og2)
    emit_ogfm(1)

    xo_t = AFP.alloc(D)
    x1_t = AFP.alloc(D)
    r_scr = AFP.alloc(512)
    st2 = AFP.alloc(8)
    xh2 = AB.alloc(D)
    sq2 = AB.alloc(D)
    mu = st2[:, 0:1]; ssq = st2[:, 1:2]; var = st2[:, 2:3]; rstd = st2[:, 3:4]
    for tt in range(8):
        S.dma("sync", xo_t, x_own[tt * 128:(tt + 1) * 128, :], "Lxo",
              waits=[("V", S.val("V")), ("S1", S.val("S1"))])
        xol = S.val("Lxo")
        for dc in range(4):
            pi, pg = psum_acq()
            for ft in range(16):
                S.emit("tensor", lambda t, f=ft, t2=tt, d2=dc, p=pi: t.matmul(
                    ps_f[p][:, :], og_fm[f][:, t2 * 128:(t2 + 1) * 128],
                    wo_sb[f][:, d2 * 512:(d2 + 1) * 512],
                    start=(f == 0), stop=(f == 15)),
                    waits=([lo_half[0 if tt < 6 else 1], ("Gwo", wo_ld)] + list(pg))
                    if ft == 0 else (),
                    inc="P" if ft == 15 else None)
            S.emit("vector", lambda v, p=pi, d2=dc: v.tensor_tensor(
                r_scr, ps_f[p][:, :], ga_bc[:, d2 * 512:(d2 + 1) * 512], AL.mult),
                waits=[("P", S.val("P")), ("Lb", S.val("Lb"))], inc="V")
            psum_rel(pi, "V")
            S.emit("vector", lambda v, d2=dc: v.tensor_tensor(
                x1_t[:, d2 * 512:(d2 + 1) * 512], r_scr,
                xo_t[:, d2 * 512:(d2 + 1) * 512], AL.add),
                waits=[("Lxo", xol)], inc="V")
        S.dma("sync", x1_d[tt * 128:(tt + 1) * 128, :], x1_t, "S1",
              waits=[("V", S.val("V"))])
        S.emit("vector", lambda v, m=mu: v.reduce_sum(m, x1_t, AX.X), inc="V")
        S.emit("scalar", lambda s, q=ssq: s.activation(sq2, x1_t, AF.Square, accum_out=q),
               waits=[("V", S.val("V"))], inc="A")
        S.emit("vector", lambda v, m=mu: v.tensor_single_scalar(m, m, 1.0 / D, AL.mult),
               inc="V")
        S.emit("vector", lambda v, m=mu, va=var: v.tensor_tensor(va, m, m, AL.mult),
               inc="V")
        S.emit("vector", lambda v, q=ssq, va=var: v.scalar_tensor_tensor(
            va, q, 1.0 / D, va, AL.mult, AL.subtract),
               waits=[("A", S.val("A"))], inc="V")
        S.emit("vector", lambda v, va=var: v.tensor_single_scalar(va, va, 1e-5, AL.add),
               inc="V")
        S.emit("vector", lambda v, va=var, r=rstd: v.reciprocal(r, va), inc="V")
        S.emit("scalar", lambda s, r=rstd: s.activation(r, r, AF.Sqrt),
               waits=[("V", S.val("V"))], inc="A")
        S.emit("vector", lambda v, m=mu, r=rstd: v.tensor_scalar(
            xh2, x1_t, m, r, AL.subtract, AL.mult),
            waits=[("A", S.val("A")), ("S2", S.val("S2"))], inc="V")
        S.dma("sync", xhat_d[tt * 128:(tt + 1) * 128, :], xh2, "S2",
              waits=[("V", S.val("V"))])
    x1_done = S.val("S1")
    ln2_done = S.val("S2")
    AB.pop(); AFP.pop()
    S.barrier()

    AB.push(); AFP.push()
    h2_v = v2(AB.alloc(DT * TOK), DT, TOK)
    for ft in range(DT):
        S.dma("sync", h2_v[:, ft, :], xhat_d[0:TOK, ft * 128:(ft + 1) * 128],
              "Lh2", waits=[("S2", ln2_done)] if ft == 0 else (), transpose=True)
    ldh2 = S.val("Lh2")
    for ft in range(DT):
        S.emit("vector", lambda v, f=ft: v.tensor_scalar(
            h2_v[:, f, :], h2_v[:, f, :], cb1_c[:, f:f + 1], sb_c[:, f:f + 1],
            AL.mult, AL.add), waits=[("Lh2", ldh2)] if ft == 0 else (), inc="V")
    h2_done = S.val("V")
    if DEBUG:
        for ft in range(DT):
            S.dma("sync", dbg_h2[ft * 128:(ft + 1) * 128, :], h2_v[:, ft, :],
                  "Sd", waits=[("V", h2_done)])

    wgb = [v2(AB.alloc(DT * 128), DT, 128) for _ in range(2)]
    wub = [v2(AB.alloc(DT * 128), DT, 128) for _ in range(2)]
    wguard = [0, 0]
    hm_q = [AB.alloc(TOK) for _ in range(4)]
    hs_q = AB.alloc(TOK)
    m_q = [AFP.alloc(TOK) for _ in range(4)]
    cc6 = [AB.alloc(TOK) for _ in range(6)]
    um_ = AB.alloc(TOK)
    msk = AB.alloc(TOK)
    silu_t = AB.alloc(512)

    for j in range(16):
        for q4 in range(4):
            it = q4 * 16 + j
            bi = it % 2
            gf = "Gf%d" % bi
            uf = "Uf%d" % bi
            S.dma("gpsimd", wgb[bi][:, :, :],
                  wgb_d[it, :, :].rearrange("p (a b) -> p a b", a=DT, b=128),
                  gf, waits=[("P", wguard[bi])])
            S.dma("gpsimd", wub[bi][:, :, :],
                  wub_d[it, :, :].rearrange("p (a b) -> p a b", a=DT, b=128),
                  uf, waits=[("P", wguard[bi])])
            wldv = S.val(gf)
            uldv = S.val(uf)
            for th in range(2):
                pi, pg = psum_acq()
                for d in range(DT):
                    S.emit("tensor", lambda t, d2=d, b2=bi, t2=th, p=pi: t.matmul(
                        ps_f[p][:, :], wgb[b2][:, d2, :], h2_v[:, d2, t2 * 512:(t2 + 1) * 512],
                        start=(d2 == 0), stop=(d2 == DT - 1)),
                        waits=([(gf, wldv), ("V", h2_done)] + list(pg)) if d == 0 else (),
                        inc="P" if d == DT - 1 else None)
                pu, pgu = psum_acq()
                for d in range(DT):
                    S.emit("tensor", lambda t, d2=d, b2=bi, t2=th, p=pu: t.matmul(
                        ps_f[p][:, :], wub[b2][:, d2, :], h2_v[:, d2, t2 * 512:(t2 + 1) * 512],
                        start=(d2 == 0), stop=(d2 == DT - 1)),
                        waits=([(uf, uldv)] + list(pgu)) if d == 0 else (),
                        inc="P" if d == DT - 1 else None)
                gud = S.val("P")
                S.emit("scalar", lambda s, p=pi: s.activation(silu_t, ps_f[p][:, :], AF.Silu),
                       waits=[("P", gud), ("V", S.val("V"))], inc="A")
                S.emit("vector", lambda v, q=q4, t2=th, p=pu: v.tensor_tensor(
                    hm_q[q][:, t2 * 512:(t2 + 1) * 512], silu_t, ps_f[p][:, :], AL.mult),
                    waits=[("A", S.val("A"))], inc="V")
                psum_rel(pi, "A"); psum_rel(pu, "V")
            wguard[bi] = S.val("P")
        hmd = S.val("V")
        for q4 in range(4):
            S.emit("scalar", lambda s, q=q4: s.activation(m_q[q], hm_q[q], AF.Square),
                   waits=[("V", hmd)] if q4 == 0 else (), inc="A")
        sqd = S.val("A")
        pairs = [(0, 1), (0, 2), (0, 3), (1, 2), (1, 3), (2, 3)]
        for i6, (a, b2) in enumerate(pairs):
            S.emit("vector", lambda v, i=i6, a2=a, b3=b2: v.tensor_tensor(
                cc6[i], m_q[a2], m_q[b3], AL.is_gt),
                waits=[("A", sqd)] if i6 == 0 else (), inc="V")
        combos = [
            (0, 0, 1, AL.add, 2, AL.add, 2.0, AL.is_ge),
            (1, 3, 4, AL.add, 0, AL.subtract, 1.0, AL.is_ge),
            (2, 5, 1, AL.subtract, 3, AL.subtract, 0.0, AL.is_ge),
            (3, 2, 4, AL.add, 5, AL.add, 1.0, AL.is_le),
        ]
        for q4, i0, i1, op1, i2, op2, thr, opm in combos:
            S.emit("vector", lambda v, a=i0, b3=i1, o=op1: v.tensor_tensor(
                um_, cc6[a], cc6[b3], o),
                waits=[("Sh", S.val("Sh"))] if q4 == 0 else (), inc="V")
            S.emit("vector", lambda v, a=i2, o=op2: v.tensor_tensor(
                um_, um_, cc6[a], o), inc="V")
            S.emit("vector", lambda v, t2=thr, o=opm: v.tensor_single_scalar(
                msk, um_, t2, o), inc="V")
            S.emit("vector", lambda v, q=q4: v.tensor_tensor(
                hs_q, hm_q[q], msk, AL.mult),
                waits=[("Sh", S.val("Sh"))], inc="V")
            it = q4 * 16 + j
            S.dma("sync", hs_d[it, :, :], hs_q, "Sh", waits=[("V", S.val("V"))])
    hs_done = S.val("Sh")
    wd_done = S.val("Wd")
    AB.pop(); AFP.pop()
    S.barrier()

    AB.push(); AFP.push()
    hsr = v2(AB.alloc(NIT * TOK), NIT, TOK)
    wdb4 = [v2(AB.alloc(4 * 512), 4, 512) for _ in range(2)]
    dguard = [0, 0]
    o_t = AFP.alloc(512)
    x1e8 = AFP.alloc(8 * 512)
    ghv = []
    for g8 in range(8):
        gh = "Gh%d" % g8
        S.dma("gpsimd", hsr[:, g8 * 8:(g8 + 1) * 8, :],
              hs_d[g8 * 8:(g8 + 1) * 8, :, :].rearrange("a p t -> p a t"),
              gh, waits=[("Sh", hs_done), ("Wd", wd_done)] if g8 == 0 else ())
        ghv.append((gh, S.val(gh)))
    pis = []
    pgs = []
    for u in range(8):
        pi, pg = psum_acq()
        pis.append(pi)
        pgs.append(list(pg))
    pre_hld = {}

    def emit_wd_load(dq2, c):
        it0 = c * 4
        bi = c % 2
        eng = "sync" if bi == 0 else "gpsimd"
        gd = "Gd%d" % bi
        S.dma(eng, wdb4[bi][:, :, :],
              wdb_d[it0:it0 + 4, :, dq2 * 512:(dq2 + 1) * 512].rearrange("a p d -> p a d"),
              gd, waits=[("P", dguard[bi])])
        pre_hld[(dq2, c)] = S.val(gd)

    S.dma("gpsimd", x1e8.rearrange("p (a t) -> p a t", a=8),
          x1_d[:, 0:512].rearrange("(a p) t -> p a t", p=128), "Le")
    x1ld = S.val("Le")
    for dq in range(4):
        dco = dq * 512
        for c in range(16):
            it0 = c * 4
            bi = c % 2
            gd = "Gd%d" % bi
            if (dq, c) not in pre_hld:
                emit_wd_load(dq, c)
            hld = pre_hld[(dq, c)]
            for ii in range(4):
                it = it0 + ii
                for u in range(8):
                    pi = pis[u]
                    w_ = []
                    if ii == 0 and u == 0:
                        w_ += [(gd, hld), ghv[min(7, (it0 + 3) // 8)]]
                    if it == 0:
                        w_ += pgs[u]
                    S.emit("tensor", lambda t, i2=it, b2=bi, ii2=ii, u2=u, p=pi: t.matmul(
                        ps_f[p][:, :], hsr[:, i2, u2 * 128:u2 * 128 + 128],
                        wdb4[b2][:, ii2, :],
                        start=(i2 == 0), stop=(i2 == NIT - 1)),
                        waits=w_, inc="P" if (ii == 3 and u == 7) else None)
            dguard[bi] = S.val("P")
        dnd = S.val("P")
        if dq < 3:
            emit_wd_load(dq + 1, 0)
            emit_wd_load(dq + 1, 1)
        pgs = []
        new_pis = []
        for u in range(8):
            rows = slice(u * 128, u * 128 + 128)
            pi = pis[u]
            S.emit("vector", lambda v, p=pi, d2=dco: v.tensor_tensor(
                o_t, ps_f[p][:, :], gb_bc[:, d2:d2 + 512], AL.mult),
                waits=[("P", dnd), ("Le", x1ld), ("So2", S.val("So2")),
                       ("Lb", S.val("Lb"))], inc="V")
            S.emit("vector", lambda v, u2=u: v.tensor_tensor(
                o_t, o_t, x1e8[:, u2 * 512:(u2 + 1) * 512], AL.add), inc="V")
            S.dma("sync", out_s[rows, dco:dco + 512], o_t, "So2",
                  waits=[("V", S.val("V"))])
            psum_rel(pi, "V")
            npi, npg = psum_acq()
            new_pis.append(npi)
            pgs.append(list(npg))
        pis = new_pis
        if dq < 3:
            S.dma("gpsimd", x1e8.rearrange("p (a t) -> p a t", a=8),
                  x1_d[:, dco + 512:dco + 1024].rearrange("(a p) t -> p a t", p=128),
                  "Le", waits=[("V", S.val("V"))])
            x1ld = S.val("Le")
    final_st = S.val("So2")
    S.emit("sync", lambda sy: sy.nop(), waits=[("So2", final_st)])
    AB.pop(); AFP.pop()

    return nc, S, dbg


def realize(nc, S):
    with contextlib.ExitStack() as es:
        for name in S.cnt:
            S.sems[name] = es.enter_context(nc.semaphore(name))
        with nc.Block() as block:
            for eng in S.ENGINES:
                if not S.ops[eng]:
                    continue
                getattr(block, eng)(lambda e, _eng=eng: [cl(e) for cl in S.ops[_eng]])
    return nc


_CACHE = {}


def get_nc():
    if "nc" not in _CACHE:
        nc, S, dbg = build_program()
        realize(nc, S)
        _CACHE["nc"] = nc
        _CACHE["dbg"] = dbg
    return _CACHE["nc"]


def _prep_inputs(inputs):
    f32 = np.float32
    x = np.ascontiguousarray(np.asarray(inputs["L0_attn_x"], f32))
    wq = np.asarray(inputs["L0_attn_Wq"], f32)
    wk = np.asarray(inputs["L0_attn_Wk"], f32)
    wv = np.asarray(inputs["L0_attn_Wv"], f32)
    wo = np.ascontiguousarray(np.asarray(inputs["L0_attn_Wo"], f32))
    wad = np.ascontiguousarray(np.asarray(inputs["L0_attn_Wad"], f32))
    wau = np.asarray(inputs["L0_attn_Wau"], f32)
    wb = np.asarray(inputs["L0_attn_Wb"], f32)
    wgd = np.ascontiguousarray(np.asarray(inputs["L0_attn_Wgd"], f32))
    wgu = np.asarray(inputs["L0_attn_Wgu"], f32)
    cq = np.asarray(inputs["L0_attn_conv_q_w"], f32)
    ck = np.asarray(inputs["L0_attn_conv_k_w"], f32)
    cv = np.asarray(inputs["L0_attn_conv_v_w"], f32)
    rms = np.asarray(inputs["L0_attn_rms_w"], f32)
    wg = np.asarray(inputs["L0_ffn_w_gate"], f32)
    wu = np.asarray(inputs["L0_ffn_w_up"], f32)
    wd = np.asarray(inputs["L0_ffn_w_down"], f32)
    cvec = np.asarray(inputs["L0_c"], f32)
    wmod = np.asarray(inputs["L0_wmod"], f32)

    wg_p = wg.reshape(D, FI // 4, 4).transpose(0, 2, 1).reshape(D, FI)
    wu_p = wu.reshape(D, FI // 4, 4).transpose(0, 2, 1).reshape(D, FI)
    wd_p = wd.reshape(FI // 4, 4, D).transpose(1, 0, 2).reshape(FI, D)
    wg_t = np.ascontiguousarray(
        wg_p.reshape(DT, 128, NIT, 128).transpose(2, 1, 0, 3).reshape(NIT, 128, DT * 128))
    wu_t = np.ascontiguousarray(
        wu_p.reshape(DT, 128, NIT, 128).transpose(2, 1, 0, 3).reshape(NIT, 128, DT * 128))
    wd_t = np.ascontiguousarray(wd_p.reshape(NIT, 128, D))

    ident = np.eye(128, dtype=f32)
    masks = np.zeros((64, 192), f32)
    masks[:, 0:64] = np.tril(np.ones((64, 64), f32), -1)
    masks[:, 64:128] = np.triu(np.ones((64, 64), f32))
    masks[:, 128:192] = np.eye(64, dtype=f32)
    mrep = np.zeros((64, 1536), f32)
    mrep[:, 0:512] = np.tile(-np.tril(np.ones((64, 64), f32), -1), (1, 8))
    mrep[:, 512:1024] = np.tile(np.triu(np.ones((64, 64), f32)), (1, 8))
    mrep[:, 1024:1536] = np.tile(np.eye(64, dtype=f32), (1, 8))
    ones1 = np.ones((1, 128), f32)
    rms_bc = np.ascontiguousarray(np.broadcast_to(np.tile(rms, HL), (128, FQ)))
    cT = np.ascontiguousarray(cvec.T)

    def diag_tiles(cw, lo):
        out = np.zeros((HL, CKW, 128, 128), f32)
        for hh in range(HL):
            for j in range(CKW):
                np.fill_diagonal(out[hh, j], cw[lo + hh * 128: lo + (hh + 1) * 128, j])
        return out

    in_maps = []
    for c in range(8):
        b = c // 4
        qd = c % 4
        fs = slice(qd * FQ, (qd + 1) * FQ)
        hsl = slice(qd * HL, (qd + 1) * HL)
        own_tiles = [k * 4 + qd for k in range(8)]
        ogidx = np.zeros((128, 32), np.int32)
        for jj in range(32):
            k, r = jj // 4, jj % 4
            g = own_tiles[k]
            if k < 6:
                ogidx[:, jj] = r * 3072 + g * 128 + np.arange(128)
            else:
                ogidx[:, jj] = r * 1024 + (g * 128 - 3072) + np.arange(128)
        m = {
            "x_b": x[b],
            "x_own": np.ascontiguousarray(
                x[b].reshape(32, 128, D)[own_tiles].reshape(TOK, D)),
            "wq": np.ascontiguousarray(wq[:, fs]),
            "wk": np.ascontiguousarray(wk[:, fs]),
            "wv": np.ascontiguousarray(wv[:, fs]),
            "cdq": diag_tiles(cq, qd * FQ),
            "cdk": diag_tiles(ck, qd * FQ),
            "cdv": diag_tiles(cv, qd * FQ),
            "wad": wad, "wgd": wgd,
            "wau": np.ascontiguousarray(wau[:, fs]),
            "wgu": np.ascontiguousarray(wgu[:, fs]),
            "wb": np.ascontiguousarray(wb[:, hsl]),
            "rms_bc": rms_bc,
            "wo": wo,
            "wmod_s": np.ascontiguousarray(wmod[:, c * 1536:(c + 1) * 1536]),
            "c_T": cT,
            "wg_t": wg_t, "wu_t": wu_t, "wd_t": wd_t,
            "ident_in": ident, "masks_in": masks, "mrep_in": mrep, "ones1_in": ones1,
            "bsel128": np.full((128, 1), float(b), f32),
            "ogidx": ogidx,
        }
        in_maps.append(m)
    return in_maps


def kernel(**inputs):
    nc = get_nc()
    in_maps = _prep_inputs(inputs)
    res = run_bass_kernel_spmd(nc, in_maps, core_ids=list(range(8)))
    out = np.empty((B, T, D), np.float32)
    for c in range(8):
        b, qd = c // 4, c % 4
        o = res.results[c]["out_s"].reshape(8, 128, D)
        for k in range(8):
            t = k * 4 + qd
            out[b, t * 128:(t + 1) * 128] = o[k]
    return out
